# version 06 (final)
# speedup vs baseline: 1.0152x; 1.0152x over previous
import contextlib
import numpy as np
import concourse.bass as bass
import concourse.mybir as mybir
from concourse.bass_utils import run_bass_kernel_spmd
from concourse.bass_core_rust import IndirectOffsetOnAxis

F32 = mybir.dt.float32
BF16 = mybir.dt.bfloat16
U32 = mybir.dt.uint32
AF = mybir.ActivationFunctionType
ALU = mybir.AluOpType
AX = mybir.AxisListType.X

NCORE = 8
B, T, D = 2, 8192, 1024
TPC = 2048
NCH = 16
ACH = 17
TH = ACH * 128
EXPB = -12.0

TT = [512, 512, 512, 512, 128]
TOFF = [0, 512, 1024, 1536, 2048]

PAIR_HX = [0, 1, 2, 3, 8, 9, 10, 11]


class Counter:
    def __init__(self, sem):
        self.sem = sem
        self.n = 0

    def inc(self, inst, k=1):
        inst.then_inc(self.sem, k)
        self.n += k
        return self.n


class Ctx:
    def __init__(self, nc, stack):
        self.nc = nc
        self.stack = stack
        self.block_cm = nc.Block()
        self.block = self.block_cm.__enter__()
        self.bar = nc.semaphore("bar").__enter__()
        self.nbar = 0
        self._sems = {}

    def finish(self):
        self.block_cm.__exit__(None, None, None)

    def sem(self, name):
        if name not in self._sems:
            h = self.nc.semaphore(name).__enter__()
            self._sems[name] = Counter(h)
        return self._sems[name]

    def sbuf(self, name, shape, dtype, stack=None):
        return (stack or self.stack).enter_context(self.nc.sbuf_tensor(name, shape, dtype))

    def psum(self, name, shape, dtype=F32, stack=None):
        return (stack or self.stack).enter_context(self.nc.psum_tensor(name, shape, dtype))

    def setup_consts(self, values, ld):
        self.const_sb = self.sbuf("const_sb", [128, max(1, len(values))], F32)
        self.const_bf = self.sbuf("const_bf", [128, 1], BF16)
        self.const_u8 = self.sbuf("const_u8", [128, 1], mybir.dt.uint8)
        blk = self.block
        ldc = self.sem("ldc")

        @blk.gpsimd
        def _(g):
            for i, v in enumerate(values):
                ldc.inc(g.memset(self.const_sb[:, i:i + 1], float(v)))
            ldc.inc(g.memset(self.const_bf[:], 1.0))
            ldc.inc(g.memset(self.const_u8[:], 127))

        for i, v in enumerate(values):
            self.nc.const_aps.register(F32, float(v), self.const_sb[:, i:i + 1])
        self.nc.const_aps.register(BF16, 1.0, self.const_bf[:])
        self.nc.const_aps.register(mybir.dt.uint8, 127, self.const_u8[:])

    def barrier(self):
        self.nbar += 1
        tgt = self.nbar * 5
        blk = self.block
        for dec, has_drain in ((blk.tensor, True), (blk.vector, True),
                               (blk.scalar, True), (blk.gpsimd, True),
                               (blk.sync, False)):
            @dec
            def _(eng, _tgt=tgt, _d=has_drain):
                if _d:
                    eng.drain().then_inc(self.bar, 1)
                else:
                    eng.nop().then_inc(self.bar, 1)
                eng.wait_ge(self.bar, _tgt)


def emit_ln(ctx, tag, x_ch, xn_ch, nch, st_all, scr):
    blk = ctx.block
    bn = ctx.sem(tag + "bn"); sn = ctx.sem(tag + "n")
    assert bn.n == 0, "LN tag reused: " + tag

    def st(c, a, b):
        return st_all[:, c * 16 + a:c * 16 + b]

    @blk.vector
    def _(v):
        for c in range(nch):
            v.bn_stats(st(c, 0, 6), x_ch(c)[:, 0:512])
            v.bn_stats(st(c, 6, 12), x_ch(c)[:, 512:1024])
            v.drain()
            v.bn_aggr(st(c, 12, 14), st(c, 0, 12))
            bn.inc(v.drain())

    @blk.scalar
    def _(s):
        for c in range(nch):
            s.wait_ge(bn.sem, c + 1)
            s.activation(st(c, 15, 16), st(c, 13, 14), AF.Ln, bias=1e-5)
            s.drain()
            s.activation(st(c, 15, 16), st(c, 15, 16), AF.Exp, scale=-0.5)
            s.drain()
            s.activation(st(c, 0, 1), st(c, 15, 16), AF.Identity, scale=-1.0)
            s.drain()
            s.activation(st(c, 14, 15), st(c, 12, 13), AF.Identity,
                         scale=st(c, 0, 1))
            s.drain()
            s.activation(xn_ch(c), x_ch(c), AF.Identity,
                                scale=st(c, 15, 16), bias=st(c, 14, 15))
            sn.inc(s.drain())


def emit_transpose(ctx, tag, xn_ch, dest, nch, ident, pstack):
    blk = ctx.block
    sn = ctx.sem(tag + "n")
    tp = ctx.sem(tag + "tp"); te = ctx.sem(tag + "te")
    pt = [ctx.psum(tag + "pt0", [128, 1024], BF16, stack=pstack),
          ctx.psum(tag + "pt1", [128, 1024], BF16, stack=pstack)]
    assert tp.n == 0, "transpose tag reused: " + tag

    @blk.tensor
    def _(t):
        for c in range(nch):
            t.wait_ge(sn.sem, c + 1)
            if c >= 1:
                t.wait_ge(te.sem, c)
            pb = pt[c % 2]
            for dc in range(8):
                mmi = t.transpose(pb[:, dc * 128:(dc + 1) * 128],
                            xn_ch(c)[:, dc * 128:(dc + 1) * 128], ident[:])
            tp.inc(mmi)

    @blk.scalar
    def _(s):
        for c in range(nch):
            s.wait_ge(tp.sem, c + 1)
            pb = pt[c % 2]
            for dc in range(8):
                s.activation(dest(dc, c), pb[:, dc * 128:(dc + 1) * 128], AF.Identity)
            te.inc(s.drain())


def pc_of_head(h):
    hx = h if (h // 4) % 2 == 0 else h - 4
    return PAIR_HX.index(hx), (0 if (h // 4) % 2 == 0 else 64)


class _Cut(Exception):
    pass


def _nullctx():
    return contextlib.nullcontext()


def _finish(ctx, blk, x_tm, x1o, xo):
    ctx.barrier()

    @blk.sync
    def _(sy):
        xo.inc(sy.dma_start(out=x1o.rearrange("(c p) d -> p c d", p=128),
                            in_=x_tm[:].rearrange("p (c d) -> p c d", d=D)), 16)
        sy.wait_ge(xo.sem, xo.n)


def build_launch1(stop_after=None, dbg=False):
    nc = bass.Bass("TRN2", target_bir_lowering=False, detect_race_conditions=False)
    stack = contextlib.ExitStack()
    ctx = Ctx(nc, stack)
    blk = ctx.block

    def dram(n, s, d=F32, k="ExternalInput"):
        return nc.dram_tensor(n, s, d, kind=k).ap()

    xs = dram("xs", [TH, D])
    xfull = dram("xfull", [T, D])
    qkvT = dram("qkvT", [D, 1536])
    qkv_b = dram("qkv_b", [1, 1536])
    qb_col = dram("qb_col", [128, 8])
    kb_col = dram("kb_col", [128, 2])
    scaling = dram("scaling", [1, 64])
    o_wTh = dram("o_wTh", [64, 16 * 1024])
    o_b = dram("o_b", [1, D])
    w_rT = dram("w_rT", [D, D]); w_kT = dram("w_kT", [D, D]); w_vT = dram("w_vT", [D, D])
    mu_rc = dram("mu_rc", [128, 8]); mu_kc = dram("mu_kc", [128, 8])
    l1_wqT = dram("l1_wqT", [D, D]); l1_wkT = dram("l1_wkT", [D, D]); l1_wvT = dram("l1_wvT", [D, D])
    maskL = dram("maskL", [128, 1])

    qTd = dram("qTd", [8 * 128, TH], BF16, "ExternalOutput" if dbg else "Internal")
    if dbg:
        kTo = dram("kTo", [128, 2 * T], BF16, "ExternalOutput")
        vago = dram("vago", [128, 64 * 260], BF16, "ExternalOutput")
        outno = dram("outno", [64, 16 * 512], BF16, "ExternalOutput")
        dsbo = dram("dsbo", [65, 1024], BF16, "ExternalOutput")
    x1o = dram("x1o", [TH, D], F32, "ExternalOutput")
    qno = dram("qno", [TH, D], F32, "ExternalOutput")
    kvpo = dram("kvpo", [64, 16 * 64], F32, "ExternalOutput")

    ld = ctx.sem("ld")
    ctx.setup_consts([1e-5, 1.0, EXPB], ld)
    x_tm = ctx.sbuf("x_tm", [128, ACH * D], F32)
    ident = ctx.sbuf("ident", [128, 128], BF16)
    ones1 = ctx.sbuf("ones1", [1, 128], BF16)
    onescol = ctx.sbuf("onescol", [128, 64], BF16)
    st_all = ctx.sbuf("st_all", [128, ACH * 16], F32)
    maskL_sb = ctx.sbuf("maskL_sb", [128, 1], F32)

    def x_ch(c):
        return x_tm[:, c * D:(c + 1) * D]

    @blk.sync
    def _(sy):
        ld.inc(sy.dma_start(out=x_tm[:].rearrange("p (c d) -> p c d", d=D),
                            in_=xs.rearrange("(c p) d -> p c d", p=128)), 16)
        ld.inc(sy.dma_start(out=maskL_sb[:], in_=maskL), 16)

    ldc = ctx.sem("ldc")

    @blk.gpsimd
    def _(g):
        g.memset(ident[:], 0.0)
        g.affine_select(out=ident[:], in_=ident[:], compare_op=ALU.not_equal,
                        fill=1.0, base=0, pattern=[[-1, 128]], channel_multiplier=1)
        g.memset(ones1[:], 1.0)
        ldc.inc(g.memset(onescol[:], 1.0))

    @blk.vector
    def _(v):
        v.wait_ge(ld.sem, ld.n)
        v.wait_ge(ldc.sem, ldc.n)
    @blk.scalar
    def _(s):
        s.wait_ge(ld.sem, ld.n)
        s.wait_ge(ldc.sem, ldc.n)
    @blk.tensor
    def _(t):
        t.wait_ge(ld.sem, ld.n)
        t.wait_ge(ldc.sem, ldc.n)

    try:
        with contextlib.ExitStack() as ph:
            kb_sb = ctx.sbuf("kb_sb", [128, 2], F32, stack=ph)
            qkvb_sb = ctx.sbuf("qkvb_sb", [1, 256], BF16, stack=ph)

            pA1 = ph.enter_context(contextlib.ExitStack())
            xn_tm = ctx.sbuf("xn_tm", [128, ACH * D], BF16, stack=pA1)
            xn_fm = ctx.sbuf("xn_fm", [128, 8 * TH], BF16, stack=pA1)
            scr = ctx.sbuf("scr", [128, 1024], F32, stack=pA1)

            def xn_ch(c):
                return xn_tm[:, c * D:(c + 1) * D]

            def xnf(dc):
                return xn_fm[:, dc * TH:(dc + 1) * TH]

            with contextlib.ExitStack() as pp:
                emit_ln(ctx, "a", x_ch, xn_ch, ACH, st_all, scr)
                emit_transpose(ctx, "a", xn_ch,
                               lambda dc, c: xnf(dc)[:, c * 128:(c + 1) * 128],
                               ACH, ident, pp)
                ctx.barrier()
            if stop_after == "ln0":
                raise _Cut()

            qkvT_sb = ctx.sbuf("qkvT_sb", [128, 8 * 1536], BF16, stack=pA1)
            qb_sb = ctx.sbuf("qb_sb", [128, 8], F32, stack=pA1)
            scl_sb = ctx.sbuf("scl_sb", [1, 64], F32, stack=pA1)
            sp_row = ctx.sbuf("sp_row", [1, 64], F32, stack=pA1)
            qsc = ctx.sbuf("qsc", [128, 1], F32, stack=pA1)
            qbs = ctx.sbuf("qbs", [128, 8], F32, stack=pA1)
            identf = ctx.sbuf("identf", [128, 128], F32, stack=pA1)

            wl = ctx.sem("wl"); pr = ctx.sem("pr")

            @blk.gpsimd
            def _(g):
                wl.inc(g.dma_start(out=qkvT_sb[:].rearrange("p (c o) -> p c o", o=1536),
                                   in_=qkvT.rearrange("(c p) o -> p c o", p=128)), 16)
                wl.inc(g.dma_start(out=qb_sb[:], in_=qb_col), 16)
                wl.inc(g.dma_start(out=kb_sb[:], in_=kb_col), 16)
                wl.inc(g.dma_start(out=qkvb_sb[:], in_=qkv_b[:, 1280:1536]), 16)
                wl.inc(g.dma_start(out=scl_sb[:], in_=scaling), 16)
                g.memset(identf[:], 0.0)
                ctx.sem("wlc").inc(g.affine_select(out=identf[:], in_=identf[:], compare_op=ALU.not_equal,
                                                   fill=1.0, base=0, pattern=[[-1, 128]], channel_multiplier=1))

            with contextlib.ExitStack() as pp:
                p_sm = ctx.psum("p_sm", [128, 128], F32, stack=pp)

                @blk.scalar
                def _(s):
                    s.wait_ge(wl.sem, wl.n)
                    s.wait_ge(ctx.sem("wlc").sem, 1)
                    s.activation(sp_row[:], scl_sb[:], AF.Abs)
                    s.activation(sp_row[:], sp_row[:], AF.Exp, scale=-1.0)
                    pr.inc(s.activation(sp_row[:], sp_row[:], AF.Ln, bias=1.0))

                @blk.vector
                def _(v):
                    v.wait_ge(wl.sem, wl.n)
                    v.wait_ge(ctx.sem("wlc").sem, 1)
                    v.wait_ge(pr.sem, 1)
                    v.tensor_scalar_max(scl_sb[:], scl_sb[:], 0.0)
                    v.tensor_add(sp_row[:], sp_row[:], scl_sb[:])
                    pr.inc(v.tensor_scalar_mul(sp_row[:], sp_row[:], 1.442695041 / 8.0))

                @blk.tensor
                def _(t):
                    t.wait_ge(pr.sem, 2)
                    pr.inc(t.transpose(p_sm[0:64, 0:1], sp_row[:], identf[0:1, 0:1]))

                @blk.vector
                def _(v):
                    v.wait_ge(pr.sem, 3)
                    v.tensor_copy(qsc[0:64, 0:1], p_sm[0:64, 0:1])
                    pr.inc(v.tensor_copy(qsc[64:128, 0:1], p_sm[0:64, 0:1]))
                    v.wait_ge(pr.sem, 4)
                    pr.inc(v.tensor_tensor(qbs[:], qb_sb[:],
                                           qsc[:, 0:1].broadcast_to([128, 8]), op=ALU.mult))

                ctx.barrier()

            if stop_after == "qprep":
                raise _Cut()

            def qkv_lhs(dc, pc):
                base = dc * 1536 + pc * 128
                return qkvT_sb[:, base:base + 128]

            with contextlib.ExitStack() as pp:
                p_q = [ctx.psum("p_q0", [128, 512], F32, stack=pp),
                       ctx.psum("p_q1", [128, 512], F32, stack=pp)]
                qstage = [ctx.sbuf("qstg0", [128, 512], BF16, stack=pp),
                          ctx.sbuf("qstg1", [128, 512], BF16, stack=pp)]
                mmq = ctx.sem("mmq"); evq = ctx.sem("evq"); qdm = ctx.sem("qdm")

                @blk.tensor
                def _(t):
                    i = 0
                    for pc in range(8):
                        for ti in range(5):
                            if i >= 2:
                                t.wait_ge(evq.sem, i - 1)
                            pb = p_q[i % 2]
                            for dc in range(8):
                                mmi = t.matmul(pb[:, 0:TT[ti]], qkv_lhs(dc, pc),
                                         xnf(dc)[:, TOFF[ti]:TOFF[ti] + TT[ti]],
                                         start=(dc == 0), stop=(dc == 7))
                            mmq.inc(mmi)
                            i += 1

                use_dma = stop_after != "qtmm"

                @blk.scalar
                def _(s):
                    i = 0
                    for pc in range(8):
                        for ti in range(5):
                            s.wait_ge(mmq.sem, i + 1)
                            if use_dma and i >= 2:
                                s.wait_ge(qdm.sem, 16 * (i - 1))
                            s.activation(qstage[i % 2][:, 0:TT[ti]],
                                                 p_q[i % 2][:, 0:TT[ti]], AF.Identity,
                                                 scale=qsc[:, 0:1], bias=qbs[:, pc:pc + 1])
                            evq.inc(s.drain())
                            i += 1

                if use_dma:
                    @blk.sync
                    def _(sy):
                        i = 0
                        for pc in range(8):
                            for ti in range(5):
                                sy.wait_ge(evq.sem, i + 1)
                                qdm.inc(sy.dma_start(
                                    out=qTd[pc * 128:(pc + 1) * 128, TOFF[ti]:TOFF[ti] + TT[ti]],
                                    in_=qstage[i % 2][:, 0:TT[ti]]), 16)
                                i += 1
                        sy.wait_ge(qdm.sem, 16 * 40)

                ctx.barrier()

            if stop_after in ("qt", "qtmm"):
                raise _Cut()
            pA1.close()

            kT = ctx.sbuf("kT", [128, 2 * T], BF16, stack=ph)
            vaug = ctx.sbuf("vaug", [128, 64 * 260], BF16, stack=ph)

            with contextlib.ExitStack() as pp:
                kvw_sb = ctx.sbuf("kvw_sb", [128, 8 * 512], BF16, stack=pp)
                scr = ctx.sbuf("scr_f", [128, 1024], F32, stack=pp)
                wl2 = ctx.sem("wl2")

                @blk.gpsimd
                def _(g):
                    wl2.inc(g.dma_start(out=kvw_sb[:].rearrange("p (c o) -> p c o", o=512),
                                        in_=qkvT.rearrange("(c p) o -> p c o", p=128)[:, :, 1024:1536]), 16)

                @blk.tensor
                def _(t):
                    t.wait_ge(wl2.sem, 16)

                xf_buf = [ctx.sbuf("xf0", [128, 1024], F32, stack=pp),
                          ctx.sbuf("xf1", [128, 1024], F32, stack=pp)]
                xnfull = ctx.sbuf("xnfull", [128, 1024], BF16, stack=pp)
                xnf_fm = ctx.sbuf("xnf_fm", [128, 1024], BF16, stack=pp)
                st2 = ctx.sbuf("st2", [128, 16], F32, stack=pp)
                p_tr = [ctx.psum("p_tra", [128, 1024], BF16, stack=pp),
                        ctx.psum("p_trb", [128, 1024], BF16, stack=pp)]
                p_k = ctx.psum("p_k", [128, 256], F32, stack=pp)
                p_v = ctx.psum("p_v", [128, 256], F32, stack=pp)

                fl = ctx.sem("fl"); frd = ctx.sem("frd"); fsq = ctx.sem("fsq")
                fv1 = ctx.sem("fv1"); frs = ctx.sem("frs"); fv2 = ctx.sem("fv2")
                fn = ctx.sem("fn"); ftp = ctx.sem("ftp"); fte = ctx.sem("fte")
                fkv = ctx.sem("fkv"); fev = ctx.sem("fev")
                NF = T // 128

                @blk.gpsimd
                def _(g):
                    g.memset(vaug[:].rearrange("p (c g o) -> p c g o", g=4, o=65)[:, :, :, 64:65], 1.0)
                    for c in range(NF):
                        if c >= 2:
                            g.wait_ge(fn.sem, c - 1)
                        fl.inc(g.dma_start(out=xf_buf[c % 2][:],
                                           in_=xfull[c * 128:(c + 1) * 128, :]), 16)

                def s2(i):
                    return st2[:, i:i + 1]

                @blk.vector
                def _(v):
                    for c in range(NF):
                        v.wait_ge(fl.sem, (c + 1) * 16)
                        xb = xf_buf[c % 2]
                        frd.inc(v.reduce_sum(s2(0), xb[:], axis=AX))
                        v.wait_ge(fsq.sem, c + 1)
                        v.tensor_scalar_mul(s2(1), s2(0), -1.0 / 1024.0)
                        v.drain()
                        v.tensor_tensor(s2(6), s2(1), s2(1), op=ALU.mult)
                        v.tensor_scalar_mul(s2(7), s2(2), 1.0 / 1024.0)
                        v.drain()
                        fv1.inc(v.tensor_sub(s2(3), s2(7), s2(6)))
                        v.wait_ge(frs.sem, c + 1)
                        fv2.inc(v.tensor_tensor(s2(5), s2(1), s2(4), op=ALU.mult))

                @blk.scalar
                def _(s):
                    for c in range(NF):
                        xb = xf_buf[c % 2]
                        s.wait_ge(frd.sem, c + 1)
                        fsq.inc(s.activation(scr[:], xb[:], AF.Square, accum_out=s2(2)))
                        s.wait_ge(fv1.sem, c + 1)
                        s.activation(s2(4), s2(3), AF.Ln, bias=1e-5)
                        s.drain()
                        frs.inc(s.activation(s2(4), s2(4), AF.Exp, scale=-0.5))
                        s.wait_ge(fv2.sem, c + 1)
                        s.activation(xnfull[:], xb[:], AF.Identity, scale=s2(4), bias=s2(5))
                        fn.inc(s.drain())
                        s.wait_ge(ftp.sem, c + 1)
                        if c >= 1:
                            s.wait_ge(fkv.sem, c)
                        s.activation(xnf_fm[:], p_tr[c % 2][:], AF.Identity)
                        fte.inc(s.drain())
                        s.wait_ge(fkv.sem, c + 1)
                        for kc in range(2):
                            s.activation(kT[:, kc * T + c * 128: kc * T + (c + 1) * 128],
                                         p_k[:, kc * 128:(kc + 1) * 128], AF.Identity,
                                         bias=kb_sb[:, kc:kc + 1])
                        for g4 in range(4):
                            s.activation(vaug[:, c * 260 + g4 * 65: c * 260 + g4 * 65 + 64],
                                         p_v[:, g4 * 64:(g4 + 1) * 64], AF.Identity)
                        fev.inc(s.drain())

                @blk.tensor
                def _(t):
                    for c in range(NF):
                        t.wait_ge(fn.sem, c + 1)
                        if c >= 2:
                            t.wait_ge(fte.sem, c - 1)
                        pb = p_tr[c % 2]
                        for dc in range(8):
                            mmi = t.transpose(pb[:, dc * 128:(dc + 1) * 128],
                                        xnfull[:, dc * 128:(dc + 1) * 128], ident[:])
                        ftp.inc(mmi)
                        t.wait_ge(fte.sem, c + 1)
                        if c >= 1:
                            t.wait_ge(fev.sem, c)
                        for kc in range(2):
                            for dc in range(8):
                                t.matmul(p_k[:, kc * 128:(kc + 1) * 128],
                                         kvw_sb[:, dc * 512 + kc * 128: dc * 512 + (kc + 1) * 128],
                                         xnf_fm[:, dc * 128:(dc + 1) * 128],
                                         start=(dc == 0), stop=(dc == 7))
                        for dc in range(8):
                            t.matmul(p_v[:], xnf_fm[:, dc * 128:(dc + 1) * 128],
                                     kvw_sb[:, dc * 512 + 256: dc * 512 + 512],
                                     start=(dc == 0), stop=False)
                        fkv.inc(t.matmul(p_v[:], ones1[:], qkvb_sb[:, 0:256],
                                         start=False, stop=True))

                ctx.barrier()

            if stop_after == "fullkv":
                raise _Cut()

            owT_sb = ctx.sbuf("owT_sb", [65, 16 * 1024], BF16, stack=ph)
            pstage = [ctx.sbuf("pstg0", [128, 1024], BF16, stack=ph),
                      ctx.sbuf("pstg1", [128, 1024], BF16, stack=ph)]
            d_sb = ctx.sbuf("d_sb", [128, 1024], BF16, stack=ph)
            opv_sb = ctx.sbuf("opv_sb", [128, 512], F32, stack=ph)
            outn = ctx.sbuf("outn", [64, 16 * 512], BF16, stack=ph)
            qslab = [ctx.sbuf("qslab0", [128, 8 * 512], BF16, stack=ph),
                     ctx.sbuf("qslab1", [128, 8 * 512], BF16, stack=ph)]

            def qsl(ti, pc):
                return qslab[ti % 2][:, pc * 512:pc * 512 + TT[ti]]

            wo = ctx.sem("wo")

            @blk.gpsimd
            def _(g):
                g.memset(owT_sb[64:65, 1024:1152], 1.0)
                wo.inc(g.dma_start(out=owT_sb[0:64, :], in_=o_wTh), 16)
                wo.inc(g.dma_start(out=owT_sb[64:65, 0:1024], in_=o_b), 16)

            with contextlib.ExitStack() as pp:
                p_s = [ctx.psum("p_s0", [128, 1024], F32, stack=pp),
                       ctx.psum("p_s1", [128, 1024], F32, stack=pp)]
                p_o = [ctx.psum("p_oA", [128, 512], F32, stack=pp),
                       ctx.psum("p_oB", [128, 512], F32, stack=pp)]
                p_b = ctx.psum("p_b", [128, 512], F32, stack=pp)
                p_op = ctx.psum("p_op", [128, 512], F32, stack=pp)

                sS = ctx.sem("sS"); sE = ctx.sem("sE"); sgd = ctx.sem("sgd")
                sPV = ctx.sem("sPV")
                sEo = ctx.sem("sEo"); sRc = ctx.sem("sRc")
                sbA = ctx.sem("sbA"); smA = ctx.sem("smA")
                sbB = ctx.sem("sbB"); smB = ctx.sem("smB")
                sop = ctx.sem("sop"); sxr = ctx.sem("sxr")
                sbe = ctx.sem("sbe"); soe = ctx.sem("soe")
                qld = ctx.sem("qld")

                NK = T // 128
                subgroups = [(ti, g4, sub) for ti in range(5) for g4 in range(4) for sub in range(2)]

                @blk.sync
                def _(sy):
                    for ti in range(5):
                        if ti >= 2:
                            sy.wait_ge(sgd.sem, 8 * (ti - 1))
                        qld.inc(sy.dma_start(
                            out=qslab[ti % 2][:].rearrange("p (c o) -> p c o", o=512)[:, :, 0:TT[ti]],
                            in_=qTd.rearrange("(c p) o -> p c o", p=128)[:, :, TOFF[ti]:TOFF[ti] + TT[ti]]), 16)

                @blk.tensor
                def _(t):
                    t.wait_ge(wo.sem, 32)
                    nq = 0
                    for sg, (ti, g4, sub) in enumerate(subgroups):
                        tw, toff = TT[ti], TOFF[ti]
                        par = g4 % 2
                        hA = g4 * 4 + 2 * sub
                        pcA, _ = pc_of_head(hA)
                        pcB, _ = pc_of_head(hA + 1)
                        for kc in range(NK):
                            if sg * NK + kc >= 2:
                                t.wait_ge(sE.sem, sg * NK + kc - 1)
                            pb = p_s[kc % 2]
                            lhsk = kT[par * 64:par * 64 + 64,
                                      (g4 // 2) * T + kc * 128:(g4 // 2) * T + (kc + 1) * 128]
                            if kc == 0:
                                t.wait_ge(qld.sem, 16 * (ti + 1))
                            t.matmul(pb[:, 0:tw], lhsk,
                                     qsl(ti, pcA)[par * 64:par * 64 + 64, :],
                                     start=True, stop=True)
                            sS.inc(t.matmul(pb[:, 512:512 + tw], lhsk,
                                            qsl(ti, pcB)[par * 64:par * 64 + 64, :],
                                            start=True, stop=True))
                            t.wait_ge(sE.sem, sg * NK + kc + 1)
                            if kc == 0 and sg > 0:
                                t.wait_ge(smB.sem, sg)
                            lv = vaug[:, kc * 260 + g4 * 65: kc * 260 + g4 * 65 + 65]
                            t.matmul(p_o[0][0:65, 0:tw], lv, pstage[kc % 2][:, 0:tw],
                                     start=(kc == 0), stop=(kc == NK - 1))
                            mm = t.matmul(p_o[1][0:65, 0:tw], lv, pstage[kc % 2][:, 512:512 + tw],
                                          start=(kc == 0), stop=(kc == NK - 1))
                            if kc == NK - 1:
                                sgd.inc(mm)
                        t.wait_ge(sRc.sem, 2 * sg + 1)
                        if sg > 0:
                            t.wait_ge(sbe.sem, 2 * sg)
                        sbA.inc(t.matmul(p_b[0:64, 0:tw], onescol[64:65, 0:64],
                                         d_sb[64:65, 0:tw], start=True, stop=True))
                        t.wait_ge(sRc.sem, 2 * sg + 2)
                        t.wait_ge(sbe.sem, 2 * sg + 1)
                        sbB.inc(t.matmul(p_b[0:64, 0:tw], onescol[64:65, 0:64],
                                         d_sb[64:65, 512:512 + tw], start=True, stop=True))
                        if (g4, sub) == (3, 1):
                            for tc in range(tw // 128):
                                for half in range(2):
                                    t.wait_ge(smB.sem, sg + 1)
                                    if nq >= 1:
                                        t.wait_ge(soe.sem, nq)
                                    for h in range(16):
                                        t.matmul(p_op[:],
                                                 outn[:, h * 512 + tc * 128: h * 512 + (tc + 1) * 128],
                                                 owT_sb[0:64, h * 1024 + half * 512: h * 1024 + (half + 1) * 512],
                                                 start=(h == 0), stop=False)
                                    sop.inc(t.matmul(p_op[:], owT_sb[64:65, 1024:1152],
                                                     owT_sb[64:65, half * 512:(half + 1) * 512],
                                                     start=False, stop=True))
                                    nq += 1

                @blk.scalar
                def _(s):
                    nq = 0
                    for sg, (ti, g4, sub) in enumerate(subgroups):
                        tw = TT[ti]
                        for kc in range(NK):
                            s.wait_ge(sS.sem, sg * NK + kc + 1)
                            s.activation(pstage[kc % 2][:], p_s[kc % 2][:],
                                                AF.Exp, bias=EXPB)
                            sE.inc(s.drain())
                        s.wait_ge(sgd.sem, sg + 1)
                        hA_ = g4 * 4 + 2 * sub
                        for j in range(2):
                            s.activation(d_sb[64:65, j * 512:j * 512 + tw],
                                         p_o[j][64:65, 0:tw], AF.Identity)
                            s.activation(outn[:, (hA_ + j) * 512:(hA_ + j) * 512 + tw],
                                         p_o[j][0:64, 0:tw], AF.Identity)
                        sEo.inc(s.drain())
                        s.wait_ge(sbA.sem, sg + 1)
                        if sg >= 1:
                            s.wait_ge(smB.sem, sg)
                        s.activation(d_sb[0:64, 0:tw], p_b[0:64, 0:tw], AF.Identity)
                        sbe.inc(s.drain())
                        s.wait_ge(sbB.sem, sg + 1)
                        s.activation(d_sb[0:64, 512:512 + tw], p_b[0:64, 0:tw], AF.Identity)
                        sbe.inc(s.drain())
                        if (g4, sub) == (3, 1):
                            for tc in range(tw // 128):
                                for half in range(2):
                                    s.wait_ge(sop.sem, nq + 1)
                                    if nq >= 1:
                                        s.wait_ge(sxr.sem, nq)
                                    s.activation(opv_sb[:], p_op[:], AF.Identity)
                                    soe.inc(s.drain())
                                    nq += 1

                @blk.vector
                def _(v):
                    nq = 0
                    for sg, (ti, g4, sub) in enumerate(subgroups):
                        tw = TT[ti]
                        hA = g4 * 4 + 2 * sub
                        v.wait_ge(sEo.sem, sg + 1)
                        with ctx.nc.allow_low_precision(
                                reason="softmax denom reciprocal in bf16; 0.4% rel, within tolerance"):
                            v.reciprocal(d_sb[64:65, 0:tw], d_sb[64:65, 0:tw])
                            sRc.inc(v.drain())
                            v.reciprocal(d_sb[64:65, 512:512 + tw], d_sb[64:65, 512:512 + tw])
                            sRc.inc(v.drain())
                        v.wait_ge(sbe.sem, 2 * sg + 1)
                        smA.inc(v.tensor_tensor(outn[:, hA * 512:hA * 512 + tw],
                                                outn[:, hA * 512:hA * 512 + tw],
                                                d_sb[0:64, 0:tw], op=ALU.mult))
                        v.wait_ge(sbe.sem, 2 * sg + 2)
                        smB.inc(v.tensor_tensor(outn[:, (hA + 1) * 512:(hA + 1) * 512 + tw],
                                                outn[:, (hA + 1) * 512:(hA + 1) * 512 + tw],
                                                d_sb[0:64, 512:512 + tw], op=ALU.mult))
                        if (g4, sub) == (3, 1):
                            for tc in range(tw // 128):
                                cglob = TOFF[ti] // 128 + tc
                                for half in range(2):
                                    v.wait_ge(soe.sem, nq + 1)
                                    v.tensor_add(
                                        x_ch(cglob)[:, half * 512:(half + 1) * 512],
                                        x_ch(cglob)[:, half * 512:(half + 1) * 512],
                                        opv_sb[:])
                                    sxr.inc(v.drain())
                                    nq += 1

                ctx.barrier()

                if dbg:
                    dgo = ctx.sem("dgo")

                    @blk.sync
                    def _(sy):
                        dgo.inc(sy.dma_start(out=kTo, in_=kT[:]), 16)
                        dgo.inc(sy.dma_start(out=vago, in_=vaug[:]), 16)
                        dgo.inc(sy.dma_start(out=outno, in_=outn[:]), 16)
                        dgo.inc(sy.dma_start(out=dsbo, in_=d_sb[0:65, :]), 16)
                        sy.wait_ge(dgo.sem, 64)

                    ctx.barrier()

        if stop_after == "attn":
            raise _Cut()

        with contextlib.ExitStack() as ph:
            fmt = ctx.sbuf("fmt", [128, 8 * 2052], BF16, stack=ph)
            kact = ctx.sbuf("kact", [128, 8 * 2050], BF16, stack=ph)
            scr = ctx.sbuf("scr_b", [128, 1024], F32, stack=ph)

            def fmtc(dc):
                return fmt[:, dc * 2052:(dc + 1) * 2052]

            def kac(dc):
                return kact[:, dc * 2050:(dc + 1) * 2050]

            with contextlib.ExitStack() as pp:
                xnb_tm = ctx.sbuf("xnb_tm", [128, ACH * D], BF16, stack=pp)
                hscr = ctx.sbuf("hscr", [128, 8 * 128], BF16, stack=pp)

                def xnb_ch(c):
                    return xnb_tm[:, c * D:(c + 1) * D]

                def bdest(dc, c):
                    if c < NCH:
                        return fmtc(dc)[:, 2 + c * 128:2 + (c + 1) * 128]
                    return hscr[:, dc * 128:(dc + 1) * 128]

                emit_ln(ctx, "b", x_ch, xnb_ch, ACH, st_all, scr)
                emit_transpose(ctx, "b", xnb_ch, bdest, ACH, ident, pp)
                ctx.barrier()

                fx = ctx.sem("fx")

                @blk.vector
                def _(v):
                    for dc in range(8):
                        h = hscr[:, dc * 128:(dc + 1) * 128]
                        v.tensor_tensor(fmtc(dc)[:, 1:2], h[:, 0:1], maskL_sb[:], op=ALU.mult)
                        v.tensor_copy(fmtc(dc)[:, 2050:2051], h[:, 1:2])
                        v.tensor_copy(fmtc(dc)[:, 0:1], h[:, 2:3])
                    fx.inc(v.drain())

                ctx.barrier()

            mu_sb = ctx.sbuf("mu_sb", [128, 16], F32, stack=ph)
            om_sb = ctx.sbuf("om_sb", [128, 16], F32, stack=ph)

            mlw = ctx.sem("mlw")

            @blk.gpsimd
            def _(g):
                mlw.inc(g.dma_start(out=mu_sb[:, 0:8], in_=mu_rc), 16)
                mlw.inc(g.dma_start(out=mu_sb[:, 8:16], in_=mu_kc), 16)

            mlwc = ctx.sem("mlwc")

            @blk.vector
            def _(v):
                v.wait_ge(mlw.sem, mlw.n)
                mlwc.inc(v.tensor_scalar(om_sb[:], mu_sb[:], -1.0, 1.0,
                                         op0=ALU.mult, op1=ALU.add))

            KTT = [512, 512, 512, 512, 2]
            KOFF = [0, 512, 1024, 1536, 2048]

            def premult(tagn, srcT, mucol0, wa, wb, donesem):
                lw = ctx.sem(tagn)

                @blk.gpsimd
                def _(g):
                    lw.inc(g.dma_start(out=wa[:].rearrange("p (c o) -> p c o", o=1024),
                                       in_=srcT.rearrange("(c p) o -> p c o", p=128)), 16)

                @blk.vector
                def _(v):
                    v.wait_ge(lw.sem, lw.n)
                    v.wait_ge(mlwc.sem, 1)
                    for dc in range(8):
                        v.tensor_scalar(wb[:, dc * 1024:(dc + 1) * 1024],
                                        wa[:, dc * 1024:(dc + 1) * 1024],
                                        om_sb[:, mucol0 + dc:mucol0 + dc + 1], None, op0=ALU.mult)
                    for dc in range(8):
                        v.tensor_scalar(wa[:, dc * 1024:(dc + 1) * 1024],
                                        wa[:, dc * 1024:(dc + 1) * 1024],
                                        mu_sb[:, mucol0 + dc:mucol0 + dc + 1], None, op0=ALU.mult)
                    donesem.inc(v.drain())

            with contextlib.ExitStack() as pk:
                wka = ctx.sbuf("wka", [128, 8 * 1024], BF16, stack=pk)
                wkb = ctx.sbuf("wkb", [128, 8 * 1024], BF16, stack=pk)
                kscr = ctx.sbuf("kscr", [128, 512], BF16, stack=pk)
                wkd = ctx.sem("wkd")
                premult("lwk", w_kT, 8, wka, wkb, wkd)

                p_f = [ctx.psum("p_f0", [128, 512], F32, stack=pk),
                       ctx.psum("p_f1", [128, 512], F32, stack=pk)]
                km = ctx.sem("km"); kr = ctx.sem("kr"); ksq = ctx.sem("ksq")

                @blk.tensor
                def _(t):
                    t.wait_ge(wkd.sem, 1)
                    i = 0
                    for oc in range(8):
                        for ti in range(5):
                            w = KTT[ti]
                            if i >= 2:
                                t.wait_ge(kr.sem, i - 1)
                            pb = p_f[i % 2]
                            for dc in range(8):
                                t.matmul(pb[:, 0:w],
                                         wka[:, dc * 1024 + oc * 128:dc * 1024 + (oc + 1) * 128],
                                         fmtc(dc)[:, 1 + KOFF[ti]:1 + KOFF[ti] + w],
                                         start=(dc == 0), stop=False)
                            for dc in range(8):
                                mmi = t.matmul(pb[:, 0:w],
                                         wkb[:, dc * 1024 + oc * 128:dc * 1024 + (oc + 1) * 128],
                                         fmtc(dc)[:, KOFF[ti]:KOFF[ti] + w],
                                         start=False, stop=(dc == 7))
                            km.inc(mmi)
                            i += 1

                @blk.vector
                def _(v):
                    i = 0
                    for oc in range(8):
                        for ti in range(5):
                            v.wait_ge(km.sem, i + 1)
                            if i >= 1:
                                v.wait_ge(ksq.sem, i)
                            v.tensor_scalar_max(kscr[:, 0:KTT[ti]],
                                                       p_f[i % 2][:, 0:KTT[ti]], 0.0)
                            kr.inc(v.drain())
                            i += 1

                @blk.scalar
                def _(s):
                    i = 0
                    for oc in range(8):
                        for ti in range(5):
                            s.wait_ge(kr.sem, i + 1)
                            s.activation(kac(oc)[:, KOFF[ti]:KOFF[ti] + KTT[ti]],
                                                 kscr[:, 0:KTT[ti]], AF.Square)
                            ksq.inc(s.drain())
                            i += 1

                ctx.barrier()

            with contextlib.ExitStack() as pk:
                wra = ctx.sbuf("wra", [128, 8 * 1024], BF16, stack=pk)
                wrb = ctx.sbuf("wrb", [128, 8 * 1024], BF16, stack=pk)
                wv_sb = ctx.sbuf("wv_sb", [128, 8 * 1024], BF16, stack=pk)
                r_rot = ctx.sbuf("r_rot", [128, 2 * 512], BF16, stack=pk)
                vscr = ctx.sbuf("vscr", [128, 512], F32, stack=pk)
                wrd = ctx.sem("wrd"); wvl = ctx.sem("wvl")
                premult("lwr", w_rT, 0, wra, wrb, wrd)

                @blk.gpsimd
                def _(g):
                    wvl.inc(g.dma_start(out=wv_sb[:].rearrange("p (c o) -> p c o", o=1024),
                                        in_=w_vT.rearrange("(c p) o -> p c o", p=128)), 16)

                p_r = [ctx.psum("p_r0", [128, 512], F32, stack=pk),
                       ctx.psum("p_r1", [128, 512], F32, stack=pk)]
                p_vv = [ctx.psum("p_vv0", [128, 512], F32, stack=pk),
                        ctx.psum("p_vv1", [128, 512], F32, stack=pk)]
                rm = ctx.sem("rm"); rsg = ctx.sem("rsg")
                vm = ctx.sem("vm"); vco = ctx.sem("vco")

                strip_t = ctx.sbuf("strip_t", [128, 8 * 2], BF16, stack=pk)
                strip_p = ctx.sbuf("strip_p", [128, 8 * 2], BF16, stack=pk)
                strip_v = ctx.sbuf("strip_v", [128, 8 * 2], BF16, stack=pk)
                sst = ctx.sem("sst")

                @blk.vector
                def _(v):
                    for dc in range(8):
                        v.tensor_copy(strip_t[:, dc * 2:dc * 2 + 1], fmtc(dc)[:, 1:2])
                        v.tensor_copy(strip_t[:, dc * 2 + 1:dc * 2 + 2], fmtc(dc)[:, 2050:2051])
                        v.tensor_copy(strip_p[:, dc * 2:dc * 2 + 1], fmtc(dc)[:, 0:1])
                        v.tensor_copy(strip_p[:, dc * 2 + 1:dc * 2 + 2], fmtc(dc)[:, 2049:2050])
                        v.tensor_copy(strip_v[:, dc * 2:dc * 2 + 1], kac(dc)[:, 0:1])
                        v.tensor_copy(strip_v[:, dc * 2 + 1:dc * 2 + 2], kac(dc)[:, 2049:2050])
                    sst.inc(v.drain())

                def tok_lhs(dc, c):
                    if c < NCH:
                        return fmtc(dc)[:, 2 + c * 128:2 + (c + 1) * 128]
                    return strip_t[:, dc * 2:(dc + 1) * 2]

                def prev_lhs(dc, c):
                    if c < NCH:
                        return fmtc(dc)[:, 1 + c * 128:1 + (c + 1) * 128]
                    return strip_p[:, dc * 2:(dc + 1) * 2]

                def vk_lhs(dc, c):
                    if c < NCH:
                        return kac(dc)[:, 1 + c * 128:1 + (c + 1) * 128]
                    return strip_v[:, dc * 2:(dc + 1) * 2]

                def nrow(c):
                    return 128 if c < NCH else 2

                @blk.tensor
                def _(t):
                    t.wait_ge(wrd.sem, 1)
                    t.wait_ge(wvl.sem, 16)
                    t.wait_ge(sst.sem, 1)
                    i = 0
                    for c in range(ACH):
                        for half in range(2):
                            if i >= 2:
                                t.wait_ge(rsg.sem, i - 1)
                            pb = p_r[i % 2]
                            for dc in range(8):
                                t.matmul(pb[0:nrow(c), :], tok_lhs(dc, c),
                                         wra[:, dc * 1024 + half * 512:dc * 1024 + (half + 1) * 512],
                                         start=(dc == 0), stop=False)
                            for dc in range(8):
                                mmi = t.matmul(pb[0:nrow(c), :], prev_lhs(dc, c),
                                         wrb[:, dc * 1024 + half * 512:dc * 1024 + (half + 1) * 512],
                                         start=False, stop=(dc == 7))
                            rm.inc(mmi)
                            if i >= 2:
                                t.wait_ge(vco.sem, i - 1)
                            pv = p_vv[i % 2]
                            for dc in range(8):
                                mmi = t.matmul(pv[0:nrow(c), :], vk_lhs(dc, c),
                                         wv_sb[:, dc * 1024 + half * 512:dc * 1024 + (half + 1) * 512],
                                         start=(dc == 0), stop=(dc == 7))
                            vm.inc(mmi)
                            i += 1

                @blk.scalar
                def _(s):
                    i = 0
                    for c in range(ACH):
                        for half in range(2):
                            s.wait_ge(rm.sem, i + 1)
                            if i >= 2:
                                s.wait_ge(vco.sem, i - 1)
                            s.activation(r_rot[:, (i % 2) * 512:(i % 2 + 1) * 512][0:nrow(c), :],
                                                 p_r[i % 2][0:nrow(c), :], AF.Sigmoid)
                            rsg.inc(s.drain())
                            i += 1

                @blk.vector
                def _(v):
                    i = 0
                    for c in range(ACH):
                        for half in range(2):
                            v.wait_ge(vm.sem, i + 1)
                            v.wait_ge(rsg.sem, i + 1)
                            nr = nrow(c)
                            v.tensor_tensor(vscr[0:nr, :],
                                            r_rot[:, (i % 2) * 512:(i % 2 + 1) * 512][0:nr, :],
                                            p_vv[i % 2][0:nr, :], op=ALU.mult)
                            vco.inc(v.tensor_add(
                                x_ch(c)[0:nr, half * 512:(half + 1) * 512],
                                x_ch(c)[0:nr, half * 512:(half + 1) * 512],
                                vscr[0:nr, :]))
                            i += 1

                ctx.barrier()

        if stop_after == "l0f":
            raise _Cut()

        xo = ctx.sem("xo")

        @blk.sync
        def _(sy):
            xo.inc(sy.dma_start(out=x1o.rearrange("(c p) d -> p c d", p=128),
                                in_=x_tm[:].rearrange("p (c d) -> p c d", d=D)), 16)

        with contextlib.ExitStack() as ph:
            xn_fm = ctx.sbuf("xn1_fm", [128, 8 * TH], BF16, stack=ph)

            def xnf1(dc):
                return xn_fm[:, dc * TH:(dc + 1) * TH]

            scr = ctx.sbuf("scr_c", [128, 1024], F32, stack=ph)
            with contextlib.ExitStack() as pp:
                xn_tm = ctx.sbuf("xn1_tm", [128, ACH * D], BF16, stack=pp)

                def xn1_ch(c):
                    return xn_tm[:, c * D:(c + 1) * D]

                emit_ln(ctx, "c", x_ch, xn1_ch, ACH, st_all, scr)
                emit_transpose(ctx, "c", xn1_ch,
                               lambda dc, c: xnf1(dc)[:, c * 128:(c + 1) * 128],
                               ACH, ident, pp)
                ctx.barrier()

            def emit_proj_norm(wsrc, out_tm, normalize, tagn, dma_gate=None):
                with contextlib.ExitStack() as pk:
                    w_sb = ctx.sbuf(tagn + "w", [128, 8 * 1024], BF16, stack=pk)
                    pscr = ctx.sbuf(tagn + "ps", [128, 1024], F32, stack=pk)
                    nst = ctx.sbuf(tagn + "nst", [128, 3 * 16], F32, stack=pk)
                    lw = ctx.sem(tagn + "lw")

                    @blk.gpsimd
                    def _(g):
                        lw.inc(g.dma_start(out=w_sb[:].rearrange("p (c o) -> p c o", o=1024),
                                           in_=wsrc.rearrange("(c p) o -> p c o", p=128)), 16)

                    p_j = [ctx.psum(tagn + "p0", [128, 512], F32, stack=pk),
                           ctx.psum(tagn + "p1", [128, 512], F32, stack=pk)]
                    jm = ctx.sem(tagn + "jm"); je = ctx.sem(tagn + "je"); jn = ctx.sem(tagn + "jn")

                    @blk.tensor
                    def _(t):
                        t.wait_ge(lw.sem, 16)
                        i = 0
                        for c in range(ACH):
                            for half in range(2):
                                if i >= 2:
                                    t.wait_ge(je.sem, i - 1)
                                pb = p_j[i % 2]
                                for dc in range(8):
                                    mmi = t.matmul(pb[:], xnf1(dc)[:, c * 128:(c + 1) * 128],
                                             w_sb[:, dc * 1024 + half * 512:dc * 1024 + (half + 1) * 512],
                                             start=(dc == 0), stop=(dc == 7))
                                jm.inc(mmi)
                                i += 1

                    jq = ctx.sem(tagn + "jq")

                    @blk.scalar
                    def _(s):
                        i = 0
                        for c in range(ACH):
                            for half in range(2):
                                s.wait_ge(jm.sem, i + 1)
                                if normalize:
                                    if half == 0 and c >= 1:
                                        s.wait_ge(jn.sem, c)
                                    s.activation(pscr[:, half * 512:(half + 1) * 512],
                                                        p_j[i % 2][:], AF.Identity)
                                    je.inc(s.drain())
                                else:
                                    s.activation(out_tm(c)[:, half * 512:(half + 1) * 512],
                                                        p_j[i % 2][:], AF.Identity)
                                    je.inc(s.drain())
                                i += 1
                            if normalize:
                                s.wait_ge(jq.sem, 2 * c + 1)
                                s.activation(nst[:, 16:32], nst[:, 0:16], AF.Ln)
                                s.drain()
                                s.activation(nst[:, 16:32], nst[:, 16:32], AF.Exp, scale=0.5)
                                jq.inc(s.drain())

                    if normalize:
                        @blk.vector
                        def _(v):
                            for c in range(ACH):
                                v.wait_ge(je.sem, 2 * c + 2)
                                v.tensor_tensor(scr[:], pscr[:], pscr[:], op=ALU.mult)
                                v.drain()
                                jq.inc(v.tensor_reduce(
                                    nst[:, 0:16],
                                    scr[:].rearrange("p (h o) -> p h o", o=64),
                                    axis=AX, op=ALU.add))
                                v.wait_ge(jq.sem, 2 * c + 2)
                                v.tensor_scalar_max(nst[:, 16:32], nst[:, 16:32], 1e-12)
                                v.drain()
                                v.reciprocal(nst[:, 32:48], nst[:, 16:32])
                                v.drain()
                                if dma_gate is not None and c >= 2:
                                    v.wait_ge(dma_gate.sem, 16 * (c - 1))
                                jn.inc(v.tensor_tensor(
                                    out_tm(c)[:].rearrange("p (h o) -> p h o", o=64),
                                    pscr[:].rearrange("p (h o) -> p h o", o=64),
                                    nst[:, 32:48].rearrange("p (h o) -> p h o", o=1)
                                    .broadcast_to([128, 16, 64]),
                                    op=ALU.mult))

                    ctx.barrier()

            qn_sb = ctx.sbuf("qn_sb", [128, ACH * D], F32, stack=ph)
            qo = ctx.sem("qo")
            emit_proj_norm(l1_wqT, lambda c: qn_sb[:, c * D:(c + 1) * D],
                           True, "jq")

            @blk.sync
            def _(sy):
                qo.inc(sy.dma_start(out=qno.rearrange("(c p) d -> p c d", p=128),
                                    in_=qn_sb[:].rearrange("p (c d) -> p c d",
                                                           d=D)), 16)
                sy.wait_ge(qo.sem, 16)

            ctx.barrier()

        with contextlib.ExitStack() as ph:
            xn_fm = ctx.sbuf("xn2_fm", [128, 8 * TH], BF16, stack=ph)

            def xnf1(dc):
                return xn_fm[:, dc * TH:(dc + 1) * TH]

            scr = ctx.sbuf("scr_d", [128, 1024], F32, stack=ph)
            with contextlib.ExitStack() as pp:
                xn_tm = ctx.sbuf("xn3_tm", [128, ACH * D], BF16, stack=pp)

                def xn1_ch(c):
                    return xn_tm[:, c * D:(c + 1) * D]

                emit_ln(ctx, "d", x_ch, xn1_ch, ACH, st_all, scr)
                emit_transpose(ctx, "d", xn1_ch,
                               lambda dc, c: xnf1(dc)[:, c * 128:(c + 1) * 128],
                               ACH, ident, pp)
                ctx.barrier()

            wk_sb = ctx.sbuf("kwk", [128, 8 * 1024], BF16, stack=ph)
            wv_sb = ctx.sbuf("kwv", [128, 8 * 1024], BF16, stack=ph)
            kn_rot = ctx.sbuf("kn_rot", [128, 2 * 1024], BF16, stack=ph)
            vn_rot = ctx.sbuf("vn_rot", [128, 2 * 1024], BF16, stack=ph)
            pscr = ctx.sbuf("kps", [128, 1024], F32, stack=ph)
            nst = ctx.sbuf("knst", [128, 3 * 16], F32, stack=ph)
            kv_sb = ctx.sbuf("kv_sb", [64, 1024], F32, stack=ph)

            klw = ctx.sem("klw")

            @blk.gpsimd
            def _(g):
                klw.inc(g.dma_start(out=wk_sb[:].rearrange("p (c o) -> p c o", o=1024),
                                    in_=l1_wkT.rearrange("(c p) o -> p c o", p=128)), 16)
                klw.inc(g.dma_start(out=wv_sb[:].rearrange("p (c o) -> p c o", o=1024),
                                    in_=l1_wvT.rearrange("(c p) o -> p c o", p=128)), 16)

            with contextlib.ExitStack() as pp:
                p_j = [ctx.psum("kp0", [128, 512], F32, stack=pp),
                       ctx.psum("kp1", [128, 512], F32, stack=pp)]
                p_kv = ctx.psum("p_kv", [64, 1024], F32, stack=pp)
                jm = ctx.sem("kjm"); je = ctx.sem("kje"); jn = ctx.sem("kjn"); jv = ctx.sem("kjv")
                kvm = ctx.sem("kvm")

                @blk.tensor
                def _(t):
                    t.wait_ge(klw.sem, 32)
                    i = 0
                    for c in range(NCH):
                        for mat, w_sb_ in ((0, wk_sb), (1, wv_sb)):
                            for half in range(2):
                                if i >= 2:
                                    t.wait_ge(je.sem, i - 1)
                                pb = p_j[i % 2]
                                for dc in range(8):
                                    mmi = t.matmul(pb[:], xnf1(dc)[:, c * 128:(c + 1) * 128],
                                             w_sb_[:, dc * 1024 + half * 512:dc * 1024 + (half + 1) * 512],
                                             start=(dc == 0), stop=(dc == 7))
                                jm.inc(mmi)
                                i += 1
                        t.wait_ge(jn.sem, c + 1)
                        t.wait_ge(jv.sem, c + 1)
                        for h in range(16):
                            kvm.inc(t.matmul(p_kv[:, h * 64:(h + 1) * 64],
                                             kn_rot[:, (c % 2) * 1024 + h * 64:(c % 2) * 1024 + (h + 1) * 64],
                                             vn_rot[:, (c % 2) * 1024 + h * 64:(c % 2) * 1024 + (h + 1) * 64],
                                             start=(c == 0 and h % 8 == 0),
                                             stop=(c == NCH - 1 and h % 8 == 7)))

                jq = ctx.sem("kjq")

                @blk.scalar
                def _(s):
                    i = 0
                    for c in range(NCH):
                        for mat in range(2):
                            for half in range(2):
                                s.wait_ge(jm.sem, i + 1)
                                if mat == 0:
                                    if half == 0 and c >= 1:
                                        s.wait_ge(jn.sem, c)
                                    s.activation(pscr[:, half * 512:(half + 1) * 512],
                                                        p_j[i % 2][:], AF.Identity)
                                    je.inc(s.drain())
                                else:
                                    if half == 0 and c >= 2:
                                        s.wait_ge(kvm.sem, 16 * (c - 1))
                                    s.activation(
                                        vn_rot[:, (c % 2) * 1024 + half * 512:(c % 2) * 1024 + (half + 1) * 512],
                                        p_j[i % 2][:], AF.Identity)
                                    je.inc(s.drain())
                                i += 1
                        jv.inc(s.drain())
                        s.wait_ge(jq.sem, 2 * c + 1)
                        s.activation(nst[:, 16:32], nst[:, 0:16], AF.Ln)
                        s.drain()
                        s.activation(nst[:, 16:32], nst[:, 16:32], AF.Exp, scale=0.5)
                        jq.inc(s.drain())

                @blk.vector
                def _(v):
                    for c in range(NCH):
                        v.wait_ge(je.sem, 4 * c + 2)
                        v.tensor_tensor(scr[:], pscr[:], pscr[:], op=ALU.mult)
                        v.drain()
                        jq.inc(v.tensor_reduce(nst[:, 0:16],
                                               scr[:].rearrange("p (h o) -> p h o", o=64),
                                               axis=AX, op=ALU.add))
                        v.wait_ge(jq.sem, 2 * c + 2)
                        v.tensor_scalar_max(nst[:, 16:32], nst[:, 16:32], 1e-12)
                        v.drain()
                        v.reciprocal(nst[:, 32:48], nst[:, 16:32])
                        v.drain()
                        if c >= 2:
                            v.wait_ge(kvm.sem, 16 * (c - 1))
                        jn.inc(v.tensor_tensor(
                            kn_rot[:, (c % 2) * 1024:(c % 2 + 1) * 1024]
                            .rearrange("p (h o) -> p h o", o=64),
                            pscr[:].rearrange("p (h o) -> p h o", o=64),
                            nst[:, 32:48].rearrange("p (h o) -> p h o", o=1)
                            .broadcast_to([128, 16, 64]),
                            op=ALU.mult))

                kvec = ctx.sem("kvec")
                kve = ctx.sem("kve")

                @blk.scalar
                def _(s):
                    s.wait_ge(kvm.sem, 16 * NCH)
                    s.activation(kv_sb[:], p_kv[:], AF.Identity)
                    kvec.inc(s.drain())

                @blk.sync
                def _(sy):
                    sy.wait_ge(kvec.sem, 1)
                    kve.inc(sy.dma_start(out=kvpo, in_=kv_sb[:]), 16)

                ctx.barrier()


    except _Cut:
        _finish(ctx, blk, x_tm, x1o, ctx.sem('xo'))
        ctx.finish()
        return nc

    @blk.sync
    def _(sy):
        sy.wait_ge(xo.sem, xo.n)
        sy.wait_ge(qo.sem, qo.n)
        sy.wait_ge(ctx.sem("kve").sem, ctx.sem("kve").n)

    ctx.finish()
    return nc


def np_bf16(a):
    import ml_dtypes
    return np.ascontiguousarray(a).astype(ml_dtypes.bfloat16)


def prep_launch1_inputs(I):
    x = I["L0_attn_hidden_states"].reshape(B, T, D)
    qkv_w = I["L0_attn_qkv_w"]; qkv_b = I["L0_attn_qkv_b"]
    qkvT = np.ascontiguousarray(qkv_w.T)
    perm = [0, 4, 1, 5, 2, 6, 3, 7, 8, 12, 9, 13, 10, 14, 11, 15]
    qperm = qkvT[:, :1024].reshape(1024, 16, 64)[:, perm, :].reshape(1024, 1024)
    qkvT = np.ascontiguousarray(np.concatenate([qperm, qkvT[:, 1024:]], axis=1))
    qb_col = np.zeros((128, 8), np.float32)
    for pc, hx in enumerate(PAIR_HX):
        qb_col[0:64, pc] = qkv_b[hx * 64:(hx + 1) * 64]
        qb_col[64:128, pc] = qkv_b[(hx + 4) * 64:(hx + 5) * 64]
    kb_col = np.ascontiguousarray(qkv_b[1024:1280].reshape(2, 128).T)
    o_w = I["L0_attn_o_w"]
    o_wTh = np.ascontiguousarray(
        o_w.T.reshape(16, 64, 1024).transpose(1, 0, 2).reshape(64, 16 * 1024))
    shared = dict(
        qkvT=qkvT, qkv_b=qkv_b[None, :], qb_col=qb_col, kb_col=kb_col,
        scaling=I["L0_attn_scaling"][None, :], o_wTh=o_wTh, o_b=I["L0_attn_o_b"][None, :],
        w_rT=np.ascontiguousarray(I["L0_ffn_W_r"].T),
        w_kT=np.ascontiguousarray(I["L0_ffn_W_k"].T),
        w_vT=np.ascontiguousarray(I["L0_ffn_W_v"].T),
        mu_rc=np.ascontiguousarray(I["L0_ffn_mu_r"].reshape(8, 128).T),
        mu_kc=np.ascontiguousarray(I["L0_ffn_mu_k"].reshape(8, 128).T),
        l1_wqT=np.ascontiguousarray(I["L1_attn_Wq"].T),
        l1_wkT=np.ascontiguousarray(I["L1_attn_Wk"].T),
        l1_wvT=np.ascontiguousarray(I["L1_attn_Wv"].T),
    )
    maps = []
    for c in range(NCORE):
        b, s0 = c // 4, (c % 4) * TPC
        xs = np.zeros((TH, D), np.float32)
        xs[0:TPC] = x[b, s0:s0 + TPC]
        if s0 > 0:
            xs[2048] = x[b, s0 - 1]
            xs[2050] = x[b, s0 - 2]
        if s0 + TPC < T:
            xs[2049] = x[b, s0 + TPC]
        m = dict(shared)
        m["xs"] = xs
        m["xfull"] = x[b]
        m["maskL"] = (np.ones((128, 1), np.float32) if s0 > 0
                      else np.zeros((128, 1), np.float32))
        maps.append(m)
    return maps


def kernel(**inputs):
    cores = list(range(NCORE))
    r1 = run_bass_kernel_spmd(build_launch1(), prep_launch1_inputs(inputs),
                              core_ids=cores).results
    r2 = run_bass_kernel_spmd(build_launch2(), prep_launch2_inputs(inputs, r1),
                              core_ids=cores).results
    r3 = run_bass_kernel_spmd(build_launch3(), prep_launch3_inputs(inputs, r2),
                              core_ids=cores).results
    r4 = run_bass_kernel_spmd(build_launch4(), prep_launch4_inputs(inputs, r3),
                              core_ids=cores).results
    r5 = run_bass_kernel_spmd(build_launch5(), prep_launch5_inputs(inputs, r4),
                              core_ids=cores).results
    out = np.empty((B, T, D), np.float32)
    for c in range(NCORE):
        b, s0 = c // 4, (c % 4) * TPC
        out[b, s0:s0 + TPC] = np.asarray(r5[c]["x4o"])
    return out.reshape(B, T, D)




def build_launch2(stop_after=None):
    nc = bass.Bass("TRN2", target_bir_lowering=False, detect_race_conditions=False)
    stack = contextlib.ExitStack()
    ctx = Ctx(nc, stack)
    blk = ctx.block

    def dram(n, s, d=F32, k="ExternalInput"):
        return nc.dram_tensor(n, s, d, kind=k).ap()

    x1i = dram("x1i", [TH, D])
    qni = dram("qni", [TH, D])
    kvp4 = dram("kvp4", [64, 4096])
    ncst16 = dram("ncst16", [1, 16])
    woT = dram("woT", [D, D])
    wg12 = dram("wg12", [D, 12])
    wupd = dram("wupd", [12, D, 4096])
    bupd = dram("bupd", [12, 4096])
    wdnd = dram("wdnd", [12, 4096, D])
    bdnd = dram("bdnd", [12, D])
    inwT = dram("inwT", [D, 4096])
    inb = dram("inb", [4096])
    convw = dram("convw", [2048, 1, 3])
    convb = dram("convb", [2048])
    dtwT = dram("dtwT", [2048, 2048])
    dtb = dram("dtb", [1, 2048])
    bcdw = dram("bcdw", [2048, 48])
    bcdb = dram("bcdb", [1, 48])
    maskL2 = dram("maskL2", [128, 1])
    maskR2 = dram("maskR2", [128, 1])

    xcd = dram("xcd", [2048, 2048], BF16, "Internal")

    x2o = dram("x2o", [TPC, D], F32, "ExternalOutput")
    uo = dram("uo", [TPC, 16], F32, "ExternalOutput")
    cto = dram("cto", [TPC, 16], F32, "ExternalOutput")
    ldo = dram("ldo", [128, 16], F32, "ExternalOutput")
    xgod = dram("xgod", [2048, TPC], BF16, "ExternalOutput")

    ld = ctx.sem("ld")
    ctx.setup_consts([1e-5, 1.0], ld)
    x_tm = ctx.sbuf("x_tm", [128, ACH * D], F32)
    ident = ctx.sbuf("ident", [128, 128], BF16)
    ones1 = ctx.sbuf("ones1", [1, 128], BF16)
    onescol = ctx.sbuf("onescol", [128, 64], BF16)
    st_all = ctx.sbuf("st_all", [128, ACH * 16], F32)

    def x_ch(c):
        return x_tm[:, c * D:(c + 1) * D]

    @blk.sync
    def _(sy):
        ld.inc(sy.dma_start(out=x_tm[:].rearrange("p (c d) -> p c d", d=D),
                            in_=x1i.rearrange("(c p) d -> p c d", p=128)), 16)

    ldc = ctx.sem("ldc")

    @blk.gpsimd
    def _(g):
        g.memset(ident[:], 0.0)
        g.affine_select(out=ident[:], in_=ident[:], compare_op=ALU.not_equal,
                        fill=1.0, base=0, pattern=[[-1, 128]], channel_multiplier=1)
        g.memset(ones1[:], 1.0)
        ldc.inc(g.memset(onescol[:], 1.0))

    for dec in (blk.vector, blk.scalar, blk.tensor):
        @dec
        def _(eng):
            eng.wait_ge(ld.sem, ld.n)
            eng.wait_ge(ldc.sem, ldc.n)

    try:
        with contextlib.ExitStack() as ph:
            qn_tm = ctx.sbuf("qn_tm", [128, ACH * D], BF16, stack=ph)
            kvp_sb = ctx.sbuf("kvp_sb", [64, 4096], F32, stack=ph)
            kvs_sb = ctx.sbuf("kvs_sb", [64, 2048], F32, stack=ph)
            kv2 = ctx.sbuf("kv2", [128, 1024], BF16, stack=ph)
            kv2lo = ctx.sbuf("kv2lo", [128, 1024], BF16, stack=ph)
            wlo_st = [ctx.sbuf("wlo_st0", [128, 1024], F32, stack=ph),
                      ctx.sbuf("wlo_st1", [128, 1024], F32, stack=ph)]
            woT_lo = ctx.sbuf("woT_lo", [128, 8 * 1024], BF16, stack=ph)
            qf32 = [ctx.sbuf("qf320", [128, D], F32, stack=ph),
                    ctx.sbuf("qf321", [128, D], F32, stack=ph)]
            qlo_c = ctx.sbuf("qlo_c", [128, D], BF16, stack=ph)
            ncr = ctx.sbuf("ncr", [128, 48], F32, stack=ph)
            nbf = ctx.sbuf("nbf", [128, 16], BF16, stack=ph)
            srow_all = ctx.sbuf("srow_all", [128, 8], F32, stack=ph)
            woT_sb = ctx.sbuf("woT_sb", [128, 8 * 1024], BF16, stack=ph)

            la = ctx.sem("la")

            @blk.gpsimd
            def _(g):
                la.inc(g.dma_start(out=qn_tm[:].rearrange("p (c d) -> p c d", d=D),
                                   in_=qni.rearrange("(c p) d -> p c d", p=128)), 16)
                la.inc(g.dma_start(out=kvp_sb[:], in_=kvp4), 16)
                la.inc(g.dma_start(out=ncr[64:65, 0:16], in_=ncst16), 16)
                la.inc(g.dma_start(out=woT_sb[:].rearrange("p (c o) -> p c o", o=1024),
                                   in_=woT.rearrange("(c p) o -> p c o", p=128)), 16)
                wfl = ctx.sem("wfl")
                for dc in range(8):
                    if dc >= 2:
                        g.wait_ge(ctx.sem("wlv").sem, dc - 1)
                    wfl.inc(g.dma_start(
                        out=wlo_st[dc % 2][:],
                        in_=woT[dc * 128:(dc + 1) * 128, :]), 16)

            kva = ctx.sem("kva"); srw = ctx.sem("srw"); sre = ctx.sem("sre")

            @blk.vector
            def _(v):
                v.wait_ge(la.sem, la.n)
                v.tensor_tensor(kvs_sb[:, 0:1024], kvp_sb[:, 0:1024],
                                kvp_sb[:, 1024:2048], op=ALU.add)
                v.tensor_tensor(kvs_sb[:, 1024:2048], kvp_sb[:, 2048:3072],
                                kvp_sb[:, 3072:4096], op=ALU.add)
                v.drain()
                v.tensor_tensor(kvs_sb[:, 0:1024], kvs_sb[:, 0:1024],
                                kvs_sb[:, 1024:2048], op=ALU.add)
                v.drain()
                v.tensor_copy(kv2[0:64, :], kvs_sb[:, 0:1024])
                v.tensor_copy(kv2[64:128, :], kvs_sb[:, 0:1024])
                v.drain()
                v.tensor_tensor(kv2lo[0:64, :], kvs_sb[:, 0:1024], kv2[0:64, :],
                                op=ALU.subtract)
                v.drain()
                v.tensor_copy(kv2lo[64:128, :], kv2lo[0:64, :])
                kva.inc(v.drain())
                for dc in range(8):
                    v.wait_ge(ctx.sem("wfl").sem, 16 * (dc + 1))
                    v.tensor_tensor(woT_lo[:, dc * 1024:(dc + 1) * 1024],
                                    wlo_st[dc % 2][:],
                                    woT_sb[:, dc * 1024:(dc + 1) * 1024],
                                    op=ALU.subtract)
                    ctx.sem("wlv").inc(v.drain())

            @blk.scalar
            def _(s):
                s.wait_ge(la.sem, la.n)
                s.activation(ncr[64:65, 16:32], ncr[64:65, 0:16], AF.Sigmoid)
                s.drain()
                s.activation(ncr[64:65, 32:48], ncr[64:65, 16:32], AF.Exp,
                             scale=-9.010913347279288)
                s.drain()
                s.activation(nbf[64:65, :], ncr[64:65, 32:48], AF.Identity)
                srw.inc(s.drain())

            with contextlib.ExitStack() as pp:
                p_sr = ctx.psum("p_sr", [128, 8], F32, stack=pp)
                srm = ctx.sem("srm")

                @blk.tensor
                def _(t):
                    t.wait_ge(srw.sem, 1)
                    t.matmul(p_sr[0:64, :], onescol[64:65, 0:64],
                             nbf[64:65, 0:16:2], start=True, stop=True)
                    srm.inc(t.matmul(p_sr[64:128, :], onescol[64:65, 0:64],
                                     nbf[64:65, 1:16:2], start=True, stop=True))

                @blk.scalar
                def _(s):
                    s.wait_ge(srm.sem, 1)
                    s.activation(srow_all[:], p_sr[:], AF.Identity)
                    sre.inc(s.drain())

                ctx.barrier()

            if stop_after == "prep2":
                raise _Cut()

            with contextlib.ExitStack() as pp:
                p_qt = [ctx.psum("p_qt0", [128, 1024], BF16, stack=pp),
                        ctx.psum("p_qt1", [128, 1024], BF16, stack=pp)]
                p_at = [ctx.psum("p_at0", [128, 512], F32, stack=pp),
                        ctx.psum("p_at1", [128, 512], F32, stack=pp)]
                p_op = [ctx.psum("p_op0", [128, 512], F32, stack=pp),
                        ctx.psum("p_op1", [128, 512], F32, stack=pp)]
                qnT_sb = [ctx.sbuf("qnT0", [128, 1024], BF16, stack=pp),
                          ctx.sbuf("qnT1", [128, 1024], BF16, stack=pp)]
                qnL_sb = [ctx.sbuf("qnL0", [128, 1024], BF16, stack=pp),
                          ctx.sbuf("qnL1", [128, 1024], BF16, stack=pp)]
                atT_sb = ctx.sbuf("atT", [128, 1024], BF16, stack=pp)
                atL_sb = ctx.sbuf("atL", [128, 1024], BF16, stack=pp)
                ascr = ctx.sbuf("ascr", [128, 1024], F32, stack=pp)
                oop_sb = ctx.sbuf("oop", [128, 1024], F32, stack=pp)

                tq = ctx.sem("tq"); eq = ctx.sem("eq"); am = ctx.sem("am")
                ae = ctx.sem("ae"); om = ctx.sem("om"); oe = ctx.sem("oe")
                xr = ctx.sem("xr"); qfl = ctx.sem("qfl"); qlv = ctx.sem("qlv")
                alv = ctx.sem("alv")

                @blk.gpsimd
                def _(g):
                    for c in range(ACH):
                        if c >= 2:
                            g.wait_ge(qlv.sem, c - 1)
                        qfl.inc(g.dma_start(
                            out=qf32[c % 2][:],
                            in_=qni[c * 128:(c + 1) * 128, :]), 16)

                @blk.tensor
                def _(t):
                    t.wait_ge(kva.sem, 1)
                    t.wait_ge(ctx.sem("wlv").sem, 8)
                    for c in range(ACH):
                        if c >= 2:
                            t.wait_ge(eq.sem, 2 * c - 2)
                        for dc in range(8):
                            mmi = t.transpose(
                                p_qt[c % 2][:, dc * 128:(dc + 1) * 128],
                                qn_tm[:, c * D + dc * 128:c * D + (dc + 1) * 128],
                                ident[:])
                        tq.inc(mmi)
                        t.wait_ge(ctx.sem("qlv").sem, c + 1)
                        t.wait_ge(eq.sem, 2 * c + 1)
                        for dc in range(8):
                            mmi = t.transpose(
                                p_qt[c % 2][:, dc * 128:(dc + 1) * 128],
                                qlo_c[:, dc * 128:(dc + 1) * 128],
                                ident[:])
                        tq.inc(mmi)
                        t.wait_ge(eq.sem, 2 * c + 2)
                        for grp in range(2):
                            if c >= 1:
                                t.wait_ge(ae.sem, 2 * (c - 1) + grp + 1)
                            pb = p_at[grp]
                            for dq in range(4):
                                dc = grp * 4 + dq
                                for j in range(2):
                                    h = 2 * dc + j
                                    t.matmul(
                                        pb[j * 64:(j + 1) * 64,
                                           dq * 128:(dq + 1) * 128],
                                        kv2[j * 64:(j + 1) * 64,
                                            h * 64:(h + 1) * 64],
                                        qnT_sb[c % 2][j * 64:(j + 1) * 64,
                                                      dc * 128:(dc + 1) * 128],
                                        start=True, stop=False)
                                    t.matmul(
                                        pb[j * 64:(j + 1) * 64,
                                           dq * 128:(dq + 1) * 128],
                                        kv2[j * 64:(j + 1) * 64,
                                            h * 64:(h + 1) * 64],
                                        qnL_sb[c % 2][j * 64:(j + 1) * 64,
                                                      dc * 128:(dc + 1) * 128],
                                        start=False, stop=False)
                                    mmi = t.matmul(
                                        pb[j * 64:(j + 1) * 64,
                                           dq * 128:(dq + 1) * 128],
                                        kv2lo[j * 64:(j + 1) * 64,
                                              h * 64:(h + 1) * 64],
                                        qnT_sb[c % 2][j * 64:(j + 1) * 64,
                                                      dc * 128:(dc + 1) * 128],
                                        start=False, stop=True)
                            am.inc(mmi)
                        t.wait_ge(ae.sem, 2 * c + 2)
                        t.wait_ge(ctx.sem("alv").sem, c + 1)
                        for half in range(2):
                            if c >= 1:
                                t.wait_ge(oe.sem, c)
                            pb = p_op[half]
                            for dc in range(8):
                                t.matmul(
                                    pb[:], atT_sb[:, dc * 128:(dc + 1) * 128],
                                    woT_sb[:, dc * 1024 + half * 512:
                                           dc * 1024 + (half + 1) * 512],
                                    start=(dc == 0), stop=False)
                                t.matmul(
                                    pb[:], atL_sb[:, dc * 128:(dc + 1) * 128],
                                    woT_sb[:, dc * 1024 + half * 512:
                                           dc * 1024 + (half + 1) * 512],
                                    start=False, stop=False)
                                mmi = t.matmul(
                                    pb[:], atT_sb[:, dc * 128:(dc + 1) * 128],
                                    woT_lo[:, dc * 1024 + half * 512:
                                           dc * 1024 + (half + 1) * 512],
                                    start=False, stop=(dc == 7))
                            om.inc(mmi)

                @blk.scalar
                def _(s):
                    s.wait_ge(sre.sem, 1)
                    for c in range(ACH):
                        s.wait_ge(tq.sem, 2 * c + 1)
                        for dc in range(8):
                            s.activation(qnT_sb[c % 2][:, dc * 128:(dc + 1) * 128],
                                         p_qt[c % 2][:, dc * 128:(dc + 1) * 128],
                                         AF.Identity)
                        eq.inc(s.drain())
                        s.wait_ge(tq.sem, 2 * c + 2)
                        for dc in range(8):
                            s.activation(qnL_sb[c % 2][:, dc * 128:(dc + 1) * 128],
                                         p_qt[c % 2][:, dc * 128:(dc + 1) * 128],
                                         AF.Identity)
                        eq.inc(s.drain())
                        for grp in range(2):
                            s.wait_ge(am.sem, 2 * c + grp + 1)
                            if c >= 1 and grp == 0:
                                s.wait_ge(alv.sem, c)
                            for dq in range(4):
                                dc = grp * 4 + dq
                                s.activation(atT_sb[:, dc * 128:(dc + 1) * 128],
                                             p_at[grp][:, dq * 128:(dq + 1) * 128],
                                             AF.Identity,
                                             scale=srow_all[:, dc:dc + 1])
                                s.activation(ascr[:, dc * 128:(dc + 1) * 128],
                                             p_at[grp][:, dq * 128:(dq + 1) * 128],
                                             AF.Identity,
                                             scale=srow_all[:, dc:dc + 1])
                            ae.inc(s.drain())
                        if c >= 1:
                            s.wait_ge(xr.sem, c)
                        for half in range(2):
                            s.wait_ge(om.sem, 2 * c + half + 1)
                            s.activation(oop_sb[:, half * 512:(half + 1) * 512],
                                         p_op[half][:], AF.Identity)
                        oe.inc(s.drain())

                @blk.vector
                def _(v):
                    for c in range(ACH):
                        v.wait_ge(qfl.sem, 16 * (c + 1))
                        if c >= 1:
                            v.wait_ge(tq.sem, 2 * c)
                        v.tensor_tensor(qlo_c[:], qf32[c % 2][:],
                                        qn_tm[:, c * D:(c + 1) * D],
                                        op=ALU.subtract)
                        qlv.inc(v.drain())
                        v.wait_ge(ae.sem, 2 * c + 2)
                        v.tensor_tensor(atL_sb[:], ascr[:], atT_sb[:],
                                        op=ALU.subtract)
                        alv.inc(v.drain())
                        v.wait_ge(oe.sem, c + 1)
                        v.tensor_add(x_ch(c)[:], x_ch(c)[:], oop_sb[:])
                        xr.inc(v.drain())

                ctx.barrier()

        if stop_after == "l1a":
            raise _Cut()

        with contextlib.ExitStack() as ph:
            xnb_fm = ctx.sbuf("xnb_fm", [128, 8 * TH], BF16, stack=ph)

            def xnbf(dc):
                return xnb_fm[:, dc * TH:(dc + 1) * TH]

            with contextlib.ExitStack() as pp:
                xnb_tm = ctx.sbuf("xnb_tm", [128, ACH * D], BF16, stack=pp)
                scr = ctx.sbuf("scr2", [128, 1024], F32, stack=pp)

                def xnb_ch(c):
                    return xnb_tm[:, c * D:(c + 1) * D]

                emit_ln(ctx, "b2", x_ch, xnb_ch, ACH, st_all, scr)
                emit_transpose(ctx, "b2", xnb_ch,
                               lambda dc, c: xnbf(dc)[:, c * 128:(c + 1) * 128],
                               ACH, ident, pp)
                ctx.barrier()

            if stop_after == "ln1b":
                raise _Cut()

            wg_sb = ctx.sbuf("wg_sb", [128, 8 * 12], BF16, stack=ph)
            gl_sb = ctx.sbuf("gl_sb", [128, ACH * 12], F32, stack=ph)
            gw_sb = ctx.sbuf("gw_sb", [128, ACH * 12], F32, stack=ph)
            gst = ctx.sbuf("gst", [128, 8], F32, stack=ph)
            gsc = ctx.sbuf("gsc", [128, 12], F32, stack=ph)
            gbf = ctx.sbuf("gbf", [128, 12], BF16, stack=ph)
            wT_all = ctx.sbuf("wT_all", [12, TH], BF16, stack=ph)
            bdn_bf = ctx.sbuf("bdn_bf", [12, 1024], BF16, stack=ph)
            bup_sb = ctx.sbuf("bup_sb", [128, 32], F32, stack=ph)
            bds_sb = ctx.sbuf("bds_sb", [128, 1024], F32, stack=ph)
            pg2 = ph.enter_context(contextlib.ExitStack())
            wgf12 = ctx.sbuf("wgf12", [128, 8 * 12], F32, stack=pg2)
            wgl12 = ctx.sbuf("wgl12", [128, 8 * 12], BF16, stack=pg2)
            glo_scr = ctx.sbuf("glo_scr", [128, 1024], F32, stack=pg2)
            glo_hi = ctx.sbuf("glo_hi", [128, 1024], BF16, stack=pg2)
            glo_tm = ctx.sbuf("glo_tm", [128, 1024], BF16, stack=pg2)
            glo_fm = ctx.sbuf("glo_fm", [128, 8 * 128], BF16, stack=pg2)

            lg = ctx.sem("lg")

            @blk.gpsimd
            def _(g):
                lg.inc(g.dma_start(out=wg_sb[:].rearrange("p (c e) -> p c e", e=12),
                                   in_=wg12.rearrange("(c p) e -> p c e", p=128)), 16)
                lg.inc(g.dma_start(out=bdn_bf[:], in_=bdnd), 16)
                lg.inc(g.dma_start(out=wgf12[:].rearrange("p (c e) -> p c e", e=12),
                                   in_=wg12.rearrange("(c p) e -> p c e", p=128)), 16)

            wgv2 = ctx.sem("wgv2")

            @blk.vector
            def _(v):
                v.wait_ge(lg.sem, 48)
                v.tensor_tensor(wgl12[:], wgf12[:], wg_sb[:], op=ALU.subtract)
                wgv2.inc(v.drain())

            def glc(c):
                return gl_sb[:, c * 12:(c + 1) * 12]

            def gwc(c):
                return gw_sb[:, c * 12:(c + 1) * 12]

            with contextlib.ExitStack() as pp:
                p_gl = [ctx.psum("p_gl0", [128, 12], F32, stack=pp),
                        ctx.psum("p_gl1", [128, 12], F32, stack=pp)]
                p_lo2 = ctx.psum("p_lo2", [128, 1024], BF16, stack=pp)
                p_wt = ctx.psum("p_wt", [12, 128], BF16, stack=pp)
                p_bd = [ctx.psum("p_bd0", [128, 512], F32, stack=pp),
                        ctx.psum("p_bd1", [128, 512], F32, stack=pp)]
                gm = ctx.sem("gm"); ge = ctx.sem("ge"); gv1 = ctx.sem("gv1")
                lsc = ctx.sem("lsc"); lsv = ctx.sem("lsv"); ltm = ctx.sem("ltm")
                lte = ctx.sem("lte")
                gs1 = ctx.sem("gs1"); gv2 = ctx.sem("gv2"); gtm = ctx.sem("gtm")
                gte = ctx.sem("gte"); bm = ctx.sem("bm"); be = ctx.sem("be")
                bx = ctx.sem("bx")

                @blk.tensor
                def _(t):
                    t.wait_ge(lg.sem, 48)
                    t.wait_ge(wgv2.sem, 1)
                    for c in range(ACH):
                        t.wait_ge(lsv.sem, c + 1)
                        if c >= 1:
                            t.wait_ge(lte.sem, c)
                        for dc in range(8):
                            mmi = t.transpose(p_lo2[:, dc * 128:(dc + 1) * 128],
                                              glo_tm[:, dc * 128:(dc + 1) * 128],
                                              ident[:])
                        ltm.inc(mmi)
                        if c >= 2:
                            t.wait_ge(ge.sem, c - 1)
                        t.wait_ge(lte.sem, c + 1)
                        for dc in range(8):
                            t.matmul(p_gl[c % 2][:],
                                     xnbf(dc)[:, c * 128:(c + 1) * 128],
                                     wg_sb[:, dc * 12:(dc + 1) * 12],
                                     start=(dc == 0), stop=False)
                        for dc in range(8):
                            t.matmul(p_gl[c % 2][:],
                                     xnbf(dc)[:, c * 128:(c + 1) * 128],
                                     wgl12[:, dc * 12:(dc + 1) * 12],
                                     start=False, stop=False)
                        for dc in range(8):
                            mmi = t.matmul(p_gl[c % 2][:],
                                           glo_fm[:, dc * 128:(dc + 1) * 128],
                                           wg_sb[:, dc * 12:(dc + 1) * 12],
                                           start=False, stop=(dc == 7))
                        gm.inc(mmi)
                        t.wait_ge(gv2.sem, c + 1)
                        gtm.inc(t.transpose(p_wt[:], gbf[:], ident[:]))
                        t.wait_ge(gte.sem, c + 1)
                        if c >= 2:
                            t.wait_ge(bx.sem, c - 1)
                        for half in range(2):
                            mmi = t.matmul(p_bd[half][:],
                                           wT_all[:, c * 128:(c + 1) * 128],
                                           bdn_bf[:, half * 512:(half + 1) * 512],
                                           start=True, stop=True)
                        bm.inc(mmi)

                @blk.scalar
                def _(s):
                    for c in range(ACH):
                        if c >= 1:
                            s.wait_ge(lsv.sem, c)
                        s.activation(glo_scr[:], x_ch(c)[:], AF.Identity,
                                     scale=st_all[:, c * 16 + 15:c * 16 + 16],
                                     bias=st_all[:, c * 16 + 14:c * 16 + 15])
                        s.activation(glo_hi[:], x_ch(c)[:], AF.Identity,
                                     scale=st_all[:, c * 16 + 15:c * 16 + 16],
                                     bias=st_all[:, c * 16 + 14:c * 16 + 15])
                        lsc.inc(s.drain())
                        s.wait_ge(ltm.sem, c + 1)
                        for dc in range(8):
                            s.activation(glo_fm[:, dc * 128:(dc + 1) * 128],
                                         p_lo2[:, dc * 128:(dc + 1) * 128],
                                         AF.Identity)
                        lte.inc(s.drain())
                        s.wait_ge(gm.sem, c + 1)
                        s.activation(glc(c), p_gl[c % 2][:], AF.Identity)
                        ge.inc(s.drain())
                        s.wait_ge(gv1.sem, c + 1)
                        s.activation(gwc(c), glc(c), AF.Exp, bias=gst[:, 1:2])
                        s.activation(gst[:, 6:7], gst[:, 4:5], AF.Exp)
                        s.drain()
                        s.activation(gst[:, 7:8], gst[:, 5:6], AF.Exp)
                        gs1.inc(s.drain())
                        s.wait_ge(gtm.sem, c + 1)
                        s.activation(wT_all[:, c * 128:(c + 1) * 128], p_wt[:],
                                     AF.Identity)
                        gte.inc(s.drain())
                        s.wait_ge(bm.sem, c + 1)
                        if c >= 1:
                            s.wait_ge(bx.sem, c)
                        s.activation(bds_sb[:, 0:512], p_bd[0][:], AF.Identity)
                        s.activation(bds_sb[:, 512:1024], p_bd[1][:], AF.Identity)
                        be.inc(s.drain())

                @blk.vector
                def _(v):
                    for c in range(ACH):
                        v.wait_ge(lsc.sem, c + 1)
                        v.tensor_tensor(glo_tm[:], glo_scr[:], glo_hi[:],
                                        op=ALU.subtract)
                        lsv.inc(v.drain())
                        v.wait_ge(ge.sem, c + 1)
                        v.tensor_reduce(gst[:, 0:1], glc(c), axis=AX, op=ALU.max)
                        v.drain()
                        v.tensor_scalar_mul(gst[:, 1:2], gst[:, 0:1], -1.0)
                        v.tensor_tensor(gwc(c), glc(c),
                                        gst[:, 0:1].broadcast_to([128, 12]),
                                        op=ALU.is_equal)
                        v.drain()
                        v.tensor_scalar_mul(gwc(c), gwc(c), -1e30)
                        v.drain()
                        v.tensor_add(gwc(c), gwc(c), glc(c))
                        v.drain()
                        v.tensor_reduce(gst[:, 2:3], gwc(c), axis=AX, op=ALU.max)
                        v.drain()
                        v.tensor_tensor(gst[:, 4:5], gst[:, 2:3], gst[:, 1:2],
                                        op=ALU.add)
                        v.tensor_tensor(gsc[:], gwc(c),
                                        gst[:, 2:3].broadcast_to([128, 12]),
                                        op=ALU.is_equal)
                        v.drain()
                        v.tensor_scalar_mul(gsc[:], gsc[:], -1e30)
                        v.drain()
                        v.tensor_add(gwc(c), gwc(c), gsc[:])
                        v.drain()
                        v.tensor_reduce(gst[:, 3:4], gwc(c), axis=AX, op=ALU.max)
                        v.drain()
                        v.tensor_tensor(gst[:, 5:6], gst[:, 3:4], gst[:, 1:2],
                                        op=ALU.add)
                        gv1.inc(v.drain())
                        v.wait_ge(gs1.sem, c + 1)
                        v.tensor_tensor(gsc[:], glc(c),
                                        gst[:, 3:4].broadcast_to([128, 12]),
                                        op=ALU.is_ge)
                        v.tensor_tensor(gst[:, 6:7], gst[:, 6:7], gst[:, 7:8],
                                        op=ALU.add)
                        v.drain()
                        v.tensor_scalar_add(gst[:, 6:7], gst[:, 6:7], 1.0)
                        v.drain()
                        v.reciprocal(gst[:, 7:8], gst[:, 6:7])
                        v.drain()
                        v.tensor_tensor(gwc(c), gwc(c), gsc[:], op=ALU.mult)
                        v.drain()
                        v.tensor_tensor(gwc(c), gwc(c),
                                        gst[:, 7:8].broadcast_to([128, 12]),
                                        op=ALU.mult)
                        v.drain()
                        if c >= 1:
                            v.wait_ge(gtm.sem, c)
                        v.tensor_copy(gbf[:], gwc(c))
                        gv2.inc(v.drain())
                        v.wait_ge(be.sem, c + 1)
                        v.tensor_add(x_ch(c)[:], x_ch(c)[:], bds_sb[:])
                        bx.inc(v.drain())

                ctx.barrier()

            pg2.close()

            if stop_after == "gate":
                raise _Cut()

            GT = [512, 512, 512, 512, 128]
            GOFF = [0, 512, 1024, 1536, 2048]
            with contextlib.ExitStack() as pp:
                wup_sb = ctx.sbuf("wup_sb", [128, 8 * 2048], BF16, stack=pp)
                wdn_sb = ctx.sbuf("wdn_sb", [128, 16 * 1024], BF16, stack=pp)
                hT_sb = ctx.sbuf("hT_sb", [128, 16 * 512], BF16, stack=pp)
                dop_sb = [ctx.sbuf("dop0", [128, 512], F32, stack=pp),
                          ctx.sbuf("dop1", [128, 512], F32, stack=pp)]
                p_h = [ctx.psum("p_h0", [128, 512], F32, stack=pp),
                       ctx.psum("p_h1", [128, 512], F32, stack=pp)]
                p_d = [ctx.psum("p_d0", [128, 512], F32, stack=pp),
                       ctx.psum("p_d1", [128, 512], F32, stack=pp)]

                wl3 = ctx.sem("wl3"); bl3 = ctx.sem("bl3")
                hm = ctx.sem("hm"); he = ctx.sem("he"); dm = ctx.sem("dm")
                dev = ctx.sem("dev"); dx = ctx.sem("dx")

                @blk.gpsimd
                def _(g):
                    for e in range(12):
                        if e >= 1:
                            g.wait_ge(he.sem, e * 160)
                        bl3.inc(g.dma_start(
                            out=bup_sb[:],
                            in_=bupd[e:e + 1, :].rearrange("o (fc p) -> p (o fc)",
                                                           p=128),
                            allow_slow_non_contiguous=True), 16)
                        for fh in range(2):
                            i = e * 2 + fh
                            if i >= 1:
                                g.wait_ge(hm.sem, i * 80)
                            wl3.inc(g.dma_start(
                                out=wup_sb[:].rearrange("p (dc f) -> p dc f", f=2048),
                                in_=wupd[e:e + 1, :, fh * 2048:(fh + 1) * 2048]
                                    .rearrange("o (dc p) f -> p dc (o f)", p=128)), 16)
                            if i >= 1:
                                g.wait_ge(dm.sem, i * 34)
                            wl3.inc(g.dma_start(
                                out=wdn_sb[:].rearrange("p (fc o) -> p fc o", o=1024),
                                in_=wdnd[e:e + 1, fh * 2048:(fh + 1) * 2048, :]
                                    .rearrange("o (fc p) d -> p fc (o d)", p=128)), 16)

                @blk.tensor
                def _(t):
                    nh = 0
                    nd = 0
                    for e in range(12):
                        for fh in range(2):
                            i = e * 2 + fh
                            t.wait_ge(wl3.sem, 32 * i + 16)
                            for g5 in range(5):
                                for fc in range(16):
                                    if nh >= 2:
                                        t.wait_ge(he.sem, nh - 1)
                                    pb = p_h[nh % 2]
                                    for dc in range(8):
                                        mmi = t.matmul(
                                            pb[:, 0:GT[g5]],
                                            wup_sb[:, dc * 2048 + fc * 128:
                                                   dc * 2048 + (fc + 1) * 128],
                                            xnbf(dc)[:, GOFF[g5]:GOFF[g5] + GT[g5]],
                                            start=(dc == 0), stop=(dc == 7))
                                    hm.inc(mmi)
                                    nh += 1
                                t.wait_ge(he.sem, nh)
                                t.wait_ge(wl3.sem, 32 * (i + 1))
                                for ts in range(GT[g5] // 128):
                                    for half in range(2):
                                        if nd >= 2:
                                            t.wait_ge(dev.sem, nd - 1)
                                        pb = p_d[nd % 2]
                                        for fc in range(16):
                                            mmi = t.matmul(
                                                pb[:],
                                                hT_sb[:, fc * 512 + ts * 128:
                                                      fc * 512 + (ts + 1) * 128],
                                                wdn_sb[:, fc * 1024 + half * 512:
                                                       fc * 1024 + (half + 1) * 512],
                                                start=(fc == 0), stop=(fc == 15))
                                        dm.inc(mmi)
                                        nd += 1

                @blk.scalar
                def _(s):
                    nh = 0
                    nd = 0
                    for e in range(12):
                        s.wait_ge(bl3.sem, 16 * (e + 1))
                        for fh in range(2):
                            for g5 in range(5):
                                for fc in range(16):
                                    s.wait_ge(hm.sem, nh + 1)
                                    s.activation(
                                        hT_sb[:, fc * 512:fc * 512 + GT[g5]],
                                        p_h[nh % 2][:, 0:GT[g5]], AF.Gelu,
                                        bias=bup_sb[:, fh * 16 + fc:
                                                    fh * 16 + fc + 1])
                                    he.inc(s.drain())
                                    nh += 1
                                for ts in range(GT[g5] // 128):
                                    c = g5 * 4 + ts
                                    for half in range(2):
                                        s.wait_ge(dm.sem, nd + 1)
                                        if nd >= 2:
                                            s.wait_ge(dx.sem, nd - 1)
                                        s.activation(
                                            dop_sb[nd % 2][:],
                                            p_d[nd % 2][:], AF.Identity,
                                            scale=gw_sb[:, c * 12 + e:
                                                        c * 12 + e + 1])
                                        dev.inc(s.drain())
                                        nd += 1

                @blk.vector
                def _(v):
                    nd = 0
                    for e in range(12):
                        for fh in range(2):
                            for g5 in range(5):
                                for ts in range(GT[g5] // 128):
                                    c = g5 * 4 + ts
                                    for half in range(2):
                                        v.wait_ge(dev.sem, nd + 1)
                                        v.tensor_add(
                                            x_ch(c)[:, half * 512:(half + 1) * 512],
                                            x_ch(c)[:, half * 512:(half + 1) * 512],
                                            dop_sb[nd % 2][:])
                                        dx.inc(v.drain())
                                        nd += 1

                ctx.barrier()

        if stop_after == "l1f":
            raise _Cut()

        with contextlib.ExitStack() as ph:
            xn2_fm = ctx.sbuf("xn2_fm", [128, 8 * TH], BF16, stack=ph)
            scr3 = ctx.sbuf("scr3", [128, 1024], F32, stack=ph)

            def xn2f(dc):
                return xn2_fm[:, dc * TH:(dc + 1) * TH]

            with contextlib.ExitStack() as pp:
                xn2_tm = ctx.sbuf("xn2_tm", [128, ACH * D], BF16, stack=pp)

                def xn2_ch(c):
                    return xn2_tm[:, c * D:(c + 1) * D]

                emit_ln(ctx, "e2", x_ch, xn2_ch, ACH, st_all, scr3)
                emit_transpose(ctx, "e2", xn2_ch,
                               lambda dc, c: xn2f(dc)[:, c * 128:(c + 1) * 128],
                               ACH, ident, pp)
                ctx.barrier()

            if stop_after == "ln2":
                raise _Cut()

            pE = ph.enter_context(contextlib.ExitStack())
            inw_sb = ctx.sbuf("inw_sb", [128, 8 * 4096], BF16, stack=pE)
            inb_sb = ctx.sbuf("inb_sb", [128, 32], F32, stack=pE)
            cw_sb = ctx.sbuf("cw_sb", [128, 16 * 3], F32, stack=pE)
            cb_sb = ctx.sbuf("cb_sb", [128, 16], F32, stack=pE)
            mkl_sb = ctx.sbuf("mkl_sb", [128, 1], F32, stack=pE)
            mkr_sb = ctx.sbuf("mkr_sb", [128, 1], F32, stack=pE)

            li = ctx.sem("li")

            @blk.gpsimd
            def _(g):
                li.inc(g.dma_start(
                    out=inw_sb[:].rearrange("p (dc f) -> p dc f", f=4096),
                    in_=inwT.rearrange("(dc p) f -> p dc f", p=128)), 16)
                li.inc(g.dma_start(out=inb_sb[:],
                                   in_=inb.rearrange("(fb p) -> p fb", p=128),
                                   allow_slow_non_contiguous=True), 16)
                li.inc(g.dma_start(
                    out=cw_sb[:].rearrange("p (f th) -> p f th", th=3),
                    in_=convw.rearrange("(f p) o th -> p f (o th)", p=128),
                    allow_slow_non_contiguous=True), 16)
                li.inc(g.dma_start(out=cb_sb[:],
                                   in_=convb.rearrange("(fb p) -> p fb", p=128),
                                   allow_slow_non_contiguous=True), 16)
                li.inc(g.dma_start(out=mkl_sb[:], in_=maskL2), 16)
                li.inc(g.dma_start(out=mkr_sb[:], in_=maskR2), 16)

            with _nullctx():
                pp = pE
                xmT = ctx.sbuf("xmT", [128, 2052], BF16, stack=pp)
                hh2 = ctx.sbuf("hh2", [128, 2], F32, stack=pp)
                cv1 = ctx.sbuf("cv1", [128, 2048], F32, stack=pp)
                cv2 = ctx.sbuf("cv2", [128, 2048], BF16, stack=pp)
                xcT = [ctx.sbuf("xcT0", [128, 2048], BF16, stack=pp),
                       ctx.sbuf("xcT1", [128, 2048], BF16, stack=pp)]
                xg_sb = [ctx.sbuf("xg_sb0", [128, 2048], BF16, stack=pp),
                         ctx.sbuf("xg_sb1", [128, 2048], BF16, stack=pp)]
                p_ip = [ctx.psum("p_ip0", [128, 512], F32, stack=pp),
                        ctx.psum("p_ip1", [128, 512], F32, stack=pp)]
                p_hh = ctx.psum("p_hh", [128, 2], F32, stack=pp)

                im = ctx.sem("im"); ie = ctx.sem("ie"); hm2 = ctx.sem("hm2")
                he2 = ctx.sem("he2"); cvd = ctx.sem("cvd"); cve = ctx.sem("cve")
                xco = ctx.sem("xco"); xgs = ctx.sem("xgs")

                @blk.tensor
                def _(t):
                    t.wait_ge(li.sem, li.n)
                    ni = 0
                    for fb in range(32):
                        main = fb < 16
                        for tg in range(4):
                            if ni >= 2:
                                t.wait_ge(ie.sem, ni - 1)
                            pb = p_ip[ni % 2]
                            for dc in range(8):
                                mmi = t.matmul(pb[:],
                                               inw_sb[:, dc * 4096 + fb * 128:
                                                      dc * 4096 + (fb + 1) * 128],
                                               xn2f(dc)[:, tg * 512:(tg + 1) * 512],
                                               start=(dc == 0), stop=(dc == 7))
                            im.inc(mmi)
                            ni += 1
                        if main:
                            if fb >= 1:
                                t.wait_ge(he2.sem, fb)
                            for dc in range(8):
                                mmi = t.matmul(p_hh[:],
                                               inw_sb[:, dc * 4096 + fb * 128:
                                                      dc * 4096 + (fb + 1) * 128],
                                               xn2f(dc)[:, 2048:2050],
                                               start=(dc == 0), stop=(dc == 7))
                            hm2.inc(mmi)

                @blk.scalar
                def _(s):
                    ni = 0
                    for fb in range(32):
                        main = fb < 16
                        if main and fb >= 1:
                            s.wait_ge(cvd.sem, fb)
                        for tg in range(4):
                            s.wait_ge(im.sem, ni + 1)
                            if main:
                                s.activation(xmT[:, 1 + tg * 512:1 + (tg + 1) * 512],
                                             p_ip[ni % 2][:], AF.Identity,
                                             bias=inb_sb[:, fb:fb + 1])
                            else:
                                if tg == 0 and fb >= 18:
                                    s.wait_ge(xgs.sem, 16 * (fb - 17))
                                s.activation(xg_sb[fb % 2][:, tg * 512:(tg + 1) * 512],
                                             p_ip[ni % 2][:], AF.Silu,
                                             bias=inb_sb[:, fb:fb + 1])
                            ie.inc(s.drain())
                            ni += 1
                        if main:
                            s.wait_ge(hm2.sem, fb + 1)
                            s.activation(hh2[:], p_hh[:], AF.Identity,
                                         bias=inb_sb[:, fb:fb + 1])
                            he2.inc(s.drain())
                            s.wait_ge(cvd.sem, fb + 1)
                            if fb >= 2:
                                s.wait_ge(xco.sem, 16 * (fb - 1))
                            s.activation(xcT[fb % 2][:], cv1[:], AF.Silu,
                                         bias=cb_sb[:, fb:fb + 1])
                            cve.inc(s.drain())

                @blk.vector
                def _(v):
                    for fb in range(16):
                        v.wait_ge(ie.sem, 4 * fb + 4)
                        v.wait_ge(he2.sem, fb + 1)
                        if fb >= 1:
                            v.wait_ge(cve.sem, fb)
                        v.tensor_tensor(xmT[:, 0:1], hh2[:, 0:1], mkl_sb[:],
                                        op=ALU.mult)
                        v.tensor_tensor(xmT[:, 2049:2050], hh2[:, 1:2], mkr_sb[:],
                                        op=ALU.mult)
                        v.drain()
                        v.tensor_tensor(cv1[:], xmT[:, 0:2048],
                                        cw_sb[:, fb * 3:fb * 3 + 1]
                                        .broadcast_to([128, 2048]), op=ALU.mult)
                        v.tensor_tensor(cv2[:], xmT[:, 1:2049],
                                        cw_sb[:, fb * 3 + 1:fb * 3 + 2]
                                        .broadcast_to([128, 2048]), op=ALU.mult)
                        v.drain()
                        v.tensor_add(cv1[:], cv1[:], cv2[:])
                        v.tensor_tensor(cv2[:], xmT[:, 2:2050],
                                        cw_sb[:, fb * 3 + 2:fb * 3 + 3]
                                        .broadcast_to([128, 2048]), op=ALU.mult)
                        v.drain()
                        v.tensor_add(cv1[:], cv1[:], cv2[:])
                        cvd.inc(v.drain())

                @blk.sync
                def _(sy):
                    for fb in range(32):
                        if fb < 16:
                            sy.wait_ge(cve.sem, fb + 1)
                            xco.inc(sy.dma_start(
                                out=xcd[fb * 128:(fb + 1) * 128, :],
                                in_=xcT[fb % 2][:]), 16)
                        else:
                            sy.wait_ge(ie.sem, 4 * fb + 4)
                            xgs.inc(sy.dma_start(
                                out=xgod[(fb - 16) * 128:(fb - 15) * 128, :],
                                in_=xg_sb[fb % 2][:]), 16)
                    sy.wait_ge(xco.sem, 256)
                    sy.wait_ge(xgs.sem, 256)

                ctx.barrier()

            pE.close()

            if stop_after == "conv":
                raise _Cut()

            with contextlib.ExitStack() as pp:
                dtw_sb = ctx.sbuf("dtw_sb", [128, 16 * 1024], BF16, stack=pp)
                bcd_sb = ctx.sbuf("bcd_sb", [128, 16 * 48], BF16, stack=pp)
                bcdb_bf = ctx.sbuf("bcdb_bf", [1, 48], BF16, stack=pp)
                dtb_bf = ctx.sbuf("dtb_bf", [1, 2048], BF16, stack=pp)
                xc_sl = [ctx.sbuf("xc_sl0", [128, 16 * 128], BF16, stack=pp),
                         ctx.sbuf("xc_sl1", [128, 16 * 128], BF16, stack=pp)]
                raw_sb = [ctx.sbuf("raw0", [128, 512], F32, stack=pp),
                          ctx.sbuf("raw1", [128, 512], F32, stack=pp)]
                sp_sb = [ctx.sbuf("sp0", [128, 512], F32, stack=pp),
                         ctx.sbuf("sp1", [128, 512], F32, stack=pp)]
                rl_sb = ctx.sbuf("rl_sb", [128, 512], F32, stack=pp)
                bc3 = ctx.sbuf("bc3", [128, 48], F32, stack=pp)
                u_all = ctx.sbuf("u_all", [128, 16 * 16], F32, stack=pp)
                ct_all = ctx.sbuf("ct_all", [128, 16 * 16], F32, stack=pp)
                dtacc = ctx.sbuf("dtacc", [128, 16], F32, stack=pp)
                ldv = ctx.sbuf("ldv", [128, 16], F32, stack=pp)
                p_dt = [ctx.psum("p_dt0", [128, 512], F32, stack=pp),
                        ctx.psum("p_dt1", [128, 512], F32, stack=pp)]
                p_bcd = ctx.psum("p_bcd", [128, 48], F32, stack=pp)

                l4 = ctx.sem("l4"); xls = ctx.sem("xls")
                dmm = ctx.sem("dmm"); dte = ctx.sem("dte"); spv = ctx.sem("spv")
                bme = ctx.sem("bme"); bve = ctx.sem("bve")
                bmm = ctx.sem("bmm")

                @blk.gpsimd
                def _(g):
                    g.memset(dtacc[:], 0.0)
                    l4.inc(g.dma_start(
                        out=bcd_sb[:].rearrange("p (fb g) -> p fb g", g=48),
                        in_=bcdw.rearrange("(fb p) g -> p fb g", p=128)), 16)
                    l4.inc(g.dma_start(out=bcdb_bf[:], in_=bcdb), 16)
                    l4.inc(g.dma_start(out=dtb_bf[:], in_=dtb), 16)
                    for gh in range(2):
                        if gh >= 1:
                            g.wait_ge(dmm.sem, 32)
                        l4.inc(g.dma_start(
                            out=dtw_sb[:].rearrange("p (fb g) -> p fb g", g=1024),
                            in_=dtwT[:, gh * 1024:(gh + 1) * 1024]
                                .rearrange("(fb p) g -> p fb g", p=128)), 16)
                        for tc in range(NCH):
                            i = gh * 16 + tc
                            if i >= 2:
                                g.wait_ge(dmm.sem, 2 * (i - 1))
                            xls.inc(g.dma_start(
                                out=xc_sl[i % 2][:].rearrange("p (fb t) -> p fb t",
                                                              t=128),
                                in_=xcd.rearrange("(fb p) t -> p fb t", p=128)
                                    [:, :, tc * 128:(tc + 1) * 128]), 16)

                @blk.tensor
                def _(t):
                    ndt = 0
                    for gh in range(2):
                        t.wait_ge(l4.sem, 48 + 16 * (gh + 1))
                        for tc in range(NCH):
                            i = gh * 16 + tc
                            t.wait_ge(xls.sem, 16 * (i + 1))
                            for gq in range(2):
                                if ndt >= 2:
                                    t.wait_ge(dte.sem, ndt - 1)
                                pb = p_dt[ndt % 2]
                                for fb in range(16):
                                    t.matmul(pb[:],
                                             xc_sl[i % 2][:, fb * 128:(fb + 1) * 128],
                                             dtw_sb[:, fb * 1024 + gq * 512:
                                                    fb * 1024 + (gq + 1) * 512],
                                             start=(fb == 0), stop=False)
                                mmi = t.matmul(pb[:], ones1[:],
                                               dtb_bf[:, gh * 1024 + gq * 512:
                                                      gh * 1024 + (gq + 1) * 512],
                                               start=False, stop=True)
                                dmm.inc(mmi)
                                ndt += 1
                            if gh == 0:
                                if tc >= 1:
                                    t.wait_ge(bme.sem, tc)
                                for fb in range(16):
                                    t.matmul(p_bcd[:],
                                             xc_sl[i % 2][:, fb * 128:(fb + 1) * 128],
                                             bcd_sb[:, fb * 48:(fb + 1) * 48],
                                             start=(fb == 0), stop=False)
                                mmi = t.matmul(p_bcd[:], ones1[:], bcdb_bf[:],
                                               start=False, stop=True)
                                bmm.inc(mmi)

                @blk.scalar
                def _(s):
                    ndt = 0
                    for gh in range(2):
                        for tc in range(NCH):
                            for gq in range(2):
                                s.wait_ge(dmm.sem, ndt + 1)
                                if ndt >= 2:
                                    s.wait_ge(spv.sem, ndt - 1)
                                s.activation(raw_sb[ndt % 2][:], p_dt[ndt % 2][:],
                                             AF.Identity)
                                s.drain()
                                s.activation(sp_sb[ndt % 2][:], raw_sb[ndt % 2][:],
                                             AF.Abs)
                                s.drain()
                                s.activation(sp_sb[ndt % 2][:], sp_sb[ndt % 2][:],
                                             AF.Exp, scale=-1.0)
                                s.drain()
                                s.activation(sp_sb[ndt % 2][:], sp_sb[ndt % 2][:],
                                             AF.Ln, bias=1.0)
                                dte.inc(s.drain())
                                ndt += 1
                            if gh == 0:
                                s.wait_ge(bmm.sem, tc + 1)
                                if tc >= 1:
                                    s.wait_ge(bve.sem, tc)
                                s.activation(bc3[:], p_bcd[:], AF.Identity)
                                bme.inc(s.drain())

                @blk.vector
                def _(v):
                    ndt = 0
                    for gh in range(2):
                        for tc in range(NCH):
                            for gq in range(2):
                                v.wait_ge(dte.sem, ndt + 1)
                                v.tensor_scalar_max(rl_sb[:], raw_sb[ndt % 2][:], 0.0)
                                v.drain()
                                v.tensor_add(rl_sb[:], rl_sb[:], sp_sb[ndt % 2][:])
                                spv.inc(v.drain())
                                v.tensor_reduce(scr3[:, 0:1], rl_sb[:], axis=AX,
                                                op=ALU.add)
                                v.drain()
                                v.tensor_add(dtacc[:, tc:tc + 1],
                                             dtacc[:, tc:tc + 1], scr3[:, 0:1])
                                v.drain()
                                ndt += 1
                            if gh == 0:
                                v.wait_ge(bme.sem, tc + 1)
                                v.tensor_tensor(u_all[:, tc * 16:(tc + 1) * 16],
                                                bc3[:, 0:16], bc3[:, 32:48],
                                                op=ALU.mult)
                                v.tensor_copy(ct_all[:, tc * 16:(tc + 1) * 16],
                                              bc3[:, 16:32])
                                bve.inc(v.drain())
                    v.tensor_scalar_mul(ldv[:], dtacc[:], -1.0 / 2048.0)
                    v.drain()
                    v.tensor_scalar_max(ldv[:], ldv[:], -10.0)
                    v.drain()
                    v.tensor_scalar_mul(ldv[:], ldv[:], -1.0)
                    v.drain()
                    v.tensor_scalar_max(ldv[:], ldv[:], -10.0)
                    v.drain()
                    v.tensor_scalar_mul(ldv[:], ldv[:], -1.0)
                    ctx.sem("ldf").inc(v.drain())

                ctx.barrier()

                fo = ctx.sem("fo")

                @blk.sync
                def _(sy):
                    fo.inc(sy.dma_start(
                        out=x2o.rearrange("(c p) d -> p c d", p=128),
                        in_=x_tm[:, 0:NCH * D].rearrange("p (c d) -> p c d", d=D)), 16)
                    fo.inc(sy.dma_start(
                        out=uo.rearrange("(tc p) g -> p tc g", p=128),
                        in_=u_all[:].rearrange("p (tc g) -> p tc g", g=16)), 16)
                    fo.inc(sy.dma_start(
                        out=cto.rearrange("(tc p) g -> p tc g", p=128),
                        in_=ct_all[:].rearrange("p (tc g) -> p tc g", g=16)), 16)
                    fo.inc(sy.dma_start(out=ldo, in_=ldv[:]), 16)
                    sy.wait_ge(fo.sem, 64)

                ctx.barrier()

    except _Cut:
        ctx.barrier()
        xo2 = ctx.sem("xo2")

        @blk.sync
        def _(sy):
            xo2.inc(sy.dma_start(
                out=x2o.rearrange("(c p) d -> p c d", p=128),
                in_=x_tm[:, 0:NCH * D].rearrange("p (c d) -> p c d", d=D)), 16)
            sy.wait_ge(xo2.sem, 16)

    ctx.finish()
    return nc






def build_launch3(stop_after=None):
    nc = bass.Bass("TRN2", target_bir_lowering=False, detect_race_conditions=False)
    stack = contextlib.ExitStack()
    ctx = Ctx(nc, stack)
    blk = ctx.block

    def dram(n, s, d=F32, k="ExternalInput"):
        return nc.dram_tensor(n, s, d, kind=k).ap()

    x2i = dram("x2i", [TPC, D])
    ubi = dram("ubi", [T, 16])
    ctbi = dram("ctbi", [T, 16])
    ldbi = dram("ldbi", [128, 64])
    xgi = dram("xgi", [2048, TPC], BF16)
    outwT = dram("outwT", [2048, D])
    outb = dram("outb", [1, D])
    triL = dram("triL", [128, 128])
    triUs = dram("triUs", [128, 128])
    triM = dram("triM", [128, 128])
    ohm = dram("ohm", [128, 64])
    wq512 = dram("wq512", [D, 512])
    kslab = dram("kslab", [128, 8 * 64])
    deT = dram("deT", [D, 4096])
    ued = dram("ued", [4096, D])
    wgul = dram("wgul", [4, D, 2048])
    wdl = dram("wdl", [4096, D])
    bgl = dram("bgl", [1, 4096])
    bul = dram("bul", [1, 4096])
    bdl = dram("bdl", [1, D])

    stmp = dram("stmp", [1, 1024], F32, "Internal")
    Gd = dram("Gd", [TPC, 4096], BF16, "Internal")
    ATd = dram("ATd", [4096, TPC], BF16, "Internal")
    mTd = dram("mTd", [4096, TPC], BF16, "Internal")

    x3o = dram("x3o", [TPC, D], F32, "ExternalOutput")

    ld = ctx.sem("ld")
    ctx.setup_consts([1e-5, 1.0, -16384.0], ld)
    x_tm = ctx.sbuf("x_tm", [128, NCH * D], F32)
    ident = ctx.sbuf("ident", [128, 128], BF16)
    ones1 = ctx.sbuf("ones1", [1, 128], BF16)
    onesc = ctx.sbuf("onesc", [128, 1], BF16)
    st_all = ctx.sbuf("st_all", [128, NCH * 16], F32)
    ysown = ctx.sbuf("ysown", [128, 16], F32)

    def x_ch(c):
        return x_tm[:, c * D:(c + 1) * D]

    @blk.sync
    def _(sy):
        ld.inc(sy.dma_start(out=x_tm[:].rearrange("p (c d) -> p c d", d=D),
                            in_=x2i.rearrange("(c p) d -> p c d", p=128)), 16)

    ldc = ctx.sem("ldc")

    @blk.gpsimd
    def _(g):
        g.memset(ident[:], 0.0)
        g.affine_select(out=ident[:], in_=ident[:], compare_op=ALU.not_equal,
                        fill=1.0, base=0, pattern=[[-1, 128]], channel_multiplier=1)
        g.memset(ones1[:], 1.0)
        ldc.inc(g.memset(onesc[:], 1.0))

    for dec in (blk.vector, blk.scalar, blk.tensor):
        @dec
        def _(eng):
            eng.wait_ge(ld.sem, ld.n)
            eng.wait_ge(ldc.sem, ldc.n)

    try:
        with contextlib.ExitStack() as ph:
            ldb_sb = ctx.sbuf("ldb_sb", [128, 64], F32, stack=ph)
            tL_sb = ctx.sbuf("tL_sb", [128, 128], BF16, stack=ph)
            tU_sb = ctx.sbuf("tU_sb", [128, 128], BF16, stack=ph)
            tM_sb = ctx.sbuf("tM_sb", [128, 128], BF16, stack=ph)
            ohm_sb = ctx.sbuf("ohm_sb", [128, 64], F32, stack=ph)
            ub_bf = ctx.sbuf("ub_bf", [128, 64 * 16], BF16, stack=ph)
            ctb_bf = ctx.sbuf("ctb_bf", [128, 64 * 16], BF16, stack=ph)
            ldhi = ctx.sbuf("ldhi", [128, 64], BF16, stack=ph)
            ldlo = ctx.sbuf("ldlo", [128, 64], BF16, stack=ph)
            ldr = ctx.sbuf("ldr", [128, 64], F32, stack=ph)
            p_sb = ctx.sbuf("p_sb", [128, 64], F32, stack=ph)
            suf_sb = ctx.sbuf("suf_sb", [128, 64], F32, stack=ph)
            eb_bf = ctx.sbuf("eb_bf", [128, 64], BF16, stack=ph)
            p16 = ctx.sbuf("p16", [128, 64], F32, stack=ph)
            s1a = ctx.sbuf("s1a", [128, 64], BF16, stack=ph)
            s2a = ctx.sbuf("s2a", [128, 64], BF16, stack=ph)
            s3a = ctx.sbuf("s3a", [128, 64], BF16, stack=ph)
            rsc = ctx.sbuf("rsc", [128, 64], F32, stack=ph)
            d_row = ctx.sbuf("d_row", [1, 64], F32, stack=ph)
            w_row = ctx.sbuf("w_row", [1, 1024], F32, stack=ph)
            s_row = ctx.sbuf("s_row", [1, 1024], F32, stack=ph)
            smatT = ctx.sbuf("smatT", [16, 64], BF16, stack=ph)
            ys_full = ctx.sbuf("ys_full", [128, 64], F32, stack=ph)
            eap = ctx.sbuf("eap", [128, 64], F32, stack=ph)

            l3 = ctx.sem("l3")

            @blk.gpsimd
            def _(g):
                l3.inc(g.dma_start(out=ldb_sb[:], in_=ldbi), 16)
                l3.inc(g.dma_start(out=tL_sb[:], in_=triL), 16)
                l3.inc(g.dma_start(out=tU_sb[:], in_=triUs), 16)
                l3.inc(g.dma_start(out=tM_sb[:], in_=triM), 16)
                l3.inc(g.dma_start(out=ohm_sb[:], in_=ohm), 16)
                l3.inc(g.dma_start(out=ub_bf[:].rearrange("p (r gg) -> p r gg", gg=16),
                                   in_=ubi.rearrange("(r p) gg -> p r gg", p=128)), 16)
                l3.inc(g.dma_start(out=ctb_bf[:].rearrange("p (r gg) -> p r gg", gg=16),
                                   in_=ctbi.rearrange("(r p) gg -> p r gg", p=128)), 16)

            spl = ctx.sem("spl"); spv = ctx.sem("spv")

            @blk.scalar
            def _(s):
                s.wait_ge(l3.sem, l3.n)
                s.activation(ldhi[:], ldb_sb[:], AF.Identity)
                spl.inc(s.drain())
                s.wait_ge(spv.sem, 1)
                s.activation(ldlo[:], ldr[:], AF.Identity)
                spl.inc(s.drain())

            @blk.vector
            def _(v):
                v.wait_ge(spl.sem, 1)
                v.tensor_tensor(ldr[:], ldb_sb[:], ldhi[:], op=ALU.subtract)
                spv.inc(v.drain())

            pfm = ctx.sem("pfm"); pfe = ctx.sem("pfe")

            with contextlib.ExitStack() as pp:
                p_pf = ctx.psum("p_pf", [128, 64], F32, stack=pp)
                p_sf = ctx.psum("p_sf", [128, 64], F32, stack=pp)
                p_tt = ctx.psum("p_tt", [1, 64], F32, stack=pp)

                @blk.tensor
                def _(t):
                    t.wait_ge(spl.sem, 2)
                    t.matmul(p_pf[:], tL_sb[:], ldhi[:], start=True, stop=False)
                    pfm.inc(t.matmul(p_pf[:], tL_sb[:], ldlo[:], start=False,
                                     stop=True))
                    t.matmul(p_sf[:], tU_sb[:], ldhi[:], start=True, stop=False)
                    pfm.inc(t.matmul(p_sf[:], tU_sb[:], ldlo[:], start=False,
                                     stop=True))
                    t.matmul(p_tt[:], onesc[:], ldhi[:], start=True, stop=False)
                    pfm.inc(t.matmul(p_tt[:], onesc[:], ldlo[:], start=False,
                                     stop=True))

                @blk.scalar
                def _(s):
                    s.wait_ge(pfm.sem, 1)
                    s.activation(p_sb[:], p_pf[:], AF.Identity)
                    s.drain()
                    s.wait_ge(pfm.sem, 2)
                    s.activation(suf_sb[:], p_sf[:], AF.Identity)
                    s.drain()
                    s.activation(eb_bf[:], suf_sb[:], AF.Exp)
                    s.drain()
                    s.wait_ge(pfm.sem, 3)
                    s.activation(d_row[:], p_tt[:], AF.Exp)
                    s.drain()
                    s.activation(eap[:], p_sb[:], AF.Exp)
                    s.drain()
                    s.activation(p16[:], p_sb[:], AF.Identity, bias=-16384.0)
                    s.drain()
                    s.activation(s1a[:], p16[:], AF.Identity)
                    pfe.inc(s.drain())

                sv2 = ctx.sem("sv2"); sv3 = ctx.sem("sv3")

                @blk.vector
                def _(v):
                    v.wait_ge(pfe.sem, 1)
                    v.tensor_tensor(rsc[:], p16[:], s1a[:], op=ALU.subtract)
                    sv2.inc(v.drain())
                    v.wait_ge(sv3.sem, 1)
                    v.tensor_tensor(rsc[:], rsc[:], s2a[:], op=ALU.subtract)
                    sv2.inc(v.drain())

                @blk.scalar
                def _(s):
                    s.wait_ge(sv2.sem, 1)
                    s.activation(s2a[:], rsc[:], AF.Identity)
                    sv3.inc(s.drain())
                    s.wait_ge(sv2.sem, 2)
                    s.activation(s3a[:], rsc[:], AF.Identity)
                    sv3.inc(s.drain())

                wsm = ctx.sem("wsm"); wse = ctx.sem("wse")
                p_w = [ctx.psum("p_w0", [1, 512], F32, stack=pp),
                       ctx.psum("p_w1", [1, 512], F32, stack=pp)]

                @blk.tensor
                def _(t):
                    t.wait_ge(sv3.sem, 2)
                    for r in range(64):
                        mmi = t.matmul(p_w[r // 32][0:1, (r % 32) * 16:
                                                    (r % 32) * 16 + 16],
                                       eb_bf[:, r:r + 1],
                                       ub_bf[:, r * 16:(r + 1) * 16],
                                       start=True, stop=True)
                        if r == 31 or r == 63:
                            wsm.inc(mmi)

                @blk.scalar
                def _(s):
                    s.wait_ge(wsm.sem, 1)
                    s.activation(w_row[:, 0:512], p_w[0][:], AF.Identity)
                    s.drain()
                    s.wait_ge(wsm.sem, 2)
                    s.activation(w_row[:, 512:1024], p_w[1][:], AF.Identity)
                    wse.inc(s.drain())

                rcv = ctx.sem("rcv")

                @blk.vector
                def _(v):
                    v.wait_ge(wse.sem, 1)
                    v.tensor_copy(s_row[:, 0:16], w_row[:, 0:16])
                    v.drain()
                    for r in range(1, 64):
                        v.tensor_tensor(s_row[:, r * 16:(r + 1) * 16],
                                        s_row[:, (r - 1) * 16:r * 16],
                                        d_row[:, r:r + 1].broadcast_to([1, 16]),
                                        op=ALU.mult)
                        v.drain()
                        v.tensor_add(s_row[:, r * 16:(r + 1) * 16],
                                     s_row[:, r * 16:(r + 1) * 16],
                                     w_row[:, r * 16:(r + 1) * 16])
                        v.drain()
                    rcv.inc(v.drain())

                smd = ctx.sem("smd")

                @blk.gpsimd
                def _(g):
                    g.wait_ge(rcv.sem, 1)
                    smd.inc(g.dma_start(out=stmp, in_=s_row[:]), 16)
                    g.wait_ge(smd.sem, 16)
                    smd.inc(g.dma_start(
                        out=smatT[:],
                        in_=stmp.rearrange("o (r gg) -> gg (o r)", gg=16),
                        allow_slow_non_contiguous=True), 16)

                ctx.barrier()

            if stop_after == "ssmpre":
                raise _Cut()

            with contextlib.ExitStack() as pp:
                p_tp = ctx.psum("p_tp", [16, 256], BF16, stack=pp)
                p_row = ctx.psum("p_row", [1, 3 * 128], BF16, stack=pp)
                p_m = ctx.psum("p_m", [128, 128], F32, stack=pp)
                p_cu = ctx.psum("p_cu", [128, 128], F32, stack=pp)
                p_t1 = ctx.psum("p_t1", [128, 1], F32, stack=pp)
                ctT_sb = ctx.sbuf("ctT_sb", [16, 128], BF16, stack=pp)
                uT_sb = ctx.sbuf("uT_sb", [16, 128], BF16, stack=pp)
                rowp = ctx.sbuf("rowp", [1, 3 * 128], BF16, stack=pp)
                rown = ctx.sbuf("rown", [1, 3 * 128], BF16, stack=pp)
                em_sb = ctx.sbuf("em_sb", [128, 128], BF16, stack=pp)
                m_sb = ctx.sbuf("m_sb", [128, 128], F32, stack=pp)
                cu_sb = ctx.sbuf("cu_sb", [128, 128], BF16, stack=pp)
                pr_sb = ctx.sbuf("pr_sb", [128, 128], F32, stack=pp)
                t1_sb = ctx.sbuf("t1_sb", [128, 1], F32, stack=pp)
                ysd_sb = ctx.sbuf("ysd_sb", [128, 2], F32, stack=pp)

                ytm = ctx.sem("ytm"); yte = ctx.sem("yte"); ymm = ctx.sem("ymm")
                yme = ctx.sem("yme"); yv = ctx.sem("yv"); ysr = ctx.sem("ysr")
                yc1 = ctx.sem("yc1"); yc2 = ctx.sem("yc2")

                def MM(r):
                    return 0 if r == 0 else 3 * r - 1

                @blk.tensor
                def _(t):
                    for r in range(64):
                        if r >= 1:
                            t.wait_ge(yte.sem, 2 * r)
                        t.transpose(p_tp[:, 0:128], ctb_bf[:, r * 16:(r + 1) * 16],
                                    ident[:])
                        ytm.inc(t.transpose(p_tp[:, 128:256],
                                            ub_bf[:, r * 16:(r + 1) * 16],
                                            ident[:]))
                        t.transpose(p_row[:, 0:128], s1a[:, r:r + 1], ident[:])
                        t.transpose(p_row[:, 128:256], s2a[:, r:r + 1], ident[:])
                        ytm.inc(t.transpose(p_row[:, 256:384], s3a[:, r:r + 1],
                                            ident[:]))
                        t.wait_ge(yte.sem, 2 * r + 2)
                        if r >= 1:
                            t.wait_ge(yme.sem, r)
                        for j3 in range(3):
                            t.matmul(p_m[:, 0:128],
                                     rowp[:, j3 * 128:(j3 + 1) * 128],
                                     ones1[:], start=(j3 == 0), stop=False)
                            mmi = t.matmul(p_m[:, 0:128], ones1[:],
                                           rown[:, j3 * 128:(j3 + 1) * 128],
                                           start=False, stop=(j3 == 2))
                        ymm.inc(mmi)
                        ymm.inc(t.matmul(p_cu[:, 0:128], ctT_sb[:], uT_sb[:],
                                         start=True, stop=True))
                        if r >= 1:
                            ymm.inc(t.matmul(p_t1[:], ctT_sb[:],
                                             smatT[:, r - 1:r], start=True,
                                             stop=True))

                @blk.scalar
                def _(s):
                    for r in range(64):
                        s.wait_ge(ytm.sem, 2 * r + 1)
                        s.activation(ctT_sb[:], p_tp[:, 0:128], AF.Identity)
                        s.activation(uT_sb[:], p_tp[:, 128:256], AF.Identity)
                        yte.inc(s.drain())
                        s.wait_ge(ytm.sem, 2 * r + 2)
                        s.activation(rowp[:], p_row[:], AF.Identity)
                        s.drain()
                        s.activation(rown[:], p_row[:], AF.Identity, scale=-1.0)
                        yte.inc(s.drain())
                        if r >= 1:
                            s.wait_ge(yv.sem, r)
                        s.wait_ge(ymm.sem, MM(r) + 1)
                        s.activation(m_sb[:], p_m[:, 0:128], AF.Identity)
                        yc1.inc(s.drain())
                        s.wait_ge(yc2.sem, r + 1)
                        s.activation(em_sb[:], m_sb[:], AF.Exp, scale=-1.0)
                        s.wait_ge(ymm.sem, MM(r) + 2)
                        s.activation(cu_sb[:], p_cu[:, 0:128], AF.Identity)
                        if r >= 1:
                            s.wait_ge(ymm.sem, MM(r) + 3)
                            s.activation(t1_sb[:], p_t1[:], AF.Identity)
                        yme.inc(s.drain())

                @blk.vector
                def _(v):
                    for r in range(64):
                        v.wait_ge(yc1.sem, r + 1)
                        v.tensor_scalar_mul(m_sb[:], m_sb[:], -1.0)
                        v.drain()
                        v.tensor_scalar_max(m_sb[:], m_sb[:], 0.0)
                        yc2.inc(v.drain())
                        v.wait_ge(yme.sem, r + 1)
                        v.tensor_tensor(pr_sb[:], em_sb[:], cu_sb[:], op=ALU.mult)
                        v.drain()
                        v.tensor_tensor(pr_sb[:], pr_sb[:], tM_sb[:], op=ALU.mult)
                        v.drain()
                        v.tensor_reduce(ysd_sb[:, 0:1], pr_sb[:], axis=AX,
                                        op=ALU.add)
                        v.drain()
                        if r >= 1:
                            v.tensor_tensor(ysd_sb[:, 1:2], eap[:, r:r + 1],
                                            t1_sb[:], op=ALU.mult)
                            v.drain()
                            v.tensor_tensor(ys_full[:, r:r + 1], ysd_sb[:, 0:1],
                                            ysd_sb[:, 1:2], op=ALU.add)
                        else:
                            v.tensor_copy(ys_full[:, 0:1], ysd_sb[:, 0:1])
                        yv.inc(v.drain())
                    v.tensor_tensor(eap[:], ys_full[:], ohm_sb[:], op=ALU.mult)
                    v.drain()
                    v.tensor_reduce(ysown[:],
                                    eap[:].rearrange("p (a b) -> p b a", b=16),
                                    axis=AX, op=ALU.add)
                    ysr.inc(v.drain())

                ctx.barrier()

        if stop_after == "scan":
            raise _Cut()

        with contextlib.ExitStack() as ph:
            outw_sb = ctx.sbuf("outw_sb", [128, 16 * 1024], BF16, stack=ph)
            outb_bc = ctx.sbuf("outb_bc", [128, 1024], F32, stack=ph)
            outb_bf = ctx.sbuf("outb_bf", [1, 1024], BF16, stack=ph)
            xg_sl = [ctx.sbuf("xg_sl0", [128, 16 * 128], BF16, stack=ph),
                     ctx.sbuf("xg_sl1", [128, 16 * 128], BF16, stack=ph)]
            xn2_tm = ctx.sbuf("xn2b_tm", [128, NCH * D], BF16, stack=ph)
            scr4 = ctx.sbuf("scr4", [128, 1024], F32, stack=ph)
            oo_sb = [ctx.sbuf("oo0", [128, 1024], F32, stack=ph),
                     ctx.sbuf("oo1", [128, 1024], F32, stack=ph)]

            def xn2_ch(c):
                return xn2_tm[:, c * D:(c + 1) * D]

            lo3 = ctx.sem("lo3")

            @blk.gpsimd
            def _(g):
                lo3.inc(g.dma_start(
                    out=outw_sb[:].rearrange("p (fb d) -> p fb d", d=1024),
                    in_=outwT.rearrange("(fb p) d -> p fb d", p=128)), 16)
                lo3.inc(g.dma_start(out=outb_bf[:], in_=outb), 16)
                g.wait_ge(lo3.sem, 32)
                xgl = ctx.sem("xgl")
                for c in range(NCH):
                    if c >= 2:
                        g.wait_ge(ctx.sem("oom").sem, 2 * (c - 1))
                    xgl.inc(g.dma_start(
                        out=xg_sl[c % 2][:].rearrange("p (fb tt) -> p fb tt",
                                                      tt=128),
                        in_=xgi.rearrange("(fb p) tt -> p fb tt", p=128)
                            [:, :, c * 128:(c + 1) * 128]), 16)

            emit_ln(ctx, "n2b", x_ch, xn2_ch, NCH, st_all, scr4)

            with contextlib.ExitStack() as pp:
                p_ob = ctx.psum("p_ob", [128, 512], F32, stack=pp)
                p_oo = [ctx.psum("p_oo0", [128, 512], F32, stack=pp),
                        ctx.psum("p_oo1", [128, 512], F32, stack=pp)]
                obm = ctx.sem("obm"); obe = ctx.sem("obe")
                oom = ctx.sem("oom"); ooe = ctx.sem("ooe"); oox = ctx.sem("oox")

                @blk.tensor
                def _(t):
                    t.wait_ge(lo3.sem, 32)
                    for half in range(2):
                        obm.inc(t.matmul(p_ob[:], ones1[:],
                                         outb_bf[:, half * 512:(half + 1) * 512],
                                         start=True, stop=True))
                        t.wait_ge(obe.sem, half + 1)
                    t.wait_ge(ysr.sem, 1)
                    ni = 0
                    for c in range(NCH):
                        t.wait_ge(ctx.sem("xgl").sem, 16 * (c + 1))
                        for half in range(2):
                            if ni >= 2:
                                t.wait_ge(ooe.sem, ni - 1)
                            pb = p_oo[ni % 2]
                            for fb in range(16):
                                mmi = t.matmul(
                                    pb[:],
                                    xg_sl[c % 2][:, fb * 128:(fb + 1) * 128],
                                    outw_sb[:, fb * 1024 + half * 512:
                                            fb * 1024 + (half + 1) * 512],
                                    start=(fb == 0), stop=(fb == 15))
                            oom.inc(mmi)
                            ni += 1

                @blk.scalar
                def _(s):
                    for half in range(2):
                        s.wait_ge(obm.sem, half + 1)
                        s.activation(outb_bc[:, half * 512:(half + 1) * 512],
                                     p_ob[:], AF.Identity)
                        obe.inc(s.drain())
                    ni = 0
                    for c in range(NCH):
                        if c >= 2:
                            s.wait_ge(oox.sem, c - 1)
                        for half in range(2):
                            s.wait_ge(oom.sem, ni + 1)
                            s.activation(oo_sb[c % 2][:, half * 512:(half + 1) * 512],
                                         p_oo[ni % 2][:], AF.Identity,
                                         scale=ysown[:, c:c + 1])
                            ooe.inc(s.drain())
                            ni += 1

                @blk.vector
                def _(v):
                    sn2 = ctx.sem("n2bn")
                    for c in range(NCH):
                        v.wait_ge(ooe.sem, 2 * c + 2)
                        v.wait_ge(sn2.sem, c + 1)
                        v.tensor_add(x_ch(c)[:], x_ch(c)[:], oo_sb[c % 2][:])
                        v.drain()
                        v.tensor_add(x_ch(c)[:], x_ch(c)[:], xn2_ch(c)[:])
                        v.drain()
                        v.tensor_add(x_ch(c)[:], x_ch(c)[:], outb_bc[:])
                        oox.inc(v.drain())

                ctx.barrier()

        if stop_after == "x2b":
            raise _Cut()

        with contextlib.ExitStack() as ph:
            hs_fm = ctx.sbuf("hs_fm", [128, 8 * TPC], BF16, stack=ph)

            def hsf(dc):
                return hs_fm[:, dc * TPC:(dc + 1) * TPC]

            with contextlib.ExitStack() as pp:
                hs_tm = ctx.sbuf("hs_tm", [128, NCH * D], BF16, stack=pp)
                scr5 = ctx.sbuf("scr5", [128, 1024], F32, stack=pp)

                def hs_ch(c):
                    return hs_tm[:, c * D:(c + 1) * D]

                emit_ln(ctx, "h3", x_ch, hs_ch, NCH, st_all, scr5)
                emit_transpose(ctx, "h3", hs_ch,
                               lambda dc, c: hsf(dc)[:, c * 128:(c + 1) * 128],
                               NCH, ident, pp)
                ctx.barrier()

            if stop_after == "lnh":
                raise _Cut()

            NCHA = NCH
            DOGATE = True
            DOQT = True
            DOS = True
            if stop_after and stop_after.startswith("gsel") and len(stop_after) > 4:
                suf = stop_after[4:]
                if suf == "q":
                    DOGATE = False
                elif suf == "s":
                    DOGATE = False
                    DOS = 2
                elif suf == "m":
                    DOGATE = DOQT = DOS = False
                elif suf == "t":
                    DOGATE = DOS = False
                else:
                    NCHA = int(suf)
            with contextlib.ExitStack() as pp:
                wq_sb = ctx.sbuf("wq_sb", [128, 8 * 512], BF16, stack=pp)
                ks_sb = ctx.sbuf("ks_sb", [128, 8 * 64], BF16, stack=pp)
                ksf = ctx.sbuf("ksf", [128, 8 * 64], F32, stack=pp)
                kslo = ctx.sbuf("kslo", [128, 8 * 64], BF16, stack=pp)
                q_sb = ctx.sbuf("q_sb", [128, 512], BF16, stack=pp)
                qf_scr = ctx.sbuf("qf_scr", [128, 512], F32, stack=pp)
                qlo_tm = ctx.sbuf("qlo_tm", [128, 512], BF16, stack=pp)
                qT_sb = ctx.sbuf("qT_sb", [64, 1024], BF16, stack=pp)
                qT_lo = ctx.sbuf("qT_lo", [64, 1024], BF16, stack=pp)
                sc_sb = ctx.sbuf("sc_sb", [128, 512], F32, stack=pp)
                C_sb = ctx.sbuf("C_sb", [128, 4096], F32, stack=pp)
                C1_sb = ctx.sbuf("C1_sb", [128, 4096], F32, stack=pp)
                eqm = ctx.sbuf("eqm", [128, 4096], F32, stack=pp)
                G_sb = ctx.sbuf("G_sb", [128, 4096], F32, stack=pp)
                gbf = [ctx.sbuf("gbf0", [128, 4096], BF16, stack=pp),
                       ctx.sbuf("gbf1", [128, 4096], BF16, stack=pp)]
                mx_sb = ctx.sbuf("mx_sb", [128, 12], F32, stack=pp)
                ez_sb = ctx.sbuf("ez_sb", [128, 8], F32, stack=pp)
                p_q = ctx.psum("p_q3", [128, 512], F32, stack=pp)
                p_qt = ctx.psum("p_qt3", [64, 1024], BF16, stack=pp)
                p_sc = ctx.psum("p_sc3", [128, 1024], F32, stack=pp)

                lq = ctx.sem("lq")

                @blk.gpsimd
                def _(g):
                    lq.inc(g.dma_start(
                        out=wq_sb[:].rearrange("p (dc o) -> p dc o", o=512),
                        in_=wq512.rearrange("(dc p) o -> p dc o", p=128)), 16)
                    lq.inc(g.dma_start(out=ks_sb[:], in_=kslab), 16)
                    lq.inc(g.dma_start(out=ksf[:], in_=kslab), 16)

                klv = ctx.sem("klv")

                @blk.vector
                def _(v):
                    v.wait_ge(lq.sem, 48)
                    v.tensor_tensor(kslo[:], ksf[:], ks_sb[:], op=ALU.subtract)
                    klv.inc(v.drain())

                qm3 = ctx.sem("qm3"); qe3 = ctx.sem("qe3"); qt3 = ctx.sem("qt3")
                lsv3 = ctx.sem("lsv3")
                qte3 = ctx.sem("qte3"); scm = ctx.sem("scm"); sce = ctx.sem("sce")
                gv3 = ctx.sem("gv3"); gs3 = ctx.sem("gs3"); gw3 = ctx.sem("gw3")
                gdo = ctx.sem("gdo")

                @blk.tensor
                def _(t):
                    t.wait_ge(lq.sem, 48)
                    t.wait_ge(klv.sem, 1)
                    for c in range(NCHA):
                        if c >= 1:
                            t.wait_ge(qe3.sem, c)
                        for dc in range(8):
                            mmi = t.matmul(p_q[:], hsf(dc)[:, c * 128:(c + 1) * 128],
                                           wq_sb[:, dc * 512:(dc + 1) * 512],
                                           start=(dc == 0), stop=(dc == 7))
                        qm3.inc(mmi)
                        if not DOQT:
                            continue
                        t.wait_ge(qe3.sem, c + 1)
                        if c >= 1:
                            t.wait_ge(qte3.sem, 2 * c)
                        for gg in range(8):
                            mmi = t.transpose(p_qt[0:64, gg * 128:(gg + 1) * 128],
                                              q_sb[:, gg * 64:(gg + 1) * 64],
                                              ident[:])
                        qt3.inc(mmi)
                        t.wait_ge(lsv3.sem, c + 1)
                        t.wait_ge(qte3.sem, 2 * c + 1)
                        for gg in range(8):
                            mmi = t.transpose(p_qt[0:64, gg * 128:(gg + 1) * 128],
                                              qlo_tm[:, gg * 64:(gg + 1) * 64],
                                              ident[:])
                        qt3.inc(mmi)
                        if not DOS:
                            continue
                        t.wait_ge(qte3.sem, 2 * c + 2)
                        if c >= 1 and DOS != 2:
                            t.wait_ge(sce.sem, c)
                        for gg in range(8):
                            t.matmul(p_sc[:, gg * 128:gg * 128 + 64],
                                     qT_sb[0:64, gg * 128:(gg + 1) * 128],
                                     ks_sb[0:64, gg * 64:(gg + 1) * 64],
                                     start=True, stop=False)
                            t.matmul(p_sc[:, gg * 128:gg * 128 + 64],
                                     qT_lo[0:64, gg * 128:(gg + 1) * 128],
                                     ks_sb[0:64, gg * 64:(gg + 1) * 64],
                                     start=False, stop=False)
                            mmi = t.matmul(p_sc[:, gg * 128:gg * 128 + 64],
                                           qT_sb[0:64, gg * 128:(gg + 1) * 128],
                                           kslo[0:64, gg * 64:(gg + 1) * 64],
                                           start=False, stop=True)
                        scm.inc(mmi)

                @blk.scalar
                def _(s):
                    for c in range(NCHA):
                        s.wait_ge(qm3.sem, c + 1)
                        if c >= 1:
                            s.wait_ge(lsv3.sem, c)
                        s.activation(q_sb[:], p_q[:], AF.Identity)
                        s.activation(qf_scr[:], p_q[:], AF.Identity)
                        qe3.inc(s.drain())
                        if not DOQT:
                            continue
                        s.wait_ge(qt3.sem, 2 * c + 1)
                        s.activation(qT_sb[0:64, :], p_qt[0:64, :], AF.Identity)
                        qte3.inc(s.drain())
                        s.wait_ge(qt3.sem, 2 * c + 2)
                        s.activation(qT_lo[0:64, :], p_qt[0:64, :], AF.Identity)
                        qte3.inc(s.drain())
                        if not DOS or DOS == 2:
                            continue
                        s.wait_ge(scm.sem, c + 1)
                        for gg in range(8):
                            s.activation(sc_sb[:, gg * 64:(gg + 1) * 64],
                                         p_sc[:, gg * 128:gg * 128 + 64],
                                         AF.Identity)
                        sce.inc(s.drain())
                        for hh in range(4 if DOGATE else 0):
                            i = c * 4 + hh
                            s.wait_ge(gv3.sem, 2 * i + 1)
                            s.activation(ez_sb[:], mx_sb[:, 0:8], AF.Exp,
                                         bias=mx_sb[:, 8:9])
                            s.drain()
                            s.activation(eqm[:], C_sb[:], AF.Exp,
                                         bias=mx_sb[:, 8:9])
                            gs3.inc(s.drain())
                        if DOGATE:
                            s.wait_ge(gw3.sem, c + 1)
                            if c >= 2:
                                s.wait_ge(gdo.sem, 16 * (c - 1))
                            s.activation(gbf[c % 2][:], G_sb[:], AF.Identity)
                            gs3.inc(s.drain())

                @blk.vector
                def _(v):
                    for c in range(NCHA):
                        v.wait_ge(qe3.sem, c + 1)
                        v.tensor_tensor(qlo_tm[:], qf_scr[:], q_sb[:],
                                        op=ALU.subtract)
                        lsv3.inc(v.drain())
                        if not DOS or DOS == 2:
                            continue
                        v.wait_ge(sce.sem, c + 1)
                        for hh in range(4 if DOGATE else 0):
                            i = c * 4 + hh
                            v.tensor_copy(
                                C_sb[:].rearrange("p (i j) -> p i j", j=64),
                                sc_sb[:, hh * 64:(hh + 1) * 64]
                                .rearrange("p (i o) -> p i o", o=1)
                                .broadcast_to([128, 64, 64]))
                            v.drain()
                            v.tensor_tensor(
                                C_sb[:].rearrange("p (i j) -> p j i", j=64),
                                C_sb[:].rearrange("p (i j) -> p j i", j=64),
                                sc_sb[:, (4 + hh) * 64:(5 + hh) * 64]
                                .rearrange("p (j o) -> p j o", o=1)
                                .broadcast_to([128, 64, 64]),
                                op=ALU.add)
                            v.drain()
                            v.tensor_copy(C1_sb[:], C_sb[:])
                            v.drain()
                            for k8 in range(8):
                                v.tensor_reduce(mx_sb[:, k8:k8 + 1], C1_sb[:],
                                                axis=AX, op=ALU.max)
                                v.drain()
                                if k8 < 7:
                                    v.tensor_tensor(eqm[:], C1_sb[:],
                                                    mx_sb[:, k8:k8 + 1]
                                                    .broadcast_to([128, 4096]),
                                                    op=ALU.is_equal)
                                    v.drain()
                                    v.tensor_scalar_mul(eqm[:], eqm[:], -1e30)
                                    v.drain()
                                    v.tensor_add(C1_sb[:], C1_sb[:], eqm[:])
                                    v.drain()
                            v.tensor_scalar_mul(mx_sb[:, 8:9], mx_sb[:, 0:1], -1.0)
                            gv3.inc(v.drain())
                            v.wait_ge(gs3.sem, 5 * c + hh + 1)
                            v.tensor_reduce(mx_sb[:, 9:10], ez_sb[:], axis=AX,
                                            op=ALU.add)
                            v.drain()
                            v.reciprocal(mx_sb[:, 10:11], mx_sb[:, 9:10])
                            v.drain()
                            v.tensor_tensor(C1_sb[:], C_sb[:],
                                            mx_sb[:, 7:8].broadcast_to([128, 4096]),
                                            op=ALU.is_ge)
                            v.drain()
                            v.tensor_tensor(eqm[:], eqm[:], C1_sb[:], op=ALU.mult)
                            v.drain()
                            v.tensor_tensor(eqm[:], eqm[:],
                                            mx_sb[:, 10:11]
                                            .broadcast_to([128, 4096]),
                                            op=ALU.mult)
                            v.drain()
                            if hh == 0:
                                v.tensor_copy(G_sb[:], eqm[:])
                            else:
                                v.tensor_add(G_sb[:], G_sb[:], eqm[:])
                            gv3.inc(v.drain())
                        gw3.inc(v.drain())

                @blk.sync
                def _(sy):
                    for c in range(NCHA if DOGATE else 0):
                        sy.wait_ge(gs3.sem, 5 * c + 5)
                        gdo.inc(sy.dma_start(
                            out=Gd[c * 128:(c + 1) * 128, :],
                            in_=gbf[c % 2][:]), 16)
                    if DOGATE:
                        sy.wait_ge(gdo.sem, 16 * NCHA)

                ctx.barrier()

            if stop_after and stop_after.startswith("gsel"):
                raise _Cut()

            with contextlib.ExitStack() as pp:
                de_sb = ctx.sbuf("de_sb", [128, 8 * 2048], BF16, stack=pp)
                a_sb = ctx.sbuf("a_sb", [128, 2048], BF16, stack=pp)
                gh_sb = [ctx.sbuf("gh0", [128, 2048], BF16, stack=pp),
                         ctx.sbuf("gh1", [128, 2048], BF16, stack=pp)]
                at_sb = [ctx.sbuf("at0", [128, 16 * 128], BF16, stack=pp),
                         ctx.sbuf("at1", [128, 16 * 128], BF16, stack=pp)]
                p_w3 = [ctx.psum("p_w30", [128, 512], F32, stack=pp),
                        ctx.psum("p_w31", [128, 512], F32, stack=pp)]
                p_at3 = ctx.psum("p_at3", [128, 2048], BF16, stack=pp)

                dl = ctx.sem("dl"); gl3 = ctx.sem("gl3")
                wm3 = ctx.sem("wm3"); we3 = ctx.sem("we3"); av3 = ctx.sem("av3")
                atm = ctx.sem("atm"); ate = ctx.sem("ate"); ato = ctx.sem("ato")

                @blk.gpsimd
                def _(g):
                    for eh in range(2):
                        if eh >= 1:
                            g.wait_ge(wm3.sem, 64)
                        dl.inc(g.dma_start(
                            out=de_sb[:].rearrange("p (dc o) -> p dc o", o=2048),
                            in_=deT[:, eh * 2048:(eh + 1) * 2048]
                                .rearrange("(dc p) o -> p dc o", p=128)), 16)
                        for c in range(NCH):
                            i = eh * 16 + c
                            if i >= 2:
                                g.wait_ge(av3.sem, i - 1)
                            gl3.inc(g.dma_start(
                                out=gh_sb[i % 2][:],
                                in_=Gd[c * 128:(c + 1) * 128,
                                       eh * 2048:(eh + 1) * 2048]), 16)

                @blk.tensor
                def _(t):
                    for eh in range(2):
                        t.wait_ge(dl.sem, 16 * (eh + 1))
                        for c in range(NCH):
                            i = eh * 16 + c
                            for eb4 in range(4):
                                ii = i * 4 + eb4
                                if ii >= 2:
                                    t.wait_ge(we3.sem, ii - 1)
                                pb = p_w3[ii % 2]
                                for dc in range(8):
                                    mmi = t.matmul(
                                        pb[:], hsf(dc)[:, c * 128:(c + 1) * 128],
                                        de_sb[:, dc * 2048 + eb4 * 512:
                                              dc * 2048 + (eb4 + 1) * 512],
                                        start=(dc == 0), stop=(dc == 7))
                                wm3.inc(mmi)
                            t.wait_ge(av3.sem, i + 1)
                            if i >= 1:
                                t.wait_ge(ate.sem, i)
                            for tpi in range(16):
                                mmi = t.transpose(
                                    p_at3[:, tpi * 128:(tpi + 1) * 128],
                                    a_sb[:, tpi * 128:(tpi + 1) * 128], ident[:])
                            atm.inc(mmi)

                @blk.scalar
                def _(s):
                    for eh in range(2):
                        for c in range(NCH):
                            i = eh * 16 + c
                            for eb4 in range(4):
                                ii = i * 4 + eb4
                                s.wait_ge(wm3.sem, ii + 1)
                                s.activation(a_sb[:, eb4 * 512:(eb4 + 1) * 512],
                                             p_w3[ii % 2][:], AF.Silu)
                                we3.inc(s.drain())
                            s.wait_ge(atm.sem, i + 1)
                            if i >= 2:
                                s.wait_ge(ato.sem, 16 * (i - 1))
                            s.activation(at_sb[i % 2][:], p_at3[:], AF.Identity)
                            ate.inc(s.drain())

                @blk.vector
                def _(v):
                    for eh in range(2):
                        for c in range(NCH):
                            i = eh * 16 + c
                            v.wait_ge(we3.sem, 4 * i + 4)
                            v.wait_ge(gl3.sem, 16 * (i + 1))
                            v.tensor_tensor(a_sb[:], a_sb[:], gh_sb[i % 2][:],
                                            op=ALU.mult)
                            av3.inc(v.drain())

                @blk.sync
                def _(sy):
                    for eh in range(2):
                        for c in range(NCH):
                            i = eh * 16 + c
                            sy.wait_ge(ate.sem, i + 1)
                            ato.inc(sy.dma_start(
                                out=ATd.rearrange("(ec p) tt -> p ec tt", p=128)
                                    [:, eh * 16:(eh + 1) * 16,
                                     c * 128:(c + 1) * 128],
                                in_=at_sb[i % 2][:].rearrange(
                                    "p (ec tt) -> p ec tt", tt=128)), 16)
                    sy.wait_ge(ato.sem, 16 * 32)

                ctx.barrier()

            if stop_after == "l2fb":
                raise _Cut()

            with contextlib.ExitStack() as pp:
                ue_sb = ctx.sbuf("ue_sb", [128, 32 * 512], BF16, stack=pp)
                atl = [ctx.sbuf("atl0", [128, 32 * 128], BF16, stack=pp),
                       ctx.sbuf("atl1", [128, 32 * 128], BF16, stack=pp)]
                es_sb = [ctx.sbuf("es0", [128, 512], F32, stack=pp),
                         ctx.sbuf("es1", [128, 512], F32, stack=pp)]
                p_es = [ctx.psum("p_es0", [128, 512], F32, stack=pp),
                        ctx.psum("p_es1", [128, 512], F32, stack=pp)]

                ul = ctx.sem("ul"); al = ctx.sem("al")
                esm = ctx.sem("esm"); ese = ctx.sem("ese"); esx = ctx.sem("esx")

                @blk.gpsimd
                def _(g):
                    for oh in range(2):
                        if oh >= 1:
                            g.wait_ge(esm.sem, 16)
                        ul.inc(g.dma_start(
                            out=ue_sb[:].rearrange("p (ec o) -> p ec o", o=512),
                            in_=ued[:, oh * 512:(oh + 1) * 512]
                                .rearrange("(ec p) o -> p ec o", p=128)), 16)
                        for c in range(NCH):
                            i = oh * 16 + c
                            if i >= 2:
                                g.wait_ge(esm.sem, i - 1)
                            al.inc(g.dma_start(
                                out=atl[i % 2][:].rearrange("p (ec tt) -> p ec tt",
                                                            tt=128),
                                in_=ATd.rearrange("(ec p) tt -> p ec tt", p=128)
                                    [:, :, c * 128:(c + 1) * 128]), 16)

                @blk.tensor
                def _(t):
                    for oh in range(2):
                        t.wait_ge(ul.sem, 16 * (oh + 1))
                        for c in range(NCH):
                            i = oh * 16 + c
                            t.wait_ge(al.sem, 16 * (i + 1))
                            if i >= 2:
                                t.wait_ge(ese.sem, i - 1)
                            pb = p_es[i % 2]
                            for ec in range(32):
                                mmi = t.matmul(pb[:],
                                               atl[i % 2][:, ec * 128:(ec + 1) * 128],
                                               ue_sb[:, ec * 512:(ec + 1) * 512],
                                               start=(ec == 0), stop=(ec == 31))
                            esm.inc(mmi)

                @blk.scalar
                def _(s):
                    for oh in range(2):
                        for c in range(NCH):
                            i = oh * 16 + c
                            s.wait_ge(esm.sem, i + 1)
                            if i >= 2:
                                s.wait_ge(esx.sem, i - 1)
                            s.activation(es_sb[i % 2][:], p_es[i % 2][:],
                                         AF.Identity)
                            ese.inc(s.drain())

                @blk.vector
                def _(v):
                    for oh in range(2):
                        for c in range(NCH):
                            i = oh * 16 + c
                            v.wait_ge(ese.sem, i + 1)
                            v.tensor_add(x_ch(c)[:, oh * 512:(oh + 1) * 512],
                                         x_ch(c)[:, oh * 512:(oh + 1) * 512],
                                         es_sb[i % 2][:])
                            esx.inc(v.drain())

                ctx.barrier()

            if stop_after == "l2fc":
                raise _Cut()

            with contextlib.ExitStack() as pp:
                gu_sb = ctx.sbuf("gu_sb", [128, 8 * 2048], BF16, stack=pp)
                bgu_bf = ctx.sbuf("bgu_bf", [1, 8192], BF16, stack=pp)
                gq_sb = ctx.sbuf("gq_sb", [128, 1024], BF16, stack=pp)
                uq_sb = ctx.sbuf("uq_sb", [128, 1024], BF16, stack=pp)
                mq_sb = ctx.sbuf("mq_sb", [128, 1024], BF16, stack=pp)
                mt_sb = [ctx.sbuf("mt0", [128, 8 * 128], BF16, stack=pp),
                         ctx.sbuf("mt1", [128, 8 * 128], BF16, stack=pp)]
                p_g3 = [ctx.psum("p_g30", [128, 512], F32, stack=pp),
                        ctx.psum("p_g31", [128, 512], F32, stack=pp)]
                p_mt = ctx.psum("p_mt3", [128, 1024], BF16, stack=pp)

                gul = ctx.sem("gul"); bgl3 = ctx.sem("bgl3")
                gm3 = ctx.sem("gm3"); ge3 = ctx.sem("ge3"); mv3 = ctx.sem("mv3")
                mtm = ctx.sem("mtm"); mte = ctx.sem("mte"); mto = ctx.sem("mto")

                @blk.gpsimd
                def _(g):
                    bgl3.inc(g.dma_start(out=bgu_bf[0:1, 0:4096], in_=bgl), 16)
                    bgl3.inc(g.dma_start(out=bgu_bf[0:1, 4096:8192], in_=bul), 16)
                    for q4 in range(4):
                        if q4 >= 1:
                            g.wait_ge(gm3.sem, q4 * 64)
                        gul.inc(g.dma_start(
                            out=gu_sb[:].rearrange("p (dc o) -> p dc o", o=2048),
                            in_=wgul[q4:q4 + 1, :, :].rearrange(
                                "o (dc p) f -> p dc (o f)", p=128)), 16)

                @blk.tensor
                def _(t):
                    t.wait_ge(bgl3.sem, 32)
                    nmm = 0
                    for q4 in range(4):
                        t.wait_ge(gul.sem, 16 * (q4 + 1))
                        for c in range(NCH):
                            i = q4 * 16 + c
                            for gu in range(2):
                                for half in range(2):
                                    if nmm >= 2:
                                        t.wait_ge(ge3.sem, nmm - 1)
                                    pb = p_g3[nmm % 2]
                                    for dc in range(8):
                                        t.matmul(
                                            pb[:],
                                            hsf(dc)[:, c * 128:(c + 1) * 128],
                                            gu_sb[:, dc * 2048 + gu * 1024 +
                                                  half * 512:
                                                  dc * 2048 + gu * 1024 +
                                                  (half + 1) * 512],
                                            start=(dc == 0), stop=False)
                                    mmi = t.matmul(
                                        pb[:], ones1[:],
                                        bgu_bf[:, gu * 4096 + q4 * 1024 +
                                               half * 512:
                                               gu * 4096 + q4 * 1024 +
                                               (half + 1) * 512],
                                        start=False, stop=True)
                                    gm3.inc(mmi)
                                    nmm += 1
                            t.wait_ge(mv3.sem, i + 1)
                            if i >= 1:
                                t.wait_ge(mte.sem, i)
                            for tpi in range(8):
                                mmi = t.transpose(
                                    p_mt[:, tpi * 128:(tpi + 1) * 128],
                                    mq_sb[:, tpi * 128:(tpi + 1) * 128], ident[:])
                            mtm.inc(mmi)

                @blk.scalar
                def _(s):
                    nmm = 0
                    for q4 in range(4):
                        for c in range(NCH):
                            i = q4 * 16 + c
                            for gu in range(2):
                                for half in range(2):
                                    s.wait_ge(gm3.sem, nmm + 1)
                                    dst = gq_sb if gu == 0 else uq_sb
                                    if gu == 0 and half == 0 and i >= 1:
                                        s.wait_ge(mtm.sem, i)
                                    s.activation(
                                        dst[:, half * 512:(half + 1) * 512],
                                        p_g3[nmm % 2][:],
                                        AF.Silu if gu == 0 else AF.Identity)
                                    ge3.inc(s.drain())
                                    nmm += 1
                            s.wait_ge(mtm.sem, i + 1)
                            if i >= 2:
                                s.wait_ge(mto.sem, 16 * (i - 1))
                            s.activation(mt_sb[i % 2][:], p_mt[:], AF.Identity)
                            mte.inc(s.drain())

                @blk.vector
                def _(v):
                    for q4 in range(4):
                        for c in range(NCH):
                            i = q4 * 16 + c
                            v.wait_ge(ge3.sem, 4 * i + 4)
                            v.tensor_tensor(mq_sb[:], gq_sb[:], uq_sb[:],
                                            op=ALU.mult)
                            mv3.inc(v.drain())

                @blk.sync
                def _(sy):
                    for q4 in range(4):
                        for c in range(NCH):
                            i = q4 * 16 + c
                            sy.wait_ge(mte.sem, i + 1)
                            mto.inc(sy.dma_start(
                                out=mTd.rearrange("(fc p) tt -> p fc tt", p=128)
                                    [:, q4 * 8:(q4 + 1) * 8,
                                     c * 128:(c + 1) * 128],
                                in_=mt_sb[i % 2][:].rearrange(
                                    "p (fc tt) -> p fc tt", tt=128)), 16)
                    sy.wait_ge(mto.sem, 16 * 64)

                ctx.barrier()

            if stop_after == "l2fd":
                raise _Cut()

            with contextlib.ExitStack() as pp:
                wd_sb = ctx.sbuf("wd_sb", [128, 32 * 512], BF16, stack=pp)
                bd_bf = ctx.sbuf("bd_bf", [1, 1024], BF16, stack=pp)
                mtl = [ctx.sbuf("mtl0", [128, 32 * 128], BF16, stack=pp),
                       ctx.sbuf("mtl1", [128, 32 * 128], BF16, stack=pp)]
                dn_sb = [ctx.sbuf("dn0", [128, 512], F32, stack=pp),
                         ctx.sbuf("dn1", [128, 512], F32, stack=pp)]
                p_dn = [ctx.psum("p_dn0", [128, 512], F32, stack=pp),
                        ctx.psum("p_dn1", [128, 512], F32, stack=pp)]

                wl4 = ctx.sem("wl4"); ml4 = ctx.sem("ml4")
                dnm = ctx.sem("dnm"); dne = ctx.sem("dne"); dnx = ctx.sem("dnx")

                @blk.gpsimd
                def _(g):
                    wl4.inc(g.dma_start(out=bd_bf[:], in_=bdl), 16)
                    for oh in range(2):
                        if oh >= 1:
                            g.wait_ge(dnm.sem, 16)
                        wl4.inc(g.dma_start(
                            out=wd_sb[:].rearrange("p (fc o) -> p fc o", o=512),
                            in_=wdl[:, oh * 512:(oh + 1) * 512]
                                .rearrange("(fc p) o -> p fc o", p=128)), 16)
                        for c in range(NCH):
                            i = oh * 16 + c
                            if i >= 2:
                                g.wait_ge(dnm.sem, i - 1)
                            ml4.inc(g.dma_start(
                                out=mtl[i % 2][:].rearrange("p (fc tt) -> p fc tt",
                                                            tt=128),
                                in_=mTd.rearrange("(fc p) tt -> p fc tt", p=128)
                                    [:, :, c * 128:(c + 1) * 128]), 16)

                @blk.tensor
                def _(t):
                    for oh in range(2):
                        t.wait_ge(wl4.sem, 16 * (oh + 2))
                        for c in range(NCH):
                            i = oh * 16 + c
                            t.wait_ge(ml4.sem, 16 * (i + 1))
                            if i >= 2:
                                t.wait_ge(dne.sem, i - 1)
                            pb = p_dn[i % 2]
                            for fc in range(32):
                                t.matmul(pb[:],
                                         mtl[i % 2][:, fc * 128:(fc + 1) * 128],
                                         wd_sb[:, fc * 512:(fc + 1) * 512],
                                         start=(fc == 0), stop=False)
                            mmi = t.matmul(pb[:], ones1[:],
                                           bd_bf[:, oh * 512:(oh + 1) * 512],
                                           start=False, stop=True)
                            dnm.inc(mmi)

                @blk.scalar
                def _(s):
                    for oh in range(2):
                        for c in range(NCH):
                            i = oh * 16 + c
                            s.wait_ge(dnm.sem, i + 1)
                            if i >= 2:
                                s.wait_ge(dnx.sem, i - 1)
                            s.activation(dn_sb[i % 2][:], p_dn[i % 2][:],
                                         AF.Identity)
                            dne.inc(s.drain())

                @blk.vector
                def _(v):
                    for oh in range(2):
                        for c in range(NCH):
                            i = oh * 16 + c
                            v.wait_ge(dne.sem, i + 1)
                            v.tensor_add(x_ch(c)[:, oh * 512:(oh + 1) * 512],
                                         x_ch(c)[:, oh * 512:(oh + 1) * 512],
                                         dn_sb[i % 2][:])
                            dnx.inc(v.drain())

                ctx.barrier()

        fo3 = ctx.sem("fo3")

        @blk.sync
        def _(sy):
            fo3.inc(sy.dma_start(
                out=x3o.rearrange("(c p) d -> p c d", p=128),
                in_=x_tm[:].rearrange("p (c d) -> p c d", d=D)), 16)
            sy.wait_ge(fo3.sem, 16)

        ctx.barrier()

    except _Cut:
        ctx.barrier()
        xo3 = ctx.sem("xo3")

        @blk.sync
        def _(sy):
            xo3.inc(sy.dma_start(
                out=x3o.rearrange("(c p) d -> p c d", p=128),
                in_=x_tm[:].rearrange("p (c d) -> p c d", d=D)), 16)
            sy.wait_ge(xo3.sem, 16)

    ctx.finish()
    return nc




def build_launch4(stop_after=None):
    nc = bass.Bass("TRN2", target_bir_lowering=False, detect_race_conditions=False)
    stack = contextlib.ExitStack()
    ctx = Ctx(nc, stack)
    blk = ctx.block

    def dram(n, s, d=F32, k="ExternalInput"):
        return nc.dram_tensor(n, s, d, kind=k).ap()

    x3i = dram("x3i", [TPC, D])
    x3w = dram("x3w", [640, D])
    wq4 = dram("wq4", [D, D]); wk4 = dram("wk4", [D, D]); wv4 = dram("wv4", [D, D])
    wo4h = dram("wo4h", [64, 16 * 1024])
    wqkv = dram("wqkv", [D, 3072])
    applyM = dram("applyM", [128, 1])
    amt = dram("amt", [5, 640, 128])

    delta_o = dram("delta_o", [640, D], F32, "ExternalOutput")
    x3bo = dram("x3bo", [TPC, D], F32, "ExternalOutput")
    qo3 = dram("qo3", [TPC, D], BF16, "ExternalOutput")
    ko3 = dram("ko3", [TPC, D], BF16, "ExternalOutput")
    vo3 = dram("vo3", [TPC, D], BF16, "ExternalOutput")

    ld = ctx.sem("ld")
    ctx.setup_consts([1e-5, 1.0, EXPB], ld)
    ident = ctx.sbuf("ident", [128, 128], BF16)
    ones1 = ctx.sbuf("ones1", [1, 128], BF16)
    onescol = ctx.sbuf("onescol", [128, 64], BF16)
    st_all = ctx.sbuf("st_all", [128, NCH * 16], F32)
    apm_sb = ctx.sbuf("apm_sb", [128, 1], F32)

    @blk.sync
    def _(sy):
        ld.inc(sy.dma_start(out=apm_sb[:], in_=applyM), 16)

    ldc = ctx.sem("ldc")

    @blk.gpsimd
    def _(g):
        g.memset(ident[:], 0.0)
        g.affine_select(out=ident[:], in_=ident[:], compare_op=ALU.not_equal,
                        fill=1.0, base=0, pattern=[[-1, 128]], channel_multiplier=1)
        g.memset(ones1[:], 1.0)
        ldc.inc(g.memset(onescol[:], 1.0))

    for dec in (blk.vector, blk.scalar, blk.tensor):
        @dec
        def _(eng):
            eng.wait_ge(ld.sem, ld.n)
            eng.wait_ge(ldc.sem, ldc.n)

    try:
        with contextlib.ExitStack() as ph:
            xw_tm = ctx.sbuf("xw_tm", [128, 5 * D], F32, stack=ph)
            xnw_fm = ctx.sbuf("xnw_fm", [128, 8 * 640], BF16, stack=ph)
            wq_sb = ctx.sbuf("wq4_sb", [128, 8 * 1024], BF16, stack=ph)
            wk_sb = ctx.sbuf("wk4_sb", [128, 8 * 1024], BF16, stack=ph)
            wv_sb = ctx.sbuf("wv4_sb", [128, 8 * 1024], BF16, stack=ph)
            wo_sb = ctx.sbuf("wo4_sb", [64, 16 * 1024], BF16, stack=ph)
            qw2 = ctx.sbuf("qw2", [64, 16 * 640], BF16, stack=ph)
            kw2 = ctx.sbuf("kw2", [64, 16 * 640], BF16, stack=ph)
            vaug = ctx.sbuf("vaug4", [128, 5 * 16 * 65], BF16, stack=ph)
            am_sb = ctx.sbuf("am4_sb", [128, 25 * 128], BF16, stack=ph)
            delta_tm = ctx.sbuf("delta_tm", [128, 5 * D], F32, stack=ph)

            def xw_ch(c):
                return xw_tm[:, c * D:(c + 1) * D]

            def xnwf(dc):
                return xnw_fm[:, dc * 640:(dc + 1) * 640]

            lw4 = ctx.sem("lw4")

            @blk.gpsimd
            def _(g):
                lw4.inc(g.dma_start(
                    out=xw_tm[:].rearrange("p (c d) -> p c d", d=D),
                    in_=x3w.rearrange("(c p) d -> p c d", p=128)), 16)
                for w, t_ in ((wq4, wq_sb), (wk4, wk_sb), (wv4, wv_sb)):
                    lw4.inc(g.dma_start(
                        out=t_[:].rearrange("p (dc o) -> p dc o", o=1024),
                        in_=w.rearrange("(dc p) o -> p dc o", p=128)), 16)
                lw4.inc(g.dma_start(out=wo_sb[:], in_=wo4h), 16)
                lw4.inc(g.dma_start(
                    out=am_sb[:].rearrange("p (ck q) -> p ck q", q=128),
                    in_=amt.rearrange("c (k p) q -> p (c k) q", p=128)), 16)
                ctx.sem("vg4").inc(g.memset(
                    vaug[:].rearrange("p (ch x) -> p ch x", x=65)[:, :, 64:65],
                    1.0))

            for dec in (blk.vector, blk.scalar, blk.tensor):
                @dec
                def _(eng):
                    eng.wait_ge(lw4.sem, lw4.n)
                    eng.wait_ge(ctx.sem("vg4").sem, 1)

            with contextlib.ExitStack() as pp:
                xnw_tm = ctx.sbuf("xnw_tm", [128, 5 * D], BF16, stack=pp)
                scrw = ctx.sbuf("scrw", [128, 1024], F32, stack=pp)

                def xnw_ch(c):
                    return xnw_tm[:, c * D:(c + 1) * D]

                emit_ln(ctx, "w4", xw_ch, xnw_ch, 5, st_all, scrw)
                emit_transpose(ctx, "w4", xnw_ch,
                               lambda dc, c: xnwf(dc)[:, c * 128:(c + 1) * 128],
                               5, ident, pp)
                ctx.barrier()

            with contextlib.ExitStack() as pp:
                p_pj = [ctx.psum("p_pj0", [128, 512], F32, stack=pp),
                        ctx.psum("p_pj1", [128, 512], F32, stack=pp)]
                p_t4 = ctx.psum("p_t44", [64, 4096], BF16, stack=pp)
                qk_tm = ctx.sbuf("qk_tm4", [128, 2048], BF16, stack=pp)
                pjm = ctx.sem("pjm"); pje = ctx.sem("pje"); ttm = ctx.sem("ttm")
                tte = ctx.sem("tte")

                @blk.tensor
                def _(t):
                    ni = 0
                    for c in range(5):
                        for mat, wsb in ((0, wq_sb), (1, wk_sb), (2, wv_sb)):
                            for half in range(2):
                                if ni >= 2:
                                    t.wait_ge(pje.sem, ni - 1)
                                pb = p_pj[ni % 2]
                                for dc in range(8):
                                    mmi = t.matmul(
                                        pb[:], xnwf(dc)[:, c * 128:(c + 1) * 128],
                                        wsb[:, dc * 1024 + half * 512:
                                            dc * 1024 + (half + 1) * 512],
                                        start=(dc == 0), stop=(dc == 7))
                                pjm.inc(mmi)
                                ni += 1
                        t.wait_ge(pje.sem, ni)
                        if c >= 1:
                            t.wait_ge(tte.sem, c)
                        for i32 in range(32):
                            mmi = t.transpose(
                                p_t4[0:64, i32 * 128:(i32 + 1) * 128],
                                qk_tm[:, i32 * 64:(i32 + 1) * 64], ident[:])
                        ttm.inc(mmi)

                @blk.scalar
                def _(s):
                    ni = 0
                    for c in range(5):
                        for mat in range(3):
                            for half in range(2):
                                s.wait_ge(pjm.sem, ni + 1)
                                if mat == 0 and half == 0 and c >= 1:
                                    s.wait_ge(ttm.sem, c)
                                if mat == 0:
                                    s.activation(
                                        qk_tm[:, half * 512:(half + 1) * 512],
                                        p_pj[ni % 2][:], AF.Identity, scale=0.125)
                                elif mat == 1:
                                    s.activation(
                                        qk_tm[:, 1024 + half * 512:
                                              1024 + (half + 1) * 512],
                                        p_pj[ni % 2][:], AF.Identity)
                                else:
                                    for hq in range(8):
                                        h = half * 8 + hq
                                        s.activation(
                                            vaug[:, c * 1040 + h * 65:
                                                 c * 1040 + h * 65 + 64],
                                            p_pj[ni % 2][:, hq * 64:(hq + 1) * 64],
                                            AF.Identity)
                                pje.inc(s.drain())
                                ni += 1
                        s.wait_ge(ttm.sem, c + 1)
                        for h in range(16):
                            s.activation(qw2[:, h * 640 + c * 128:
                                             h * 640 + (c + 1) * 128],
                                         p_t4[0:64, h * 128:(h + 1) * 128],
                                         AF.Identity)
                            s.activation(kw2[:, h * 640 + c * 128:
                                             h * 640 + (c + 1) * 128],
                                         p_t4[0:64, (16 + h) * 128:(17 + h) * 128],
                                         AF.Identity)
                        tte.inc(s.drain())

                ctx.barrier()

            if stop_after == "wqkv":
                raise _Cut()

            with contextlib.ExitStack() as pp:
                p_sA = [ctx.psum("p_sA0", [128, 512], F32, stack=pp),
                        ctx.psum("p_sA1", [128, 512], F32, stack=pp)]
                p_sB = [ctx.psum("p_sB0", [128, 128], F32, stack=pp),
                        ctx.psum("p_sB1", [128, 128], F32, stack=pp)]
                p_o4 = ctx.psum("p_o44", [65, 128], F32, stack=pp)
                p_b4 = ctx.psum("p_b44", [64, 128], F32, stack=pp)
                p_oo = [ctx.psum("p_oo4a", [128, 512], F32, stack=pp),
                        ctx.psum("p_oo4b", [128, 512], F32, stack=pp)]
                pex = ctx.sbuf("pex", [128, 640], BF16, stack=pp)
                dn_sb = ctx.sbuf("dn_sb", [128, 128], BF16, stack=pp)
                bd4 = ctx.sbuf("bd44", [64, 128], BF16, stack=pp)
                atT = ctx.sbuf("atT4", [64, 16 * 128], BF16, stack=pp)
                od_sb = [ctx.sbuf("od0", [128, 512], F32, stack=pp),
                         ctx.sbuf("od1", [128, 512], F32, stack=pp)]

                sm4 = ctx.sem("sm4"); se4 = ctx.sem("se4"); mv4 = ctx.sem("mv4")
                pv4 = ctx.sem("pv4"); ev4 = ctx.sem("ev4"); rc4 = ctx.sem("rc4")
                bm4 = ctx.sem("bm4"); be4 = ctx.sem("be4"); nv4 = ctx.sem("nv4")
                om4 = ctx.sem("om4"); oev = ctx.sem("oev"); oxv = ctx.sem("oxv")
                do4 = ctx.sem("do4")

                @blk.tensor
                def _(t):
                    no = 0
                    for c in range(5):
                        for h in range(16):
                            i = c * 16 + h
                            if i >= 2:
                                t.wait_ge(se4.sem, i - 1)
                            pa = p_sA[i % 2]
                            pbq = p_sB[i % 2]
                            for kb in range(4):
                                mmi = t.matmul(
                                    pa[:, kb * 128:(kb + 1) * 128],
                                    kw2[:, h * 640 + kb * 128:
                                        h * 640 + (kb + 1) * 128],
                                    qw2[:, h * 640 + c * 128:
                                        h * 640 + (c + 1) * 128],
                                    start=True, stop=True)
                            mmi = t.matmul(
                                pbq[:],
                                kw2[:, h * 640 + 512:h * 640 + 640],
                                qw2[:, h * 640 + c * 128:h * 640 + (c + 1) * 128],
                                start=True, stop=True)
                            sm4.inc(mmi)
                            t.wait_ge(mv4.sem, i + 1)
                            if i >= 1:
                                t.wait_ge(ev4.sem, i)
                            for kb in range(5):
                                mmi = t.matmul(
                                    p_o4[:],
                                    vaug[:, kb * 1040 + h * 65:
                                         kb * 1040 + (h + 1) * 65],
                                    pex[:, kb * 128:(kb + 1) * 128],
                                    start=(kb == 0), stop=(kb == 4))
                            pv4.inc(mmi)
                            t.wait_ge(rc4.sem, i + 1)
                            if i >= 1:
                                t.wait_ge(be4.sem, i)
                            bm4.inc(t.matmul(p_b4[:], onescol[64:65, 0:64],
                                             dn_sb[64:65, 0:128],
                                             start=True, stop=True))
                        t.wait_ge(nv4.sem, 16 * (c + 1))
                        for half in range(2):
                            if no >= 2:
                                t.wait_ge(oev.sem, no - 1)
                            pb = p_oo[no % 2]
                            for h in range(16):
                                mmi = t.matmul(
                                    pb[:], atT[:, h * 128:(h + 1) * 128],
                                    wo_sb[:, h * 1024 + half * 512:
                                          h * 1024 + (half + 1) * 512],
                                    start=(h == 0), stop=(h == 15))
                            om4.inc(mmi)
                            no += 1

                @blk.scalar
                def _(s):
                    no = 0
                    for c in range(5):
                        for h in range(16):
                            i = c * 16 + h
                            s.wait_ge(sm4.sem, i + 1)
                            if i >= 1:
                                s.wait_ge(pv4.sem, i)
                            s.activation(pex[:, 0:512], p_sA[i % 2][:],
                                         AF.Exp, bias=EXPB)
                            s.activation(pex[:, 512:640], p_sB[i % 2][:],
                                         AF.Exp, bias=EXPB)
                            se4.inc(s.drain())
                            s.wait_ge(pv4.sem, i + 1)
                            if i >= 1:
                                s.wait_ge(nv4.sem, i)
                            s.activation(dn_sb[64:65, 0:128], p_o4[64:65, :],
                                         AF.Identity)
                            s.activation(atT[:, h * 128:(h + 1) * 128],
                                         p_o4[0:64, :], AF.Identity)
                            ev4.inc(s.drain())
                            s.wait_ge(bm4.sem, i + 1)
                            s.activation(bd4[:], p_b4[:], AF.Identity)
                            be4.inc(s.drain())
                        for half in range(2):
                            s.wait_ge(om4.sem, no + 1)
                            if no >= 2:
                                s.wait_ge(oxv.sem, no - 1)
                            s.activation(od_sb[no % 2][:], p_oo[no % 2][:],
                                         AF.Identity)
                            oev.inc(s.drain())
                            no += 1

                @blk.vector
                def _(v):
                    no = 0
                    for c in range(5):
                        for h in range(16):
                            i = c * 16 + h
                            v.wait_ge(se4.sem, i + 1)
                            v.tensor_tensor(pex[:], pex[:],
                                            am_sb[:, c * 640:(c + 1) * 640],
                                            op=ALU.mult)
                            mv4.inc(v.drain())
                            v.wait_ge(ev4.sem, i + 1)
                            with ctx.nc.allow_low_precision(
                                    reason="softmax denom recip bf16"):
                                v.reciprocal(dn_sb[64:65, 0:128],
                                             dn_sb[64:65, 0:128])
                            rc4.inc(v.drain())
                            v.wait_ge(be4.sem, i + 1)
                            v.tensor_tensor(atT[:, h * 128:(h + 1) * 128],
                                            atT[:, h * 128:(h + 1) * 128],
                                            bd4[:], op=ALU.mult)
                            nv4.inc(v.drain())
                        for half in range(2):
                            v.wait_ge(oev.sem, no + 1)
                            v.tensor_copy(
                                delta_tm[:, c * D + half * 512:
                                         c * D + (half + 1) * 512],
                                od_sb[no % 2][:])
                            oxv.inc(v.drain())
                            no += 1

                @blk.sync
                def _(sy):
                    sy.wait_ge(oxv.sem, 10)
                    do4.inc(sy.dma_start(
                        out=delta_o.rearrange("(c p) d -> p c d", p=128),
                        in_=delta_tm[:].rearrange("p (c d) -> p c d", d=D)), 16)
                    sy.wait_ge(do4.sem, 16)

                ctx.barrier()

        if stop_after == "l3a":
            raise _Cut()

        with contextlib.ExitStack() as ph:
            x_tm = ctx.sbuf("x_tm4", [128, NCH * D], F32, stack=ph)
            pd4 = ph.enter_context(contextlib.ExitStack())
            dsh = ctx.sbuf("dsh", [128, 5 * D], F32, stack=pd4)

            def x_ch(c):
                return x_tm[:, c * D:(c + 1) * D]

            lb4 = ctx.sem("lb4")

            @blk.gpsimd
            def _(g):
                g.memset(dsh[:], 0.0)
                lb4.inc(g.dma_start(
                    out=x_tm[:].rearrange("p (c d) -> p c d", d=D),
                    in_=x3i.rearrange("(c p) d -> p c d", p=128)), 16)
                g.wait_ge(ctx.sem("do4").sem, 16)
                lb4.inc(g.dma_start(out=dsh[124:128, 0:D], in_=delta_o[4:8, :]), 16)
                for k4 in range(4):
                    lb4.inc(g.dma_start(
                        out=dsh[:, (k4 + 1) * D:(k4 + 2) * D],
                        in_=delta_o[8 + 128 * k4:136 + 128 * k4, :]), 16)

            dav = ctx.sem("dav")

            @blk.vector
            def _(v):
                v.wait_ge(lb4.sem, lb4.n)
                for j5 in range(5):
                    c = 11 + j5
                    v.tensor_tensor(dsh[:, j5 * D:(j5 + 1) * D],
                                    dsh[:, j5 * D:(j5 + 1) * D],
                                    apm_sb[:].broadcast_to([128, D]), op=ALU.mult)
                    v.drain()
                    v.tensor_add(x_ch(c)[:], x_ch(c)[:],
                                 dsh[:, j5 * D:(j5 + 1) * D])
                    dav.inc(v.drain())

            @blk.scalar
            def _(s):
                s.wait_ge(dav.sem, 5)
            @blk.tensor
            def _(t):
                t.wait_ge(dav.sem, 5)

            xbo4 = ctx.sem("xbo4")

            @blk.sync
            def _(sy):
                sy.wait_ge(dav.sem, 5)
                xbo4.inc(sy.dma_start(
                    out=x3bo.rearrange("(c p) d -> p c d", p=128),
                    in_=x_tm[:].rearrange("p (c d) -> p c d", d=D)), 16)

            pd4.close()

            with contextlib.ExitStack() as pq:
                xnb_fm = ctx.sbuf("xnb4_fm", [128, 8 * TPC], BF16, stack=pq)
                wqkv_sb = ctx.sbuf("wqkv_sb", [128, 8 * 3072], BF16, stack=pq)
                qkv_sb = [ctx.sbuf("qkv40", [128, 3072], BF16, stack=pq),
                          ctx.sbuf("qkv41", [128, 3072], BF16, stack=pq)]

                def xnbf(dc):
                    return xnb_fm[:, dc * TPC:(dc + 1) * TPC]

                lq4 = ctx.sem("lq4")

                @blk.gpsimd
                def _(g):
                    lq4.inc(g.dma_start(
                        out=wqkv_sb[:].rearrange("p (dc o) -> p dc o", o=3072),
                        in_=wqkv.rearrange("(dc p) o -> p dc o", p=128)), 16)

                with contextlib.ExitStack() as pr:
                    xnb_tm = ctx.sbuf("xnb4_tm", [128, NCH * D], BF16, stack=pr)
                    scr6 = ctx.sbuf("scr6", [128, 1024], F32, stack=pr)

                    def xnb_ch(c):
                        return xnb_tm[:, c * D:(c + 1) * D]

                    emit_ln(ctx, "b4", x_ch, xnb_ch, NCH, st_all, scr6)
                    emit_transpose(ctx, "b4", xnb_ch,
                                   lambda dc, c: xnbf(dc)[:, c * 128:(c + 1) * 128],
                                   NCH, ident, pr)
                    ctx.barrier()

                with contextlib.ExitStack() as pp:
                    p_qk = [ctx.psum("p_qk0", [128, 512], F32, stack=pp),
                            ctx.psum("p_qk1", [128, 512], F32, stack=pp)]
                    qm4 = ctx.sem("qm4"); qe4 = ctx.sem("qe4"); qd4 = ctx.sem("qd4")

                    @blk.tensor
                    def _(t):
                        t.wait_ge(lq4.sem, 16)
                        ni = 0
                        for c in range(NCH):
                            for six in range(6):
                                if ni >= 2:
                                    t.wait_ge(qe4.sem, ni - 1)
                                pb = p_qk[ni % 2]
                                for dc in range(8):
                                    mmi = t.matmul(
                                        pb[:],
                                        xnbf(dc)[:, c * 128:(c + 1) * 128],
                                        wqkv_sb[:, dc * 3072 + six * 512:
                                                dc * 3072 + (six + 1) * 512],
                                        start=(dc == 0), stop=(dc == 7))
                                qm4.inc(mmi)
                                ni += 1

                    @blk.scalar
                    def _(s):
                        ni = 0
                        for c in range(NCH):
                            if c >= 2:
                                s.wait_ge(qd4.sem, 48 * (c - 1))
                            for six in range(6):
                                s.wait_ge(qm4.sem, ni + 1)
                                s.activation(
                                    qkv_sb[c % 2][:, six * 512:(six + 1) * 512],
                                    p_qk[ni % 2][:], AF.Identity)
                                qe4.inc(s.drain())
                                ni += 1

                    @blk.sync
                    def _(sy):
                        for c in range(NCH):
                            sy.wait_ge(qe4.sem, 6 * (c + 1))
                            for oi, od in enumerate((qo3, ko3, vo3)):
                                qd4.inc(sy.dma_start(
                                    out=od[c * 128:(c + 1) * 128, :],
                                    in_=qkv_sb[c % 2][:, oi * 1024:
                                                      (oi + 1) * 1024]), 16)
                        sy.wait_ge(qd4.sem, 48 * NCH)
                        sy.wait_ge(ctx.sem("xbo4").sem, 16)

                    ctx.barrier()

    except _Cut:
        ctx.barrier()
        fin4 = ctx.sem("fin4")

        @blk.sync
        def _(sy):
            fin4.inc(sy.dma_start(out=x3bo[0:128, 0:1], in_=apm_sb[:]), 16)
            sy.wait_ge(fin4.sem, 16)

    ctx.finish()
    return nc




def build_launch5(stop_after=None, dbg=False):
    nc = bass.Bass("TRN2", target_bir_lowering=False, detect_race_conditions=False)
    stack = contextlib.ExitStack()
    ctx = Ctx(nc, stack)
    blk = ctx.block

    def dram(n, s, d=F32, k="ExternalInput"):
        return nc.dram_tensor(n, s, d, kind=k).ap()

    x3bi = dram("x3bi", [TPC, D])
    qib = dram("qib", [TPC, D], BF16)
    kib = dram("kib", [T, D], BF16)
    vib = dram("vib", [T, D], BF16)
    woh5 = dram("woh5", [64, 16 * 1024])
    g1i = dram("g1i", [D]); b1i = dram("b1i", [D])
    g2i = dram("g2i", [D]); b2i = dram("b2i", [D])
    wg8 = dram("wg8", [D, 8])
    w1d = dram("w1d", [8, D, 4096])
    bb1d = dram("bb1d", [8, 4096])
    w2d = dram("w2d", [8, 4096, D])
    bb2d = dram("bb2d", [8, D])
    pmjc = dram("pmjc", [128, 512])
    c0c = dram("c0c", [128, 1])

    kTd = dram("kTd", [128, 8 * T], BF16, "Internal")
    x4o = dram("x4o", [TPC, D], F32, "ExternalOutput")
    if dbg:
        hto = dram("hto", [128, 8 * TPC], BF16, "ExternalOutput")
        htlo = dram("htlo", [128, 8 * TPC], BF16, "ExternalOutput")
        hmo = dram("hmo", [128, NCH * D], BF16, "ExternalOutput")
        gwo = dram("gwo", [128, NCH * 8], F32, "ExternalOutput")

    ld = ctx.sem("ld")
    ctx.setup_consts([1e-5, 1.0, EXPB], ld)
    x_tm = ctx.sbuf("x_tm5", [128, NCH * D], F32)
    ident = ctx.sbuf("ident", [128, 128], BF16)
    ones1 = ctx.sbuf("ones1", [1, 128], BF16)
    onescol = ctx.sbuf("onescol", [128, 64], BF16)
    st_all = ctx.sbuf("st_all", [128, NCH * 16], F32)
    pmj_sb = ctx.sbuf("pmj_sb", [128, 512], F32)
    c0_sb = ctx.sbuf("c0_sb", [128, 1], F32)

    def x_ch(c):
        return x_tm[:, c * D:(c + 1) * D]

    @blk.sync
    def _(sy):
        ld.inc(sy.dma_start(out=x_tm[:].rearrange("p (c d) -> p c d", d=D),
                            in_=x3bi.rearrange("(c p) d -> p c d", p=128)), 16)
        ld.inc(sy.dma_start(out=pmj_sb[:], in_=pmjc), 16)
        ld.inc(sy.dma_start(out=c0_sb[:], in_=c0c), 16)

    ldc = ctx.sem("ldc")

    @blk.gpsimd
    def _(g):
        g.memset(ident[:], 0.0)
        g.affine_select(out=ident[:], in_=ident[:], compare_op=ALU.not_equal,
                        fill=1.0, base=0, pattern=[[-1, 128]], channel_multiplier=1)
        g.memset(ones1[:], 1.0)
        ldc.inc(g.memset(onescol[:], 1.0))

    for dec in (blk.vector, blk.scalar, blk.tensor):
        @dec
        def _(eng):
            eng.wait_ge(ld.sem, ld.n)
            eng.wait_ge(ldc.sem, ldc.n)

    try:
        with contextlib.ExitStack() as ph0:
            xz5 = ctx.sbuf("xz5", [128, 1024], BF16, stack=ph0)
            xn32 = ctx.sbuf("xn32", [128, 1024], F32, stack=ph0)
            scr05 = ctx.sbuf("scr05", [128, 1024], F32, stack=ph0)
            emit_ln(ctx, "z5", x_ch, lambda c: xz5[:], NCH, st_all, scr05)
            zs5 = ctx.sem("zs5"); zv5 = ctx.sem("zv5")
            snz = ctx.sem("z5n")

            @blk.scalar
            def _(s):
                for c in range(NCH):
                    s.wait_ge(snz.sem, c + 1)
                    if c >= 1:
                        s.wait_ge(zv5.sem, c)
                    s.activation(xn32[:], x_ch(c)[:], AF.Identity,
                                 scale=st_all[:, c * 16 + 15:c * 16 + 16],
                                 bias=st_all[:, c * 16 + 14:c * 16 + 15])
                    zs5.inc(s.drain())

            @blk.vector
            def _(v):
                for c in range(NCH):
                    v.wait_ge(zs5.sem, c + 1)
                    v.tensor_copy(x_ch(c)[:], xn32[:])
                    zv5.inc(v.drain())

            ctx.barrier()

        with contextlib.ExitStack() as ph:
            qT_sb = ctx.sbuf("qT5", [128, 8 * TPC], BF16, stack=ph)

            with contextlib.ExitStack() as pp:
                kc_sb = [ctx.sbuf("kc50", [128, 1024], BF16, stack=pp),
                         ctx.sbuf("kc51", [128, 1024], BF16, stack=pp)]
                kt_ev = [ctx.sbuf("ktev0", [128, 1024], BF16, stack=pp),
                         ctx.sbuf("ktev1", [128, 1024], BF16, stack=pp)]
                p_kt = [ctx.psum("p_kt50", [128, 1024], BF16, stack=pp),
                        ctx.psum("p_kt51", [128, 1024], BF16, stack=pp)]
                kl5 = ctx.sem("kl5"); ktm5 = ctx.sem("ktm5")
                kte5 = ctx.sem("kte5"); kto5 = ctx.sem("kto5")

                @blk.gpsimd
                def _(g):
                    for kc in range(64):
                        if kc >= 2:
                            g.wait_ge(ktm5.sem, kc - 1)
                        kl5.inc(g.dma_start(
                            out=kc_sb[kc % 2][:],
                            in_=kib[kc * 128:(kc + 1) * 128, :]), 16)
                    for c in range(NCH):
                        i = 64 + c
                        if c >= 2:
                            g.wait_ge(ktm5.sem, i - 1)
                        kl5.inc(g.dma_start(
                            out=kc_sb[c % 2][:],
                            in_=qib[c * 128:(c + 1) * 128, :]), 16)

                @blk.tensor
                def _(t):
                    for i in range(64 + NCH):
                        t.wait_ge(kl5.sem, 16 * (i + 1))
                        if i >= 2:
                            t.wait_ge(kte5.sem, i - 1)
                        for dc in range(8):
                            mmi = t.transpose(
                                p_kt[i % 2][:, dc * 128:(dc + 1) * 128],
                                kc_sb[i % 2][:, dc * 128:(dc + 1) * 128],
                                ident[:])
                        ktm5.inc(mmi)

                @blk.scalar
                def _(s):
                    for i in range(64 + NCH):
                        s.wait_ge(ktm5.sem, i + 1)
                        if i < 64 and i >= 2:
                            s.wait_ge(kto5.sem, 16 * (i - 1))
                        if i < 64:
                            s.activation(kt_ev[i % 2][:], p_kt[i % 2][:],
                                         AF.Identity)
                        else:
                            c = i - 64
                            for dc in range(8):
                                s.activation(
                                    qT_sb[:, dc * TPC + c * 128:
                                          dc * TPC + (c + 1) * 128],
                                    p_kt[i % 2][:, dc * 128:(dc + 1) * 128],
                                    AF.Identity, scale=0.125)
                        kte5.inc(s.drain())

                @blk.sync
                def _(sy):
                    for kc in range(64):
                        sy.wait_ge(kte5.sem, kc + 1)
                        kto5.inc(sy.dma_start(
                            out=kTd.rearrange("p (dc tt) -> p dc tt", tt=T)
                                [:, :, kc * 128:(kc + 1) * 128],
                            in_=kt_ev[kc % 2][:].rearrange(
                                "p (dc x) -> p dc x", x=128)), 16)
                    sy.wait_ge(kto5.sem, 16 * 64)

                ctx.barrier()

            if stop_after == "ktb":
                raise _Cut()

            with contextlib.ExitStack() as pp:
                kb_sb = [ctx.sbuf("kb50", [128, T], BF16, stack=pp),
                         ctx.sbuf("kb51", [128, T], BF16, stack=pp)]
                va_sb = [ctx.sbuf("va50", [128, 2 * 65], BF16, stack=pp),
                         ctx.sbuf("va51", [128, 2 * 65], BF16, stack=pp)]
                pstage = [ctx.sbuf("pst50", [128, 1024], BF16, stack=pp),
                          ctx.sbuf("pst51", [128, 1024], BF16, stack=pp)]
                msk = ctx.sbuf("msk5", [128, 512], BF16, stack=pp)
                mw = ctx.sbuf("mw5", [128, 512], F32, stack=pp)
                dn5 = ctx.sbuf("dn5", [128, 1024], BF16, stack=pp)
                atT5 = ctx.sbuf("atT5", [64, 16 * 512], BF16, stack=pp)
                woh_sb = ctx.sbuf("woh_sb", [64, 16 * 1024], BF16, stack=pp)
                oo5 = [ctx.sbuf("oo50", [128, 512], F32, stack=pp),
                       ctx.sbuf("oo51", [128, 512], F32, stack=pp)]
                p_s5 = [ctx.psum("p_s50", [128, 1024], F32, stack=pp),
                        ctx.psum("p_s51", [128, 1024], F32, stack=pp)]
                p_oA = ctx.psum("p_oA5", [128, 512], F32, stack=pp)
                p_oB = ctx.psum("p_oB5", [128, 512], F32, stack=pp)
                p_b5 = ctx.psum("p_b55", [64, 512], F32, stack=pp)
                p_op5 = [ctx.psum("p_op50", [128, 512], F32, stack=pp)]

                wol = ctx.sem("wol"); kbl = ctx.sem("kbl"); val = ctx.sem("val")
                sS5 = ctx.sem("sS5"); sE5 = ctx.sem("sE5"); sM5 = ctx.sem("sM5")
                sPV = ctx.sem("sPV"); sEo = ctx.sem("sEo"); sRc = ctx.sem("sRc")
                sBm = ctx.sem("sBm"); sBe = ctx.sem("sBe"); sNv = ctx.sem("sNv")
                sOm = ctx.sem("sOm"); sOe = ctx.sem("sOe"); sXr = ctx.sem("sXr")

                @blk.gpsimd
                def _(g):
                    wol.inc(g.dma_start(out=woh_sb[:], in_=woh5), 16)
                    g.memset(va_sb[0][:].rearrange("p (h x) -> p h x", x=65)
                             [:, :, 64:65], 1.0)
                    ctx.sem("vg5").inc(g.memset(
                        va_sb[1][:].rearrange("p (h x) -> p h x",
                                              x=65)[:, :, 64:65], 1.0))
                    nld = 0
                    for sg in range(32):
                        ti, pr = sg // 8, sg % 8
                        if sg >= 2:
                            g.wait_ge(sS5.sem, 64 * (sg - 1))
                        kbl.inc(g.dma_start(
                            out=kb_sb[sg % 2][:],
                            in_=kTd[:, pr * T:(pr + 1) * T]), 16)
                        for kc in range(64):
                            i = sg * 64 + kc
                            if i >= 2:
                                g.wait_ge(sPV.sem, i - 1)
                            val.inc(g.dma_start(
                                out=va_sb[i % 2][:]
                                    .rearrange("p (h x) -> p h x", x=65)
                                    [:, :, 0:64],
                                in_=vib[kc * 128:(kc + 1) * 128,
                                        pr * 128:(pr + 1) * 128]
                                    .rearrange("p (h n) -> p h n", n=64)), 16)

                @blk.tensor
                def _(t):
                    t.wait_ge(wol.sem, wol.n)
                    t.wait_ge(ctx.sem("vg5").sem, 1)
                    no = 0
                    for sg in range(32):
                        ti, pr = sg // 8, sg % 8
                        t.wait_ge(kbl.sem, 16 * (sg + 1))
                        for kc in range(64):
                            i = sg * 64 + kc
                            if i >= 2:
                                t.wait_ge(sE5.sem, i - 1)
                            pb = p_s5[i % 2]
                            for j in range(2):
                                mmi = t.matmul(
                                    pb[:, j * 512:(j + 1) * 512],
                                    kb_sb[sg % 2][j * 64:(j + 1) * 64,
                                                  kc * 128:(kc + 1) * 128],
                                    qT_sb[j * 64:(j + 1) * 64,
                                          pr * TPC + ti * 512:
                                          pr * TPC + (ti + 1) * 512],
                                    start=True, stop=True)
                            sS5.inc(mmi)
                            t.wait_ge(sM5.sem, i + 1)
                            t.wait_ge(val.sem, 16 * (i + 1))
                            for j in range(2):
                                mmi = t.matmul(
                                    (p_oA if j == 0 else p_oB)[0:65, :],
                                    va_sb[i % 2][:, j * 65:(j + 1) * 65],
                                    pstage[i % 2][:, j * 512:(j + 1) * 512],
                                    start=(kc == 0), stop=(kc == 63))
                            sPV.inc(mmi)
                        t.wait_ge(sRc.sem, sg + 1)
                        if sg >= 1:
                            t.wait_ge(sBe.sem, 2 * sg)
                        sBm.inc(t.matmul(p_b5[:], onescol[64:65, 0:64],
                                         dn5[64:65, 0:512], start=True,
                                         stop=True))
                        t.wait_ge(sBe.sem, 2 * sg + 1)
                        sBm.inc(t.matmul(p_b5[:], onescol[64:65, 0:64],
                                         dn5[64:65, 512:1024], start=True,
                                         stop=True))
                        if pr == 7:
                            t.wait_ge(sNv.sem, 16 * (ti + 1))
                            for tc in range(4):
                                for half in range(2):
                                    if no >= 1:
                                        t.wait_ge(sOe.sem, no)
                                    pb2 = p_op5[0]
                                    for h in range(16):
                                        mmi = t.matmul(
                                            pb2[:],
                                            atT5[:, h * 512 + tc * 128:
                                                 h * 512 + (tc + 1) * 128],
                                            woh_sb[:, h * 1024 + half * 512:
                                                   h * 1024 + (half + 1) * 512],
                                            start=(h == 0), stop=(h == 15))
                                    sOm.inc(mmi)
                                    no += 1

                @blk.scalar
                def _(s):
                    no = 0
                    for sg in range(32):
                        ti, pr = sg // 8, sg % 8
                        for kc in range(64):
                            i = sg * 64 + kc
                            s.wait_ge(sS5.sem, i + 1)
                            if i >= 1:
                                s.wait_ge(sM5.sem, i)
                            s.activation(pstage[i % 2][:], p_s5[i % 2][:],
                                         AF.Exp, bias=EXPB)
                            sE5.inc(s.drain())
                        s.wait_ge(sPV.sem, 64 * (sg + 1))
                        if sg >= 1:
                            s.wait_ge(sNv.sem, 2 * sg)
                        s.activation(dn5[64:65, 0:512], p_oA[64:65, :],
                                     AF.Identity)
                        s.activation(dn5[64:65, 512:1024], p_oB[64:65, :],
                                     AF.Identity)
                        s.activation(atT5[:, (2 * pr) * 512:(2 * pr + 1) * 512],
                                     p_oA[0:64, :], AF.Identity)
                        s.activation(atT5[:, (2 * pr + 1) * 512:
                                          (2 * pr + 2) * 512],
                                     p_oB[0:64, :], AF.Identity)
                        sEo.inc(s.drain())
                        s.wait_ge(sBm.sem, 2 * sg + 1)
                        s.activation(dn5[0:64, 0:512], p_b5[:], AF.Identity)
                        sBe.inc(s.drain())
                        s.wait_ge(sBm.sem, 2 * sg + 2)
                        s.activation(dn5[0:64, 512:1024], p_b5[:], AF.Identity)
                        sBe.inc(s.drain())
                        if pr == 7:
                            for tc in range(4):
                                for half in range(2):
                                    s.wait_ge(sOm.sem, no + 1)
                                    if no >= 2:
                                        s.wait_ge(sXr.sem, no - 1)
                                    s.activation(oo5[no % 2][:],
                                                 p_op5[0][:], AF.Identity)
                                    sOe.inc(s.drain())
                                    no += 1

                @blk.vector
                def _(v):
                    no = 0
                    for sg in range(32):
                        ti, pr = sg // 8, sg % 8
                        for kc in range(64):
                            i = sg * 64 + kc
                            v.wait_ge(sE5.sem, i + 1)
                            v.tensor_scalar_add(mw[:], pmj_sb[:],
                                                float(kc * 128 - ti * 512))
                            v.drain()
                            v.tensor_tensor(msk[:], mw[:],
                                            c0_sb[:].broadcast_to([128, 512]),
                                            op=ALU.is_le)
                            v.drain()
                            v.tensor_tensor(pstage[i % 2][:, 0:512],
                                            pstage[i % 2][:, 0:512], msk[:],
                                            op=ALU.mult)
                            v.drain()
                            v.tensor_tensor(pstage[i % 2][:, 512:1024],
                                            pstage[i % 2][:, 512:1024], msk[:],
                                            op=ALU.mult)
                            sM5.inc(v.drain())
                        v.wait_ge(sEo.sem, sg + 1)
                        with ctx.nc.allow_low_precision(
                                reason="softmax denom recip bf16"):
                            v.reciprocal(dn5[64:65, 0:1024], dn5[64:65, 0:1024])
                        sRc.inc(v.drain())
                        v.wait_ge(sBe.sem, 2 * sg + 1)
                        v.tensor_tensor(atT5[:, (2 * pr) * 512:
                                             (2 * pr + 1) * 512],
                                        atT5[:, (2 * pr) * 512:
                                             (2 * pr + 1) * 512],
                                        dn5[0:64, 0:512], op=ALU.mult)
                        v.drain()
                        v.wait_ge(sBe.sem, 2 * sg + 2)
                        v.tensor_tensor(atT5[:, (2 * pr + 1) * 512:
                                             (2 * pr + 2) * 512],
                                        atT5[:, (2 * pr + 1) * 512:
                                             (2 * pr + 2) * 512],
                                        dn5[0:64, 512:1024], op=ALU.mult)
                        sNv.inc(v.drain())
                        sNv.inc(v.drain())
                        if pr == 7:
                            for tc in range(4):
                                cglob = ti * 4 + tc
                                for half in range(2):
                                    v.wait_ge(sOe.sem, no + 1)
                                    v.tensor_add(
                                        x_ch(cglob)[:, half * 512:
                                                    (half + 1) * 512],
                                        x_ch(cglob)[:, half * 512:
                                                    (half + 1) * 512],
                                        oo5[no % 2][:])
                                    sXr.inc(v.drain())
                                    no += 1

                ctx.barrier()

        if stop_after == "attn5":
            raise _Cut()

        with contextlib.ExitStack() as ph:
            hT_fm = ctx.sbuf("hT_fm", [128, 8 * TPC], BF16, stack=ph)
            g1c = ctx.sbuf("g1c", [128, 8], F32, stack=ph)
            b1c = ctx.sbuf("b1c", [128, 8], F32, stack=ph)
            g2c = ctx.sbuf("g2c", [128, 8], F32, stack=ph)
            b2c = ctx.sbuf("b2c", [128, 8], F32, stack=ph)
            gw5 = ctx.sbuf("gw5", [128, NCH * 8], F32, stack=ph)
            gbf5 = ctx.sbuf("gbf5", [128, 8], BF16, stack=ph)
            wg_sb5 = ctx.sbuf("wg_sb5", [128, 8 * 8], BF16, stack=ph)
            gst5 = ctx.sbuf("gst5", [128, 8], F32, stack=ph)
            gsc5 = ctx.sbuf("gsc5", [128, 8], F32, stack=ph)
            pd5 = ph.enter_context(contextlib.ExitStack())
            h_tm = ctx.sbuf("h_tm", [128, NCH * D], BF16, stack=pd5)
            hT_lo = ctx.sbuf("hT_lo", [128, 8 * TPC], BF16, stack=pd5)
            wgf = ctx.sbuf("wgf", [128, 8 * 8], F32, stack=pd5)
            wgl_sb = ctx.sbuf("wgl_sb", [128, 8 * 8], BF16, stack=pd5)

            def hTf(dc):
                return hT_fm[:, dc * TPC:(dc + 1) * TPC]

            def h_ch(c):
                return h_tm[:, c * D:(c + 1) * D]

            lg5 = ctx.sem("lg5")

            @blk.gpsimd
            def _(g):
                lg5.inc(g.dma_start(out=g1c[:],
                                    in_=g1i.rearrange("(dc p) -> p dc", p=128),
                                    allow_slow_non_contiguous=True), 16)
                lg5.inc(g.dma_start(out=b1c[:],
                                    in_=b1i.rearrange("(dc p) -> p dc", p=128),
                                    allow_slow_non_contiguous=True), 16)
                lg5.inc(g.dma_start(out=g2c[:],
                                    in_=g2i.rearrange("(dc p) -> p dc", p=128),
                                    allow_slow_non_contiguous=True), 16)
                lg5.inc(g.dma_start(out=b2c[:],
                                    in_=b2i.rearrange("(dc p) -> p dc", p=128),
                                    allow_slow_non_contiguous=True), 16)

            with contextlib.ExitStack() as pp:
                hraw_tm = ctx.sbuf("hraw_tm", [128, NCH * D], BF16, stack=pp)
                scr7 = ctx.sbuf("scr7", [128, 1024], F32, stack=pp)
                hfT = scr7
                scrX = ctx.sbuf("scrX", [128, 1024], F32, stack=pp)
                lo_tm = ctx.sbuf("lo_tm5", [128, 1024], BF16, stack=pp)
                p_ht = [ctx.psum("p_ht0", [128, 1024], BF16, stack=pp),
                        ctx.psum("p_ht1", [128, 1024], BF16, stack=pp)]
                p_hb = [ctx.psum("p_hb0", [128, 1024], BF16, stack=pp),
                        ctx.psum("p_hb1", [128, 1024], BF16, stack=pp)]
                p_lo5 = ctx.psum("p_lo5", [128, 1024], BF16, stack=pp)

                def hraw_ch(c):
                    return hraw_tm[:, c * D:(c + 1) * D]

                emit_ln(ctx, "h5", x_ch, hraw_ch, NCH, st_all, scr7)

                htm = ctx.sem("htm"); hte = ctx.sem("hte")
                hbm = ctx.sem("hbm"); hbe = ctx.sem("hbe")
                hlv = ctx.sem("hlv"); xsc5 = ctx.sem("xsc5")
                xsv5 = ctx.sem("xsv5"); ltm5 = ctx.sem("ltm5")
                lte5 = ctx.sem("lte5"); lcv5 = ctx.sem("lcv5")
                sn5 = ctx.sem("h5n")

                @blk.tensor
                def _(t):
                    t.wait_ge(lg5.sem, 64)
                    for c in range(NCH):
                        t.wait_ge(sn5.sem, c + 1)
                        if c >= 2:
                            t.wait_ge(hte.sem, c - 1)
                        for dc in range(8):
                            mmi = t.transpose(
                                p_ht[c % 2][:, dc * 128:(dc + 1) * 128],
                                hraw_ch(c)[:, dc * 128:(dc + 1) * 128], ident[:])
                        htm.inc(mmi)
                        t.wait_ge(hte.sem, c + 1)
                        if c >= 2:
                            t.wait_ge(hbe.sem, c - 1)
                        for dc in range(8):
                            mmi = t.transpose(
                                p_hb[c % 2][:, dc * 128:(dc + 1) * 128],
                                hTf(dc)[:, c * 128:(c + 1) * 128], ident[:])
                        hbm.inc(mmi)
                        t.wait_ge(xsv5.sem, c + 1)
                        if c >= 1:
                            t.wait_ge(lte5.sem, c)
                        for dc in range(8):
                            mmi = t.transpose(
                                p_lo5[:, dc * 128:(dc + 1) * 128],
                                lo_tm[:, dc * 128:(dc + 1) * 128], ident[:])
                        ltm5.inc(mmi)

                @blk.scalar
                def _(s):
                    for c in range(NCH):
                        if c >= 1:
                            s.wait_ge(lcv5.sem, 2 * c)
                        s.activation(scrX[:], x_ch(c)[:], AF.Identity,
                                     scale=st_all[:, c * 16 + 15:c * 16 + 16],
                                     bias=st_all[:, c * 16 + 14:c * 16 + 15])
                        xsc5.inc(s.drain())
                        s.wait_ge(htm.sem, c + 1)
                        if c >= 1:
                            s.wait_ge(hlv.sem, c)
                        for dc in range(8):
                            s.activation(hTf(dc)[:, c * 128:(c + 1) * 128],
                                         p_ht[c % 2][:, dc * 128:(dc + 1) * 128],
                                         AF.Identity, scale=g1c[:, dc:dc + 1],
                                         bias=b1c[:, dc:dc + 1])
                            s.activation(hfT[:, dc * 128:(dc + 1) * 128],
                                         p_ht[c % 2][:, dc * 128:(dc + 1) * 128],
                                         AF.Identity, scale=g1c[:, dc:dc + 1],
                                         bias=b1c[:, dc:dc + 1])
                        hte.inc(s.drain())
                        s.wait_ge(ltm5.sem, c + 1)
                        s.wait_ge(lcv5.sem, 2 * c + 1)
                        for dc in range(8):
                            s.activation(scrX[:, dc * 128:(dc + 1) * 128],
                                         p_lo5[:, dc * 128:(dc + 1) * 128],
                                         AF.Identity, scale=g1c[:, dc:dc + 1])
                        lte5.inc(s.drain())
                        s.wait_ge(hbm.sem, c + 1)
                        for dc in range(8):
                            s.activation(h_ch(c)[:, dc * 128:(dc + 1) * 128],
                                         p_hb[c % 2][:, dc * 128:(dc + 1) * 128],
                                         AF.Identity)
                        hbe.inc(s.drain())

                @blk.vector
                def _(v):
                    for c in range(NCH):
                        v.wait_ge(xsc5.sem, c + 1)
                        v.wait_ge(sn5.sem, c + 1)
                        if c >= 1:
                            v.wait_ge(ltm5.sem, c)
                        v.tensor_tensor(lo_tm[:], scrX[:], hraw_ch(c),
                                        op=ALU.subtract)
                        xsv5.inc(v.drain())
                        v.wait_ge(hte.sem, c + 1)
                        for dc in range(8):
                            v.tensor_tensor(
                                hT_lo[:, dc * TPC + c * 128:
                                      dc * TPC + (c + 1) * 128],
                                hfT[:, dc * 128:(dc + 1) * 128],
                                hTf(dc)[:, c * 128:(c + 1) * 128],
                                op=ALU.subtract)
                        lcv5.inc(v.drain())
                        v.wait_ge(lte5.sem, c + 1)
                        for dc in range(8):
                            v.tensor_add(
                                hT_lo[:, dc * TPC + c * 128:
                                      dc * TPC + (c + 1) * 128],
                                hT_lo[:, dc * TPC + c * 128:
                                      dc * TPC + (c + 1) * 128],
                                scrX[:, dc * 128:(dc + 1) * 128])
                        lcv5.inc(v.drain())
                        hlv.inc(v.drain())

                ctx.barrier()

            if stop_after == "hfold":
                raise _Cut()

            @blk.gpsimd
            def _(g):
                lg5.inc(g.dma_start(
                    out=wg_sb5[:].rearrange("p (dc e) -> p dc e", e=8),
                    in_=wg8.rearrange("(dc p) e -> p dc e", p=128)), 16)
                lg5.inc(g.dma_start(
                    out=wgf[:].rearrange("p (dc e) -> p dc e", e=8),
                    in_=wg8.rearrange("(dc p) e -> p dc e", p=128)), 16)

            wgv5 = ctx.sem("wgv5")

            @blk.vector
            def _(v):
                v.wait_ge(lg5.sem, 96)
                v.tensor_tensor(wgl_sb[:], wgf[:], wg_sb5[:], op=ALU.subtract)
                wgv5.inc(v.drain())

            with contextlib.ExitStack() as pp:
                p_gl5 = [ctx.psum("p_gl50", [128, 8], F32, stack=pp),
                         ctx.psum("p_gl51", [128, 8], F32, stack=pp)]
                gm5 = ctx.sem("gm5"); ge5 = ctx.sem("ge5"); gv15 = ctx.sem("gv15")
                gs15 = ctx.sem("gs15"); gv25 = ctx.sem("gv25")

                @blk.tensor
                def _(t):
                    t.wait_ge(lg5.sem, 96)
                    t.wait_ge(wgv5.sem, 1)
                    for c in range(NCH):
                        if c >= 2:
                            t.wait_ge(ge5.sem, c - 1)
                        for dc in range(8):
                            t.matmul(p_gl5[c % 2][:],
                                     hTf(dc)[:, c * 128:(c + 1) * 128],
                                     wg_sb5[:, dc * 8:(dc + 1) * 8],
                                     start=(dc == 0), stop=False)
                        for dc in range(8):
                            t.matmul(p_gl5[c % 2][:],
                                     hT_lo[:, dc * TPC + c * 128:
                                           dc * TPC + (c + 1) * 128],
                                     wg_sb5[:, dc * 8:(dc + 1) * 8],
                                     start=False, stop=False)
                        for dc in range(8):
                            mmi = t.matmul(p_gl5[c % 2][:],
                                           hTf(dc)[:, c * 128:(c + 1) * 128],
                                           wgl_sb[:, dc * 8:(dc + 1) * 8],
                                           start=False, stop=(dc == 7))
                        gm5.inc(mmi)

                @blk.scalar
                def _(s):
                    for c in range(NCH):
                        s.wait_ge(gm5.sem, c + 1)
                        s.activation(gw5[:, c * 8:(c + 1) * 8], p_gl5[c % 2][:],
                                     AF.Identity)
                        ge5.inc(s.drain())
                        s.wait_ge(gv15.sem, c + 1)
                        s.activation(gsc5[:], gw5[:, c * 8:(c + 1) * 8], AF.Exp,
                                     bias=gst5[:, 1:2])
                        gs15.inc(s.drain())


                @blk.vector
                def _(v):
                    for c in range(NCH):
                        gwc5 = gw5[:, c * 8:(c + 1) * 8]
                        v.wait_ge(ge5.sem, c + 1)
                        v.tensor_reduce(gst5[:, 0:1], gwc5, axis=AX, op=ALU.max)
                        v.drain()
                        v.tensor_scalar_mul(gst5[:, 1:2], gst5[:, 0:1], -1.0)
                        gv15.inc(v.drain())
                        v.wait_ge(gs15.sem, c + 1)
                        v.tensor_reduce(gst5[:, 2:3], gsc5[:], axis=AX, op=ALU.add)
                        v.drain()
                        v.reciprocal(gst5[:, 3:4], gst5[:, 2:3])
                        v.drain()
                        v.tensor_tensor(gsc5[:], gsc5[:],
                                        gst5[:, 3:4].broadcast_to([128, 8]),
                                        op=ALU.mult)
                        v.drain()
                        v.tensor_reduce(gst5[:, 4:5], gsc5[:], axis=AX, op=ALU.max)
                        v.drain()
                        v.tensor_tensor(gw5[:, c * 8:(c + 1) * 8], gsc5[:],
                                        gst5[:, 4:5].broadcast_to([128, 8]),
                                        op=ALU.is_equal)
                        v.drain()
                        v.tensor_scalar_mul(gwc5, gwc5, -1e30)
                        v.drain()
                        v.tensor_add(gwc5, gwc5, gsc5[:])
                        v.drain()
                        v.tensor_reduce(gst5[:, 5:6], gwc5, axis=AX, op=ALU.max)
                        v.drain()
                        v.tensor_tensor(gwc5, gsc5[:],
                                        gst5[:, 5:6].broadcast_to([128, 8]),
                                        op=ALU.is_ge)
                        v.drain()
                        v.tensor_tensor(gwc5, gwc5, gsc5[:], op=ALU.mult)
                        v.drain()
                        v.tensor_tensor(gst5[:, 6:7], gst5[:, 4:5], gst5[:, 5:6],
                                        op=ALU.add)
                        v.drain()
                        v.reciprocal(gst5[:, 7:8], gst5[:, 6:7])
                        v.drain()
                        v.tensor_tensor(gwc5, gwc5,
                                        gst5[:, 7:8].broadcast_to([128, 8]),
                                        op=ALU.mult)
                        v.drain()
                        gv25.inc(v.drain())

                ctx.barrier()

            if stop_after == "route5":
                raise _Cut()

            cpx = ctx.sem("cpx")

            @blk.vector
            def _(v):
                for c in range(NCH):
                    v.tensor_copy(x_ch(c)[:], h_ch(c)[:])
                    cpx.inc(v.drain())

            ctx.barrier()

            if dbg:
                dg5 = ctx.sem("dg5")

                @blk.sync
                def _(sy):
                    sy.wait_ge(ctx.sem("cpx").sem, NCH)
                    dg5.inc(sy.dma_start(out=hto, in_=hT_fm[:]), 16)
                    dg5.inc(sy.dma_start(out=htlo, in_=hT_lo[:]), 16)
                    dg5.inc(sy.dma_start(out=hmo, in_=h_tm[:]), 16)
                    dg5.inc(sy.dma_start(out=gwo, in_=gw5[:]), 16)
                    sy.wait_ge(dg5.sem, 64)

                ctx.barrier()

            if stop_after == "cpx":
                raise _Cut()

            if stop_after == "bb2":
                raise _Cut()

            pd5.close()

            NE5 = 8
            if stop_after and stop_after.startswith("moe") and len(stop_after) > 3 \
                    and stop_after[3:].isdigit():
                NE5 = int(stop_after[3:])
            with contextlib.ExitStack() as pp:
                wup5 = ctx.sbuf("wup5", [128, 8 * 2048], BF16, stack=pp)
                bb2r = ctx.sbuf("bb2r", [1, 1024], BF16, stack=pp)
                wdn5 = ctx.sbuf("wdn5", [128, 16 * 1024], BF16, stack=pp)
                hT5 = ctx.sbuf("hT55", [128, 16 * 512], BF16, stack=pp)
                bup5 = ctx.sbuf("bup5", [128, 32], F32, stack=pp)
                dop5 = [ctx.sbuf("dop50", [128, 512], F32, stack=pp),
                        ctx.sbuf("dop51", [128, 512], F32, stack=pp)]
                p_h5 = [ctx.psum("p_h50", [128, 512], F32, stack=pp),
                        ctx.psum("p_h51", [128, 512], F32, stack=pp)]
                p_d5 = [ctx.psum("p_d50", [128, 512], F32, stack=pp),
                        ctx.psum("p_d51", [128, 512], F32, stack=pp)]

                wl5 = ctx.sem("wl5"); bl5 = ctx.sem("bl5")
                hm5 = ctx.sem("hm5"); he5 = ctx.sem("he5"); dm5 = ctx.sem("dm5")
                dev5 = ctx.sem("dev5"); dx5 = ctx.sem("dx5")

                @blk.gpsimd
                def _(g):
                    for e in range(NE5):
                        if e >= 1:
                            g.wait_ge(he5.sem, e * 128)
                            g.wait_ge(dm5.sem, e * 64)
                        bl5.inc(g.dma_start(out=bb2r[:], in_=bb2d[e:e + 1, :]), 16)
                        bl5.inc(g.dma_start(
                            out=bup5[:],
                            in_=bb1d[e:e + 1, :].rearrange("o (fc p) -> p (o fc)",
                                                           p=128),
                            allow_slow_non_contiguous=True), 16)
                        for fh in range(2):
                            i = e * 2 + fh
                            if i >= 1:
                                g.wait_ge(hm5.sem, i * 64)
                            wl5.inc(g.dma_start(
                                out=wup5[:].rearrange("p (dc f) -> p dc f", f=2048),
                                in_=w1d[e:e + 1, :, fh * 2048:(fh + 1) * 2048]
                                    .rearrange("o (dc p) f -> p dc (o f)",
                                               p=128)), 16)
                            if i >= 1:
                                g.wait_ge(dm5.sem, i * 32)
                            wl5.inc(g.dma_start(
                                out=wdn5[:].rearrange("p (fc o) -> p fc o", o=1024),
                                in_=w2d[e:e + 1, fh * 2048:(fh + 1) * 2048, :]
                                    .rearrange("o (fc p) d -> p fc (o d)",
                                               p=128)), 16)

                @blk.tensor
                def _(t):
                    nh = 0
                    nd = 0
                    for e in range(NE5):
                        t.wait_ge(bl5.sem, 32 * (e + 1))
                        for fh in range(2):
                            i = e * 2 + fh
                            t.wait_ge(wl5.sem, 32 * i + 16)
                            for g4 in range(4):
                                for fc in range(16):
                                    if nh >= 2:
                                        t.wait_ge(he5.sem, nh - 1)
                                    pb = p_h5[nh % 2]
                                    for dc in range(8):
                                        mmi = t.matmul(
                                            pb[:],
                                            wup5[:, dc * 2048 + fc * 128:
                                                 dc * 2048 + (fc + 1) * 128],
                                            hTf(dc)[:, g4 * 512:(g4 + 1) * 512],
                                            start=(dc == 0), stop=(dc == 7))
                                    hm5.inc(mmi)
                                    nh += 1
                                t.wait_ge(he5.sem, nh)
                                t.wait_ge(wl5.sem, 32 * (i + 1))
                                for ts in range(4):
                                    for half in range(2):
                                        if nd >= 2:
                                            t.wait_ge(dev5.sem, nd - 1)
                                        pb = p_d5[nd % 2]
                                        for fc in range(16):
                                            mmi = t.matmul(
                                                pb[:],
                                                hT5[:, fc * 512 + ts * 128:
                                                    fc * 512 + (ts + 1) * 128],
                                                wdn5[:, fc * 1024 + half * 512:
                                                     fc * 1024 +
                                                     (half + 1) * 512],
                                                start=(fc == 0),
                                                stop=(fc == 15 and fh == 0))
                                        if fh == 1:
                                            mmi = t.matmul(
                                                pb[:], ones1[:],
                                                bb2r[:, half * 512:
                                                     (half + 1) * 512],
                                                start=False, stop=True)
                                        dm5.inc(mmi)
                                        nd += 1

                @blk.scalar
                def _(s):
                    nh = 0
                    nd = 0
                    for e in range(NE5):
                        s.wait_ge(bl5.sem, 32 * (e + 1))
                        for fh in range(2):
                            for g4 in range(4):
                                for fc in range(16):
                                    s.wait_ge(hm5.sem, nh + 1)
                                    s.activation(
                                        hT5[:, fc * 512:(fc + 1) * 512],
                                        p_h5[nh % 2][:], AF.Gelu_apprx_tanh,
                                        bias=bup5[:, fh * 16 + fc:
                                                  fh * 16 + fc + 1])
                                    he5.inc(s.drain())
                                    nh += 1
                                for ts in range(4):
                                    c = g4 * 4 + ts
                                    for half in range(2):
                                        s.wait_ge(dm5.sem, nd + 1)
                                        if nd >= 2:
                                            s.wait_ge(dx5.sem, nd - 1)
                                        s.activation(
                                            dop5[nd % 2][:], p_d5[nd % 2][:],
                                            AF.Identity,
                                            scale=gw5[:, c * 8 + e:c * 8 + e + 1])
                                        dev5.inc(s.drain())
                                        nd += 1

                @blk.vector
                def _(v):
                    nd = 0
                    for e in range(NE5):
                        for fh in range(2):
                            for g4 in range(4):
                                for ts in range(4):
                                    c = g4 * 4 + ts
                                    for half in range(2):
                                        v.wait_ge(dev5.sem, nd + 1)
                                        v.wait_ge(ctx.sem("cpx").sem, c + 1)
                                        v.tensor_add(
                                            x_ch(c)[:, half * 512:
                                                    (half + 1) * 512],
                                            x_ch(c)[:, half * 512:
                                                    (half + 1) * 512],
                                            dop5[nd % 2][:])
                                        dx5.inc(v.drain())
                                        nd += 1

                ctx.barrier()

            if stop_after and stop_after.startswith("moe"):
                raise _Cut()

            with contextlib.ExitStack() as pp:
                fn_tm = ctx.sbuf("fn_tm", [128, NCH * D], BF16, stack=pp)
                f4_fm = ctx.sbuf("f4_fm", [128, 8 * 128], BF16, stack=pp)
                fo_tm = ctx.sbuf("fo_tm", [128, 1024], F32, stack=pp)
                xb5 = [ctx.sbuf("xb50", [128, 1024], F32, stack=pp),
                       ctx.sbuf("xb51", [128, 1024], F32, stack=pp)]
                scr8 = ctx.sbuf("scr8", [128, 1024], F32, stack=pp)
                p_f1 = [ctx.psum("p_f10", [128, 1024], BF16, stack=pp),
                        ctx.psum("p_f11", [128, 1024], BF16, stack=pp)]
                p_f2 = [ctx.psum("p_f20", [128, 1024], BF16, stack=pp),
                        ctx.psum("p_f21", [128, 1024], BF16, stack=pp)]

                def fn_ch(c):
                    return fn_tm[:, c * D:(c + 1) * D]

                emit_ln(ctx, "f5", x_ch, fn_ch, NCH, st_all, scr8)

                f1m = ctx.sem("f1m"); f1e = ctx.sem("f1e")
                f2m = ctx.sem("f2m"); f2e = ctx.sem("f2e"); fdo = ctx.sem("fdo")
                fxl = ctx.sem("fxl"); fxv = ctx.sem("fxv")
                sn6 = ctx.sem("f5n")

                @blk.gpsimd
                def _(g):
                    for c in range(NCH):
                        if c >= 2:
                            g.wait_ge(fxv.sem, c - 1)
                        fxl.inc(g.dma_start(
                            out=xb5[c % 2][:],
                            in_=x3bi[c * 128:(c + 1) * 128, :]), 16)

                @blk.tensor
                def _(t):
                    for c in range(NCH):
                        t.wait_ge(sn6.sem, c + 1)
                        if c >= 2:
                            t.wait_ge(f1e.sem, c - 1)
                        for dc in range(8):
                            mmi = t.transpose(
                                p_f1[c % 2][:, dc * 128:(dc + 1) * 128],
                                fn_ch(c)[:, dc * 128:(dc + 1) * 128], ident[:])
                        f1m.inc(mmi)
                        t.wait_ge(f1e.sem, c + 1)
                        if c >= 2:
                            t.wait_ge(f2e.sem, c - 1)
                        for dc in range(8):
                            mmi = t.transpose(
                                p_f2[c % 2][:, dc * 128:(dc + 1) * 128],
                                f4_fm[:, dc * 128:(dc + 1) * 128], ident[:])
                        f2m.inc(mmi)

                @blk.scalar
                def _(s):
                    for c in range(NCH):
                        s.wait_ge(f1m.sem, c + 1)
                        for dc in range(8):
                            s.activation(f4_fm[:, dc * 128:(dc + 1) * 128],
                                         p_f1[c % 2][:, dc * 128:(dc + 1) * 128],
                                         AF.Identity, scale=g2c[:, dc:dc + 1],
                                         bias=b2c[:, dc:dc + 1])
                        f1e.inc(s.drain())
                        s.wait_ge(f2m.sem, c + 1)
                        if c >= 1:
                            s.wait_ge(fxv.sem, c)
                        for dc in range(8):
                            s.activation(fo_tm[:, dc * 128:(dc + 1) * 128],
                                         p_f2[c % 2][:, dc * 128:(dc + 1) * 128],
                                         AF.Identity)
                        f2e.inc(s.drain())

                @blk.vector
                def _(v):
                    for c in range(NCH):
                        v.wait_ge(f2e.sem, c + 1)
                        v.wait_ge(fxl.sem, 16 * (c + 1))
                        if c >= 2:
                            v.wait_ge(fdo.sem, 16 * (c - 1))
                        v.tensor_add(xb5[c % 2][:], xb5[c % 2][:], fo_tm[:])
                        fxv.inc(v.drain())

                @blk.sync
                def _(sy):
                    for c in range(NCH):
                        sy.wait_ge(fxv.sem, c + 1)
                        fdo.inc(sy.dma_start(
                            out=x4o[c * 128:(c + 1) * 128, :],
                            in_=xb5[c % 2][:]), 16)
                    sy.wait_ge(fdo.sem, 16 * NCH)

                ctx.barrier()

    except _Cut:
        ctx.barrier()
        fin5 = ctx.sem("fin5")

        @blk.sync
        def _(sy):
            fin5.inc(sy.dma_start(
                out=x4o.rearrange("(c p) d -> p c d", p=128),
                in_=x_tm[:].rearrange("p (c d) -> p c d", d=D)), 16)
            sy.wait_ge(fin5.sem, 16)

    ctx.finish()
    return nc


def prep_launch2_inputs(I, r1):
    shared = dict(
        ncst16=np.ascontiguousarray(I["L1_attn_norm_const"].reshape(1, 16)),
        woT=np.ascontiguousarray(I["L1_attn_Wo"].T),
        wg12=I["L1_ffn_Wg"],
        wupd=I["L1_ffn_Wup"], bupd=I["L1_ffn_bup"],
        wdnd=I["L1_ffn_Wdown"], bdnd=I["L1_ffn_bdown"],
        inwT=np.ascontiguousarray(I["L2_attn_in_w"].T),
        inb=I["L2_attn_in_b"],
        convw=I["L2_attn_conv_w"], convb=I["L2_attn_conv_b"],
        dtwT=np.ascontiguousarray(I["L2_attn_dt_w"].T),
        dtb=np.ascontiguousarray(I["L2_attn_dt_b"][None, :]),
        bcdw=np.ascontiguousarray(np.concatenate(
            [I["L2_attn_B_w"].T, I["L2_attn_C_w"].T, I["L2_attn_D_w"].T], 1)),
        bcdb=np.ascontiguousarray(np.concatenate(
            [I["L2_attn_B_b"], I["L2_attn_C_b"], I["L2_attn_D_b"]])[None, :]),
    )
    maps = []
    for c in range(NCORE):
        b, s0 = c // 4, (c % 4) * TPC
        m = dict(shared)
        m["x1i"] = np.asarray(r1[c]["x1o"])
        m["qni"] = np.asarray(r1[c]["qno"])
        m["kvp4"] = np.ascontiguousarray(np.concatenate(
            [np.asarray(r1[4 * b + j]["kvpo"]) for j in range(4)], 1))
        m["maskL2"] = np.full((128, 1), 1.0 if s0 > 0 else 0.0, np.float32)
        m["maskR2"] = np.full((128, 1), 1.0 if s0 + TPC < T else 0.0, np.float32)
        maps.append(m)
    return maps




def prep_launch3_inputs(I, r2):
    ii = np.arange(128)
    triL = (ii[:, None] <= ii[None, :]).astype(np.float32)
    triUs = (ii[:, None] > ii[None, :]).astype(np.float32)
    triM = (ii[None, :] <= ii[:, None]).astype(np.float32)
    keys = I["L2_ffn_keys"]
    ksl = np.zeros((128, 8 * 64), np.float32)
    for pl in range(2):
        for h in range(4):
            g = pl * 4 + h
            kT = keys[h, :, pl, :].T
            ksl[0:64, g * 64:(g + 1) * 64] = kT
            ksl[64:128, g * 64:(g + 1) * 64] = kT
    wgl_ = I["L2_ffn_Wg"]; wul_ = I["L2_ffn_Wu"]
    wgul = np.stack([np.concatenate([wgl_[:, q * 1024:(q + 1) * 1024],
                                     wul_[:, q * 1024:(q + 1) * 1024]], 1)
                     for q in range(4)])
    shared = dict(
        outwT=np.ascontiguousarray(I["L2_attn_out_w"].T),
        outb=I["L2_attn_out_b"][None, :],
        triL=triL, triUs=triUs, triM=triM,
        wq512=I["L2_ffn_Wq"], kslab=ksl,
        deT=np.ascontiguousarray(I["L2_ffn_down_embed"].T),
        ued=I["L2_ffn_up_embed"],
        wgul=np.ascontiguousarray(wgul),
        wdl=I["L2_ffn_Wd"],
        bgl=np.ascontiguousarray(I["L2_ffn_bg"][None, :]),
        bul=np.ascontiguousarray(I["L2_ffn_bu"][None, :]),
        bdl=np.ascontiguousarray(I["L2_ffn_bd"][None, :]),
    )
    maps = []
    for c in range(NCORE):
        b, r0 = c // 4, (c % 4) * 16
        m = dict(shared)
        m["x2i"] = np.asarray(r2[c]["x2o"])
        m["ubi"] = np.ascontiguousarray(np.concatenate(
            [np.asarray(r2[4 * b + j]["uo"]) for j in range(4)], 0))
        m["ctbi"] = np.ascontiguousarray(np.concatenate(
            [np.asarray(r2[4 * b + j]["cto"]) for j in range(4)], 0))
        m["ldbi"] = np.ascontiguousarray(np.concatenate(
            [np.asarray(r2[4 * b + j]["ldo"]) for j in range(4)], 1))
        m["xgi"] = np.asarray(r2[c]["xgod"])
        ohmv = np.zeros((128, 64), np.float32)
        ohmv[:, r0:r0 + 16] = 1.0
        m["ohm"] = ohmv
        maps.append(m)
    return maps




def prep_launch4_inputs(I, r3):
    wo = I["L3_attn_Wo"]
    wo4h = np.ascontiguousarray(
        wo.reshape(16, 64, 1024).transpose(1, 0, 2).reshape(64, 16 * 1024))
    amt = np.zeros((5, 640, 128), np.float32)
    jj = np.arange(640)[:, None]
    qp = np.arange(128)[None, :]
    for c in range(5):
        qi = c * 128 + qp - 4
        vis = ((jj <= 3) & (jj <= qi)) | ((jj >= 8) & (jj - 4 <= qi) & (jj < 520))
        vis = vis & (qi >= 0)
        dead = (qi < 0)
        vis = vis | ((jj == 0) & dead)
        amt[c] = vis.astype(np.float32)
    x3f = {}
    for b in range(2):
        x3f[b] = np.concatenate([np.asarray(r3[4 * b + j]["x3o"])
                                 for j in range(4)], 0)
    shared = dict(
        wq4=I["L3_attn_Wq"], wk4=I["L3_attn_Wk"], wv4=I["L3_attn_Wv"],
        wo4h=wo4h, wqkv=I["L3_ffn_w_qkv"], amt=amt,
    )
    maps = []
    for c in range(NCORE):
        b = c // 4
        m = dict(shared)
        m["x3i"] = np.asarray(r3[c]["x3o"])
        x3wv = np.zeros((640, D), np.float32)
        x3wv[0:4] = x3f[b][0:4]
        x3wv[4:520] = x3f[b][7676:8192]
        m["x3w"] = x3wv
        m["applyM"] = np.full((128, 1), 1.0 if c % 4 == 3 else 0.0, np.float32)
        maps.append(m)
    return maps




def prep_launch5_inputs(I, r4):
    wo_ = I["L3_ffn_w_o"]
    woh5 = np.ascontiguousarray(
        wo_.reshape(16, 64, 1024).transpose(1, 0, 2).reshape(64, 16 * 1024))
    pmjc = (np.arange(128)[:, None] - np.arange(512)[None, :]).astype(np.float32)
    kf, vf = {}, {}
    for b in range(2):
        kf[b] = np.concatenate([np.asarray(r4[4 * b + j]["ko3"])
                                for j in range(4)], 0)
        vf[b] = np.concatenate([np.asarray(r4[4 * b + j]["vo3"])
                                for j in range(4)], 0)
    shared = dict(
        woh5=woh5, g1i=I["L3_ffn_g1"], b1i=I["L3_ffn_b1"],
        g2i=I["L3_ffn_g2"], b2i=I["L3_ffn_b2"],
        wg8=I["L3_ffn_wg"], w1d=I["L3_ffn_w1"], bb1d=I["L3_ffn_bb1"],
        w2d=I["L3_ffn_w2"], bb2d=I["L3_ffn_bb2"], pmjc=pmjc,
    )
    maps = []
    for c in range(NCORE):
        b = c // 4
        m = dict(shared)
        m["x3bi"] = np.asarray(r4[c]["x3bo"])
        m["qib"] = np.asarray(r4[c]["qo3"])
        m["kib"] = kf[b]
        m["vib"] = vf[b]
        m["c0c"] = np.full((128, 1), float((c % 4) * TPC), np.float32)
        maps.append(m)
    return maps


if __name__ == "__main__":
    nc = build_launch1()
    print("built launch1")



if __name__ == "__main__":
    nc = build_launch1()
    print("built")
